# version 17
# baseline (speedup 1.0000x reference)
VERSION = 19

import os
import numpy as np
import ml_dtypes
import concourse.bass as bass
import concourse.mybir as mybir
from concourse.bass_utils import run_bass_kernel_spmd
from contextlib import ExitStack
from collections import deque

F32 = mybir.dt.float32
BF16 = mybir.dt.bfloat16
AF = mybir.ActivationFunctionType
ALU = mybir.AluOpType
AX = mybir.AxisListType

D = 1024
NCORE = 8
TPC = 2048
NT = 4
TS = 512
NCH = 16
FC = 8
NH = 16
EPS = 1e-5
DBG = int(os.environ.get("KDBG", "9"))
DBG2 = int(os.environ.get("KDBG2", "9"))
FULLSYNC = os.environ.get("BASS_SIM_FP64") == "1"


def _hadamard_bf16():
    h = np.array([[1.0]], dtype=np.float64)
    while h.shape[0] < D:
        h = np.block([[h, h], [h, -h]])
    h = (h / 32.0).astype(np.float32)
    hp = h.reshape(FC, 128, D).transpose(1, 0, 2).reshape(128, FC * D)
    return hp.astype(ml_dtypes.bfloat16)


def _sel16_f32():
    s = np.zeros((16, FC * 128), np.float32)
    for p in range(FC):
        s[2 * p, p * 128: p * 128 + 64] = 1.0
        s[2 * p + 1, p * 128 + 64: p * 128 + 128] = 1.0
    return s.astype(ml_dtypes.bfloat16)


class Buf:
    __slots__ = ("name", "writer", "readers")

    def __init__(self, name):
        self.name = name
        self.writer = None
        self.readers = []


class Prog:
    ENGS = ("tensor", "vector", "scalar", "gpsimd", "sync")
    DMAQ = ("din", "din1", "dout", "dout1", "dcc")

    def __init__(self):
        self.ops = {e: [] for e in self.ENGS}
        self.cnt = {e: 0 for e in list(self.ENGS) + list(self.DMAQ)}
        self.hw = {e: {} for e in self.ENGS}
        self.selfhw = {e: 0 for e in self.ENGS}

    def _collect(self, eng, reads, writes):
        waits = {}

        def need(tok):
            if tok is None:
                return
            te, tv, sm = tok
            if te == eng and not FULLSYNC:
                if not (sm and (self.cnt[eng] - tv) < 3):
                    return
            if waits.get(te, -1) < tv:
                waits[te] = tv

        for b in reads:
            need(b.writer)
        for b in writes:
            need(b.writer)
            for t2 in b.readers:
                need(t2)
        out = []
        for e2, v in sorted(waits.items()):
            if e2 == eng:
                if v > self.selfhw[eng]:
                    out.append((e2, v))
                    self.selfhw[eng] = v
            elif self.hw[eng].get(e2, 0) < v:
                out.append((e2, v))
                self.hw[eng][e2] = v
        return out

    def op(self, eng, fn, reads=(), writes=(), small=False, after=()):
        waits = self._collect(eng, reads, writes)
        for tok in after:
            te, tv, _sm = tok
            if te == eng:
                continue
            if self.hw[eng].get(te, 0) < tv:
                waits.append((te, tv))
                self.hw[eng][te] = tv
        self.cnt[eng] += 1
        tok = (eng, self.cnt[eng], small)
        self.ops[eng].append((waits, fn, eng, 1))
        for b in writes:
            b.writer = tok
            b.readers = []
        for b in reads:
            b.readers.append(tok)
        return tok

    def dma(self, eng, q, fn, reads=(), writes=()):
        waits = self._collect(eng, reads, writes)
        if self.cnt[q] > 0 and self.hw[eng].get(q, 0) < self.cnt[q]:
            waits.append((q, self.cnt[q]))
            self.hw[eng][q] = self.cnt[q]
        self.cnt[q] += 16
        tok = (q, self.cnt[q], False)
        self.ops[eng].append((waits, fn, q, 16))
        for b in writes:
            b.writer = tok
            b.readers = []
        for b in reads:
            b.readers.append(tok)
        return tok

    def wait(self, eng, tok):
        te, tv, _ = tok
        if te == eng:
            if tv > self.selfhw[eng]:
                self.selfhw[eng] = tv
                self.ops[eng].append(([(te, tv)], None, None, 0))
            return
        if self.hw[eng].get(te, 0) < tv:
            self.hw[eng][te] = tv
            self.ops[eng].append(([(te, tv)], None, None, 0))

    def emit(self, engobj, eng, sems):
        for waits, fn, incsem, incval in self.ops[eng]:
            for (se, v) in waits:
                engobj.wait_ge(sems[se], v)
            if fn is None:
                continue
            inst = fn(engobj)
            if incval:
                inst.then_inc(sems[incsem], incval)


def build_nc(bust_len, launch):
    nc = bass.Bass("TRN2", target_bir_lowering=False)
    g = {}

    def din(name, shape, dt=F32):
        g[name] = nc.dram_tensor(name, shape, dt, kind="ExternalInput")

    din("h16", [128, FC * D], BF16)
    din("i128", [128, 128], BF16)
    din("ones1", [128, 1], BF16)
    din("onesrow", [1, 128])
    din("constf", [128, 8])
    din("bust", [1, bust_len])
    if launch == 1:
        din("xT", [D, TPC])
        din("wk_l", [D, D]); din("wv_l", [D, D]); din("wffn_l", [D, D])
        kv_out = nc.dram_tensor("kv_out", [128, 520], F32, kind="ExternalOutput")
        x1_out = nc.dram_tensor("x1_out", [D, TPC], F32, kind="ExternalOutput")
        ln2_out = nc.dram_tensor("ln2_out", [D, TPC], F32, kind="ExternalOutput")
        wqt_out = nc.dram_tensor("wqt_out", [128, FC * D], F32, kind="ExternalOutput")
    else:
        din("wq_l", [D, D]); din("wfc_l", [D, D])
        din("wqt_f", [128, FC * D])
        din("gb", [128, 16])
        din("sel16", [16, FC * 128], BF16)
        din("x1f", [D, TPC])
        din("ln2f", [D, TPC])
        din("kvparts", [8 * 128, 520])
        yT = nc.dram_tensor("yT", [D, TPC], F32, kind="ExternalOutput")

    P = Prog()
    with ExitStack() as ctx:
        e = ctx.enter_context
        ctx.enter_context(nc.allow_low_precision(
            reason="bf16 residual stream / operand casts are deliberate; "
                   "matmuls accumulate fp32 in PSUM"))

        def sb(name, shape, dt):
            return e(nc.sbuf_tensor(name, shape, dt))

        x16 = sb("x16", [128, FC * TPC], BF16)
        ln2 = sb("ln2s", [128, FC * TPC], BF16)
        h16 = sb("h16s", [128, FC * D], BF16)
        wqt = sb("wqt", [128, FC * D], BF16)
        wS3 = sb("wS3", [128, FC * D], BF16)
        wS4 = sb("wS4", [128, FC * D], BF16)
        SW = 2048 if launch == 1 else 1024
        stage = sb("stage", [128, SW], F32)
        LW = 2 if launch == 1 else 1
        PRW = 2 * D
        u32b = sb("u32b", [128, PRW], BF16)
        bb32 = sb("bb32", [128, PRW], BF16)
        v32 = sb("v32", [128, LW * PRW], BF16)
        sqb = sb("sqb", [128, 1040], F32)
        rs16 = sb("rs16", [128, PRW], BF16)
        xqT = sb("xqT", [128, FC * TS], BF16)
        if launch == 2:
            qt = sb("qt", [128, FC * TS], BF16)
            ao16 = sb("ao16", [128, FC * TS], BF16)
        if launch == 1:
            k16 = sb("k16", [128, D], BF16)
            v16 = sb("v16", [128, D], BF16)
            wbld = sb("wbld", [128, D], F32)
        statc = sb("statc", [128, 40], F32)
        ksum = sb("ksum", [128, 16], F32) if launch == 1 else None
        r2b32 = sb("r2b32", [128, TS], F32 if launch == 1 else BF16)
        aux32 = sb("aux32", [128, TS], F32)
        tmpv = sb("tmpv", [128, TS], BF16)
        tmpg = sb("tmpg", [128, TS], BF16)
        if launch == 2:
            rdrow = aux32[0:16, :]
            mbf = aux32
        kvsb = sb("kvsb", [128, 520], F32) if launch == 2 else stage[:, 0:520]
        rdb32 = kvsb[:, 0:512] if launch == 2 else None
        if launch == 2:
            kvbd = sb("kvbd", [128, FC * 128], BF16)
            dbd = sb("dbd", [128, FC * 16], BF16)
            rdr2 = sb("rdr2", [16, TS], BF16)
        rowmt = sb("rowmt", [1, TS], F32)
        rowqt = sb("rowqt", [1, TS], F32)
        rowm = rowmt[0:1, :]
        rowq = rowqt[0:1, :]
        i128s = sb("i128s", [128, 128], BF16)
        ones1s = sb("ones1s", [128, 1], BF16)
        onesrs = sb("onesrs", [1, 128], F32)
        if launch == 2:
            sel16s = sb("sel16s", [16, FC * 128], BF16)
            gbs = sb("gbs", [128, 16], F32)
        cfs = sb("cfs", [128, 8], F32)
        busts = sb("busts", [1, bust_len], F32)

        pb = [e(nc.psum_tensor(f"pb{i}", [128, 512], F32)) for i in range(8)]

        semnames = list(Prog.ENGS) + list(Prog.DMAQ)
        sems = {n: e(nc.semaphore("sem_" + n)) for n in semnames}

        B = {}

        def buf(nm):
            if nm not in B:
                B[nm] = Buf(nm)
            return B[nm]

        PB = [buf(f"pb{i}") for i in range(8)]
        X16 = [buf(f"x16_{t}") for t in range(NT)]
        LN2 = [buf(f"ln2_{t}") for t in range(NT)]
        SH = [buf("stage_0"), buf("stage_1"), buf("stage_2"), buf("stage_3")]
        _ldpc = [0]
        xtcnt = {}

        def piece_jobs(srcn, dstt, tilebufs, tiles, cpeng):
            ar = g[srcn].ap().rearrange("(f p) t -> f p t", p=128)
            pieces = [(t, f) for t in tiles for f in range(FC)]
            qs = []

            def issue(k):
                t, f = pieces[k]
                i = _ldpc[0] % 2
                _ldpc[0] += 1
                qs.append(i)
                P.dma("sync", "din" if i == 0 else "din1",
                      lambda s, f=f, t=t, i=i, ar=ar: s.dma_start(
                          out=stage[:, i * 512:(i + 1) * 512],
                          in_=ar[f][:, t * TS:(t + 1) * TS]),
                      reads=(SH[i],), writes=(SH[i],))

            def copy(k):
                t, f = pieces[k]
                i = qs[k]

                def fn(eng, f=f, t=t, i=i):
                    src = stage[:, i * 512:(i + 1) * 512]
                    dst = dstt[:, f * TPC + t * TS: f * TPC + (t + 1) * TS]
                    return eng.copy(dst, src) if cpeng == "scalar" \
                        else eng.tensor_copy(dst, src)
                P.op(cpeng, fn, reads=(SH[i],), writes=(tilebufs[t],))
                xtcnt[(srcn, t)] = xtcnt.get((srcn, t), 0) + 1

            n = len(pieces)
            jobs = [lambda: issue(0)]
            for k in range(n):
                def jk(k=k):
                    if k + 1 < n:
                        issue(k + 1)
                    copy(k)
                jobs.append(jk)
            for j in jobs:
                j.light = True
            return jobs

        def wpiece_jobs(srcn, slot, slotbuf, cpeng="gpsimd"):
            if srcn == "wqt_f":
                ar = None
            else:
                ar = g[srcn].ap().rearrange("(c p) o -> c p o", p=128)
            qs = []

            def issue(k):
                i = _ldpc[0] % 2
                _ldpc[0] += 1
                qs.append(i)
                if ar is None:
                    P.dma("sync", "din" if i == 0 else "din1",
                          lambda s, k=k, i=i: s.dma_start(
                              out=stage[:, i * 512:(i + 1) * 512],
                              in_=g["wqt_f"].ap()[:, k * 512:(k + 1) * 512]),
                          reads=(SH[i],), writes=(SH[i],))
                else:
                    c, h = k // 2, k % 2
                    P.dma("sync", "din" if i == 0 else "din1",
                          lambda s, c=c, h=h, i=i, ar=ar: s.dma_start(
                              out=stage[:, i * 512:(i + 1) * 512],
                              in_=ar[c][:, h * 512:(h + 1) * 512]),
                          reads=(SH[i],), writes=(SH[i],))

            def copy(k):
                i = qs[k]
                c, h = k // 2, k % 2
                base = k * 512 if ar is None else c * D + h * 512

                def fn(eng, base=base, i=i):
                    src = stage[:, i * 512:(i + 1) * 512]
                    return eng.copy(slot[:, base:base + 512], src) if cpeng == "scalar"                         else eng.tensor_copy(slot[:, base:base + 512], src)
                P.op(cpeng, fn, reads=(SH[i],), writes=(slotbuf,))

            jobs = [lambda: issue(0)]
            for k in range(16):
                def jk(k=k):
                    if k + 1 < 16:
                        issue(k + 1)
                    copy(k)
                jobs.append(jk)
            for j in jobs:
                j.light = True
            return jobs

        constlist = [(i128s, "i128"), (ones1s, "ones1"), (onesrs, "onesrow"),
                     (cfs, "constf"), (busts, "bust")]
        if launch == 2:
            constlist += [(sel16s, "sel16"), (gbs, "gb")]
        for (dst, src) in constlist:
            P.dma("sync", "din", lambda s, d=dst, so=src: s.dma_start(out=d[:, :], in_=g[so].ap()),
                  writes=(buf("consts"),))

        bleft = [FC]

        def mk_build():
            wr = g["wffn_l"].ap().rearrange("(c p) i -> c p i", p=128)
            wqtr = wqt_out.ap().rearrange("p (i o c) -> o i p c", i=8, o=8)
            WH = (SH[2], SH[3])

            def bdma(oc):
                P.dma("sync", "din1", lambda s, oc=oc: s.dma_start(
                    out=stage[:, 1024:2048], in_=wr[oc]), reads=WH, writes=WH)

            def bjob(oc):
                P.op("vector", lambda v: v.tensor_reduce(
                    statc[:, 0:1], stage[:, 1024:2048], AX.X, ALU.add,
                    apply_absolute_value=True),
                    reads=WH, writes=(buf("sc0"),), small=True)
                P.op("vector", lambda v: v.tensor_scalar(
                    statc[:, 1:2], statc[:, 0:1], 1.0 / D, 1e-5, ALU.mult, ALU.max),
                    reads=(buf("sc0"),), writes=(buf("sc1"),), small=True)
                P.op("vector", lambda v: v.reciprocal(statc[:, 2:3], statc[:, 1:2]),
                     reads=(buf("sc1"),), writes=(buf("sc2"),), small=True)
                P.op("vector", lambda v: v.tensor_single_scalar(
                    statc[:, 3:4], statc[:, 1:2], 0.5, ALU.mult),
                    reads=(buf("sc1"),), writes=(buf("sc3"),), small=True)
                P.op("scalar", lambda s: s.activation(
                    wbld[:, :], stage[:, 1024:2048], AF.Copy, scale=statc[:, 2:3]),
                    reads=WH + (buf("sc2"),), writes=(buf("wbld"),))
                P.op("vector", lambda v: v.tensor_scalar(
                    stage[:, 1024:2048], wbld[:, :], 0.5, 1.0, ALU.is_gt, ALU.mult),
                    reads=(buf("wbld"),), writes=WH)
                P.op("vector", lambda v: v.tensor_scalar(
                    wbld[:, :], wbld[:, :], -0.5, 1.0, ALU.is_lt, ALU.mult),
                    reads=(buf("wbld"),), writes=(buf("wbld"),))
                P.op("vector", lambda v: v.tensor_sub(
                    stage[:, 1024:2048], stage[:, 1024:2048], wbld[:, :]),
                    reads=WH + (buf("wbld"),), writes=WH)
                P.op("scalar", lambda s: s.activation(
                    tmpg[:, 0:512], stage[:, 1024:1536], AF.Copy, scale=statc[:, 3:4]),
                    reads=WH + (buf("sc3"),), writes=(buf("tmpg"),))
                P.op("scalar", lambda s: s.activation(
                    tmpv[:, 0:512], stage[:, 1536:2048], AF.Copy, scale=statc[:, 3:4]),
                    reads=WH + (buf("sc3"),), writes=(buf("tmpv"),))
                if oc + 1 < FC:
                    bdma(oc + 1)
                for ic in range(FC):
                    bk = 4 + (ic % 2)
                    srcop = tmpg if ic < 4 else tmpv
                    srcb = buf("tmpg") if ic < 4 else buf("tmpv")
                    co = (ic % 4) * 128
                    P.op("tensor", lambda t, co=co, bk=bk, srcop=srcop: t.matmul(
                        pb[bk][:, 0:128], srcop[:, co:co + 128], i128s[:, :],
                        start=True, stop=True),
                        reads=(srcb, buf("consts"), PB[bk]), writes=(PB[bk],))
                    P.op("scalar", lambda s, ic=ic, oc=oc, bk=bk: s.copy(
                        wqt[:, ic * D + oc * 128: ic * D + (oc + 1) * 128], pb[bk][:, 0:128]),
                        reads=(PB[bk],), writes=(buf("wqt"),))
                    rdst = r2b32 if ic < 4 else aux32
                    rb = buf("r2b32") if ic < 4 else buf("aux32")
                    j = ic % 4
                    P.op("scalar", lambda s, j=j, bk=bk, rdst=rdst: s.copy(
                        rdst[:, j * 128:(j + 1) * 128], pb[bk][:, 0:128]),
                        reads=(PB[bk],), writes=(rb,))
                for j in range(4):
                    P.dma("gpsimd", "dout", lambda gp, oc=oc, j=j: gp.dma_start(
                        out=wqtr[oc][j], in_=r2b32[:, j * 128:(j + 1) * 128]),
                        reads=(buf("r2b32"),))
                    P.dma("gpsimd", "dout1", lambda gp, oc=oc, j=j: gp.dma_start(
                        out=wqtr[oc][4 + j], in_=aux32[:, j * 128:(j + 1) * 128]),
                        reads=(buf("aux32"),))
                bleft[0] -= 1
            bdma(0)
            return [lambda oc=oc: bjob(oc) for oc in range(FC)]

        def kv_ingest():
            kvr = g["kvparts"].ap().rearrange("(r p) c -> r p c", p=128)
            P.op("vector", lambda v: v.memset(kvbd[:, :], 0.0), writes=(buf("kvbd"),))
            P.op("vector", lambda v: v.memset(dbd[:, :], 0.0), writes=(buf("dbd"),))

            def slotb(r):
                return (buf("sqbA"), buf("sqbB")) if r % 2 == 0 else (buf("sqbB"),)

            def dmain(r):
                off = (r % 2) * 520
                P.dma("sync", "din" if r % 2 == 0 else "din1",
                      lambda s, r=r, off=off: s.dma_start(
                          out=sqb[:, off:off + 520], in_=kvr[r]),
                      reads=slotb(r), writes=slotb(r))
            dmain(0)
            for r in range(8):
                if r + 1 < 8:
                    dmain(r + 1)
                off = (r % 2) * 520
                if r == 0:
                    P.op("gpsimd", lambda g2, off=off: g2.tensor_copy(
                        kvsb[:, :], sqb[:, off:off + 520]),
                        reads=slotb(r), writes=(buf("kvsb"),))
                else:
                    P.op("gpsimd", lambda g2, off=off, r=r: g2.tensor_tensor(
                        kvsb[:, :], sqb[:, off:off + 520], kvsb[:, :], ALU.add),
                        reads=slotb(r) + (buf("kvsb"),), writes=(buf("kvsb"),))
            for p_ in range(FC):
                P.op("gpsimd", lambda g2, p_=p_: g2.tensor_copy(
                    kvbd[0:64, p_ * 128: p_ * 128 + 64], kvsb[0:64, p_ * 64:(p_ + 1) * 64]),
                    reads=(buf("kvsb"),), writes=(buf("kvbd"),))
                P.op("gpsimd", lambda g2, p_=p_: g2.tensor_copy(
                    kvbd[64:128, p_ * 128 + 64: p_ * 128 + 128], kvsb[64:128, p_ * 64:(p_ + 1) * 64]),
                    reads=(buf("kvsb"),), writes=(buf("kvbd"),))
                P.op("gpsimd", lambda g2, p_=p_: g2.tensor_copy(
                    dbd[0:64, p_ * 16 + 2 * p_: p_ * 16 + 2 * p_ + 1], kvsb[0:64, 512 + p_:513 + p_]),
                    reads=(buf("kvsb"),), writes=(buf("dbd"),), small=True)
                P.op("gpsimd", lambda g2, p_=p_: g2.tensor_copy(
                    dbd[64:128, p_ * 16 + 2 * p_ + 1: p_ * 16 + 2 * p_ + 2], kvsb[64:128, 512 + p_:513 + p_]),
                    reads=(buf("kvsb"),), writes=(buf("dbd"),), small=True)

        def ffn_phase(tile, final, fill=None):
            def F():
                if fill:
                    fill.popleft()()
                    if fill and len(fill) > 16 and getattr(fill[0], "light", False):
                        fill.popleft()()
            t0 = tile * TS

            def emit_xh_g(gch):
                for half, bk in ((0, 0), (1, 1)):
                    def mmg(t, bk=bk, gch=gch, half=half):
                        last = None
                        for kc in range(FC):
                            last = t.matmul(pb[bk][:, 0:512],
                                            x16[:, kc * TPC + gch * 128: kc * TPC + (gch + 1) * 128],
                                            h16[:, kc * D + half * 512: kc * D + half * 512 + 512],
                                            start=(kc == 0), stop=(kc == FC - 1))
                        return last
                    P.op("tensor", mmg, reads=(X16[gch // 4], buf("h16"), PB[bk]), writes=(PB[bk],))
                    F()

            def emit_xh(cc):
                emit_xh_g(tile * 4 + cc)

            if final or tile == 0:
                emit_xh(0)
            for pp in range(2):
                if DBG2 < 1:
                    continue
                pph = (pp % 2) * PRW if LW == 2 else 0
                pbuf = str(pp % 2) if LW == 2 else "0"
                vb, bbb = buf("v32" + pbuf), buf("bb32")
                ubb = buf("u32b")
                for ci in range(2):
                    cc = 2 * pp + ci
                    co = pph + ci * D
                    t_m1 = P.op("vector", lambda v, ci=ci: v.tensor_scalar_mul(
                        statc[:, 8 + ci:9 + ci], pb[0][:, 0:1], 1.0 / 32),
                        reads=(PB[0],), writes=(buf(f"sc8{ci}"),), small=True)
                    t_a0 = P.op("vector", lambda v, ci=ci: v.tensor_reduce(
                        statc[:, 32 + ci:33 + ci], pb[0][:, 1:512], AX.X, ALU.max,
                        apply_absolute_value=True),
                        reads=(PB[0],), writes=(buf(f"sc32{ci}"),), small=True)
                    t_a1 = P.op("vector", lambda v, ci=ci: v.tensor_reduce(
                        statc[:, 34 + ci:35 + ci], pb[1][:, 0:512], AX.X, ALU.max,
                        apply_absolute_value=True),
                        reads=(PB[1],), writes=(buf(f"sc34{ci}"),), small=True)
                    t_sqa = P.op("scalar", lambda s: s.activation(
                        sqb[:, 0:512], pb[0][:, 0:512], AF.Square),
                        reads=(PB[0],), writes=(buf("sqbA"),), after=(t_m1, t_a0))
                    t_sqb = P.op("scalar", lambda s: s.activation(
                        sqb[:, 512:1024], pb[1][:, 0:512], AF.Square),
                        reads=(PB[1],), writes=(buf("sqbB"),), after=(t_a1,))
                    P.op("vector", lambda v, ci=ci: v.tensor_tensor(
                        statc[:, 36 + ci:37 + ci], statc[:, 32 + ci:33 + ci],
                        statc[:, 34 + ci:35 + ci], ALU.max),
                        reads=(buf(f"sc32{ci}"), buf(f"sc34{ci}")), writes=(buf(f"sc36{ci}"),), small=True)
                    P.op("vector", lambda v, ci=ci: v.tensor_single_scalar(
                        statc[:, 12 + ci:13 + ci], statc[:, 36 + ci:37 + ci], 1e-5, ALU.max),
                        reads=(buf(f"sc36{ci}"),), writes=(buf(f"sc12{ci}"),), small=True)
                    P.op("vector", lambda v, ci=ci: v.reciprocal(
                        statc[:, 14 + ci:15 + ci], statc[:, 12 + ci:13 + ci]),
                        reads=(buf(f"sc12{ci}"),), writes=(buf(f"sc14{ci}"),), small=True)
                    P.op("vector", lambda v, ci=ci: v.tensor_scalar_mul(
                        statc[:, 16 + ci:17 + ci], statc[:, 14 + ci:15 + ci], 7.0),
                        reads=(buf(f"sc14{ci}"),), writes=(buf(f"sc16{ci}"),), small=True)
                    P.op("vector", lambda v, co=co, ci=ci: v.tensor_scalar_mul(
                        v32[:, co:co + 512], pb[0][:, 0:512], statc[:, 16 + ci:17 + ci]),
                        reads=(PB[0], buf(f"sc16{ci}")), writes=(vb,), after=(t_sqa,))
                    P.op("vector", lambda v, co=co, ci=ci: v.tensor_scalar_mul(
                        v32[:, co + 512:co + 1024], pb[1][:, 0:512], statc[:, 16 + ci:17 + ci]),
                        reads=(PB[1], buf(f"sc16{ci}")), writes=(vb,), after=(t_sqb,))
                    if cc + 1 < 4:
                        emit_xh(cc + 1)
                    P.op("vector", lambda v, ci=ci: v.reduce_sum(
                        statc[:, 10 + ci:11 + ci], sqb[:, 0:D], AX.X),
                        reads=(buf("sqbA"), buf("sqbB")), writes=(buf(f"sc10{ci}"),), small=True)
                if DBG2 < 2:
                    continue
                P.op("vector", lambda v, pph=pph: v.memset(
                    v32[:, pph:pph + PRW].rearrange("p (c d) -> p c d", d=D)[:, :, 0:1], 0.0),
                    writes=(vb,), small=True)
                P.op("vector", lambda v: v.tensor_tensor(
                    statc[:, 18:20], statc[:, 8:10], statc[:, 8:10], ALU.mult),
                    reads=(buf("sc80"), buf("sc81")), writes=(buf("sc18"),), small=True)
                P.op("vector", lambda v: v.tensor_single_scalar(
                    statc[:, 20:22], statc[:, 10:12], 1.0 / D, ALU.mult),
                    reads=(buf("sc100"), buf("sc101")), writes=(buf("sc20"),), small=True)
                P.op("vector", lambda v: v.tensor_sub(statc[:, 22:24], statc[:, 20:22], statc[:, 18:20]),
                     reads=(buf("sc20"), buf("sc18")), writes=(buf("sc22"),), small=True)
                P.op("scalar", lambda s: s.activation(statc[:, 24:26], statc[:, 22:24], AF.Ln, bias=cfs[:, 0:1]),
                     reads=(buf("sc22"), buf("consts")), writes=(buf("sc24"),), small=True)
                P.op("scalar", lambda s: s.activation(statc[:, 26:28], statc[:, 24:26], AF.Exp, scale=cfs[:, 1:2]),
                     reads=(buf("sc24"), buf("consts")), writes=(buf("sc26"),), small=True)
                P.op("vector", lambda v: v.tensor_tensor(
                    statc[:, 28:30], statc[:, 12:14], statc[:, 26:28], ALU.mult),
                    reads=(buf("sc120"), buf("sc121"), buf("sc26")), writes=(buf("sc28"),), small=True)
                P.op("vector", lambda v: v.tensor_single_scalar(
                    statc[:, 28:30], statc[:, 28:30], 1.0 / 7, ALU.mult),
                    reads=(buf("sc28"),), writes=(buf("sc28"),), small=True)
                if DBG2 < 3:
                    continue
                P.op("scalar", lambda s, pph=pph: s.activation(
                    u32b[:, :], v32[:, pph:pph + PRW], AF.Abs),
                    reads=(vb,), writes=(ubb,))
                P.op("vector", lambda v, pph=pph: v.tensor_scalar(
                    rs16[:, :], v32[:, pph:pph + PRW], 0.0, 2.0, ALU.is_ge, ALU.mult),
                    reads=(vb,), writes=(buf("rs16"),))
                P.op("vector", lambda v: v.tensor_single_scalar(
                    rs16[:, :], rs16[:, :], -1.0, ALU.add),
                    reads=(buf("rs16"),), writes=(buf("rs16"),))
                P.op("vector", lambda v: v.tensor_scalar(
                    bb32[:, :], u32b[:, :], 3.5, 4.0, ALU.is_ge, ALU.mult),
                    reads=(ubb,), writes=(bbb,))
                P.op("vector", lambda v: v.tensor_sub(u32b[:, :], u32b[:, :], bb32[:, :]),
                     reads=(ubb, bbb), writes=(ubb,))
                P.op("vector", lambda v: v.tensor_scalar(
                    bb32[:, :], u32b[:, :], 1.5, 2.0, ALU.is_ge, ALU.mult),
                    reads=(ubb,), writes=(bbb,))
                P.op("vector", lambda v: v.tensor_sub(u32b[:, :], u32b[:, :], bb32[:, :]),
                     reads=(ubb, bbb), writes=(ubb,))
                P.op("vector", lambda v: v.tensor_scalar(
                    bb32[:, :], u32b[:, :], 0.5, 1.0, ALU.is_ge, ALU.mult),
                    reads=(ubb,), writes=(bbb,))
                P.op("vector", lambda v: v.tensor_sub(u32b[:, :], u32b[:, :], bb32[:, :]),
                     reads=(ubb, bbb), writes=(ubb,))
                P.op("vector", lambda v: v.tensor_mul(bb32[:, :], rs16[:, :], u32b[:, :]),
                     reads=(buf("rs16"), ubb), writes=(bbb,))
                P.op("vector", lambda v, pph=pph: v.tensor_sub(
                    v32[:, pph:pph + PRW], v32[:, pph:pph + PRW], bb32[:, :]),
                    reads=(vb, bbb), writes=(vb,))
                if DBG2 < 4:
                    continue
                for ci in range(2):
                    cc = 2 * pp + ci
                    co = pph + ci * D
                    P.op("vector", lambda v, co=co, ci=ci: v.tensor_scalar_mul(
                        rs16[:, ci * D:(ci + 1) * D], v32[:, co:co + D], statc[:, 28 + ci:29 + ci]),
                        reads=(vb, buf("sc28")), writes=(buf("rs16"),))
                for f in range(FC):
                    bk = 4 + (f % 2)
                    for ci in range(2):
                        cc = 2 * pp + ci
                        P.op("tensor", lambda t, f=f, ci=ci, bk=bk: t.matmul(
                            pb[bk][:, ci * 128:(ci + 1) * 128], rs16[:, ci * D + f * 128: ci * D + (f + 1) * 128],
                            i128s[:, :], start=True, stop=True),
                            reads=(buf("rs16"), buf("consts"), PB[bk]), writes=(PB[bk],))
                    P.op("scalar", lambda s, f=f, pp=pp, bk=bk: s.copy(
                        xqT[:, f * TS + pp * 256: f * TS + (pp + 1) * 256], pb[bk][:, 0:256]),
                        reads=(PB[bk],), writes=(buf("xqT"),))
                    F()
            if not final and tile + 1 < NT:
                while xtcnt.get(("xT", tile + 1), 0) < FC and fill:
                    fill.popleft()()
                emit_xh_g((tile + 1) * 4)
            if launch == 1 and tile == 0:
                while bleft[0] > 0 and fill:
                    fill.popleft()()
            for oc in range(FC if DBG2 >= 5 else 0):
                bk = 4 + (oc % 2)

                def mmo(t, oc=oc, bk=bk):
                    last = None
                    for ic in range(FC):
                        last = t.matmul(pb[bk][:, 0:512], wqt[:, ic * D + oc * 128: ic * D + (oc + 1) * 128],
                                        xqT[:, ic * TS:(ic + 1) * TS],
                                        start=(ic == 0), stop=(ic == FC - 1))
                    return last
                P.op("tensor", mmo, reads=(buf("wqt"), buf("xqT"), PB[bk]), writes=(PB[bk],))
                F()
                if not final:
                    P.op("vector", lambda v, oc=oc, bk=bk, t0=t0: v.tensor_tensor(
                        x16[:, oc * TPC + t0: oc * TPC + t0 + TS], pb[bk][:, 0:512],
                        x16[:, oc * TPC + t0: oc * TPC + t0 + TS], ALU.add),
                        reads=(PB[bk], X16[tile]), writes=(X16[tile],))
                else:
                    i = oc % 2
                    hb = buf("sqbA") if i == 0 else buf("sqbB")
                    P.op("vector", lambda v, oc=oc, bk=bk, t0=t0, i=i: v.tensor_tensor(
                        sqb[:, i * TS:(i + 1) * TS], pb[bk][:, 0:512],
                        x16[:, oc * TPC + t0: oc * TPC + t0 + TS], ALU.add),
                        reads=(PB[bk], X16[tile], hb), writes=(hb,))
                    P.dma("gpsimd", "dout" if oc % 2 == 0 else "dout1",
                          lambda gp, oc=oc, tile=tile, i=i: gp.dma_start(
                              out=yT.ap()[oc * 128:(oc + 1) * 128, tile * TS:(tile + 1) * TS],
                              in_=sqb[:, i * TS:(i + 1) * TS]),
                          reads=(hb,))

        def ln2_block(tile):
            t0 = tile * TS

            def mmsum(t, t0=t0):
                last = None
                for f in range(FC):
                    last = t.matmul(pb[4][0:1, 0:512], ones1s[:, 0:1],
                                    x16[:, f * TPC + t0: f * TPC + t0 + TS],
                                    start=(f == 0), stop=(f == FC - 1))
                return last
            P.op("tensor", mmsum, reads=(X16[tile], buf("consts"), PB[4]), writes=(PB[4],))
            for f in range(FC):
                P.op("scalar", lambda s, f=f, t0=t0: s.activation(
                    xqT[:, f * TS:(f + 1) * TS], x16[:, f * TPC + t0: f * TPC + t0 + TS], AF.Square),
                    reads=(X16[tile],), writes=(buf("xqT"),))

            def mmsq(t):
                last = None
                for f in range(FC):
                    last = t.matmul(pb[5][0:1, 0:512], ones1s[:, 0:1],
                                    xqT[:, f * TS:(f + 1) * TS],
                                    start=(f == 0), stop=(f == FC - 1))
                return last
            P.op("tensor", mmsq, reads=(buf("xqT"), buf("consts"), PB[5]), writes=(PB[5],))
            P.op("vector", lambda v: v.tensor_scalar_mul(rowm, pb[4][0:1, 0:512], 1.0 / D),
                 reads=(PB[4],), writes=(buf("rowm"),))
            P.op("vector", lambda v: v.tensor_scalar_mul(rowq, pb[5][0:1, 0:512], 1.0 / D),
                 reads=(PB[5],), writes=(buf("rowq"),))
            P.op("tensor", lambda t: t.matmul(pb[4][:, 0:512], onesrs[0:1, :], rowm, start=True, stop=True),
                 reads=(buf("rowm"), buf("consts"), PB[4]), writes=(PB[4],))
            P.op("vector", lambda v: v.tensor_tensor(rowm, rowm, rowm, ALU.mult),
                 reads=(buf("rowm"),), writes=(buf("rowm"),), small=True)
            P.op("vector", lambda v: v.tensor_sub(rowq, rowq, rowm),
                 reads=(buf("rowq"), buf("rowm")), writes=(buf("rowq"),))
            P.op("scalar", lambda s: s.activation(rowm, rowq, AF.Ln, bias=cfs[0:1, 0:1]),
                 reads=(buf("rowq"), buf("consts")), writes=(buf("rowm"),))
            P.op("scalar", lambda s: s.activation(rowq, rowm, AF.Exp, scale=cfs[0:1, 1:2]),
                 reads=(buf("rowm"), buf("consts")), writes=(buf("rowq"),))
            P.op("tensor", lambda t: t.matmul(pb[5][:, 0:512], onesrs[0:1, :], rowq, start=True, stop=True),
                 reads=(buf("rowq"), buf("consts"), PB[5]), writes=(PB[5],))
            P.op("vector", lambda v: v.tensor_copy(r2b32[:, :], pb[5][:, 0:512]),
                 reads=(PB[5],), writes=(buf("r2b32"),))
            P.op("vector", lambda v: v.tensor_copy(aux32[:, :], pb[4][:, 0:512]),
                 reads=(PB[4],), writes=(buf("aux32"),))
            for f in range(FC):
                eng = "vector" if f < 5 else "gpsimd"
                tmp = tmpv if f < 5 else tmpg
                tb = buf("tmpv") if f < 5 else buf("tmpg")
                P.op(eng, lambda e2, f=f, t0=t0, tmp=tmp: e2.tensor_sub(
                    tmp[:, :], x16[:, f * TPC + t0: f * TPC + t0 + TS], aux32[:, :]),
                    reads=(X16[tile], buf("aux32")), writes=(tb,))
                P.op(eng, lambda e2, f=f, t0=t0, tmp=tmp: e2.tensor_mul(
                    ln2[:, f * TPC + t0: f * TPC + t0 + TS], tmp[:, :], r2b32[:, :]),
                    reads=(tb, buf("r2b32")), writes=(LN2[tile],))

        def bnd_out(srct, outt, sbuf_, f, t, qi):
            hb = buf("sqbA") if qi == 0 else buf("sqbB")
            P.op("scalar",
                 lambda s, f=f, t=t, qi=qi, srct=srct: s.copy(
                     sqb[:, qi * 512:(qi + 1) * 512],
                     srct[:, f * TPC + t * TS: f * TPC + (t + 1) * TS]),
                 reads=(sbuf_, hb), writes=(hb,))
            P.dma("gpsimd", "dout" if qi % 2 == 0 else "dout1",
                  lambda gp, f=f, t=t, qi=qi, outt=outt: gp.dma_start(
                      out=outt.ap()[f * 128:(f + 1) * 128, t * TS:(t + 1) * TS],
                      in_=sqb[:, qi * 512:(qi + 1) * 512]),
                  reads=(hb,))

        def kv_jobs(tile):
            jobs = []
            for cc in range(4):
                gch = tile * 4 + cc

                def ja(gch=gch, tile=tile):
                    for half in range(2):
                        def mmk(t, gch=gch, half=half):
                            last = None
                            for kc in range(FC):
                                last = t.matmul(pb[2 + half][:, 0:512],
                                                ln2[:, kc * TPC + gch * 128: kc * TPC + (gch + 1) * 128],
                                                wS3[:, kc * D + half * 512: kc * D + half * 512 + 512],
                                                start=(kc == 0), stop=(kc == FC - 1))
                            return last
                        P.op("tensor", mmk, reads=(LN2[tile], buf("wS3"), PB[2 + half]), writes=(PB[2 + half],))
                    P.op("scalar", lambda s: s.activation(k16[:, 0:512], pb[2][:, 0:512], AF.Sigmoid),
                         reads=(PB[2],), writes=(buf("k16"),))
                    P.op("scalar", lambda s: s.activation(k16[:, 512:1024], pb[3][:, 0:512], AF.Sigmoid),
                         reads=(PB[3],), writes=(buf("k16"),))

                def jb(gch=gch, tile=tile):
                    for half in range(2):
                        def mmv(t, gch=gch, half=half):
                            last = None
                            for kc in range(FC):
                                last = t.matmul(pb[2 + half][:, 0:512],
                                                ln2[:, kc * TPC + gch * 128: kc * TPC + (gch + 1) * 128],
                                                wS4[:, kc * D + half * 512: kc * D + half * 512 + 512],
                                                start=(kc == 0), stop=(kc == FC - 1))
                            return last
                        P.op("tensor", mmv, reads=(LN2[tile], buf("wS4"), PB[2 + half]), writes=(PB[2 + half],))
                    P.op("scalar", lambda s: s.copy(v16[:, 0:512], pb[2][:, 0:512]),
                         reads=(PB[2],), writes=(buf("v16"),))
                    P.op("scalar", lambda s: s.copy(v16[:, 512:1024], pb[3][:, 0:512]),
                         reads=(PB[3],), writes=(buf("v16"),))

                def jc(gch=gch):
                    def mmkv(t, gch=gch):
                        last = None
                        for p_ in range(FC):
                            bank = pb[6] if p_ < 4 else pb[7]
                            colo = (p_ % 4) * 128
                            last = t.matmul(bank[:, colo:colo + 128],
                                            k16[:, p_ * 128:(p_ + 1) * 128],
                                            v16[:, p_ * 128:(p_ + 1) * 128],
                                            start=(gch == 0 and (p_ % 4) == 0),
                                            stop=(gch == NCH - 1 and (p_ % 4) == 3))
                        return last
                    P.op("tensor", mmkv, reads=(buf("k16"), buf("v16"), PB[6], PB[7]),
                         writes=(PB[6], PB[7]))

                    def mmks(t):
                        last = None
                        for p_ in range(FC):
                            last = t.matmul(pb[2][:, p_:p_ + 1], k16[:, p_ * 128:(p_ + 1) * 128],
                                            ones1s[:, 0:1], start=(p_ == 0), stop=(p_ == FC - 1))
                        return last
                    P.op("tensor", mmks, reads=(buf("k16"), buf("consts"), PB[2]), writes=(PB[2],))
                    sks, dks = (gch % 2) * 8, ((gch + 1) % 2) * 8
                    if gch == 0:
                        P.op("vector", lambda v, dks=dks: v.tensor_copy(ksum[:, dks:dks + 8], pb[2][:, 0:8]),
                             reads=(PB[2],), writes=(buf("ksum"),), small=True)
                    else:
                        P.op("vector", lambda v, sks=sks, dks=dks: v.tensor_tensor(
                            ksum[:, dks:dks + 8], pb[2][:, 0:8], ksum[:, sks:sks + 8], ALU.add),
                            reads=(PB[2], buf("ksum")), writes=(buf("ksum"),), small=True)
                jobs += [ja, jb, jc]
            return jobs

        if launch == 1:
            P.dma("sync", "din", lambda s: s.dma_start(out=h16[:, :], in_=g["h16"].ap()),
                  writes=(buf("h16"),))
            for j in piece_jobs("xT", x16, X16, (0,), "scalar"):
                j()
            jobs = deque()
            if DBG >= 2:
                jobs.extend(mk_build())
            else:
                bleft[0] = 0
            jobs.extend(wpiece_jobs("wk_l", wS3, buf("wS3"), "scalar"))
            jobs.extend(piece_jobs("xT", x16, X16, (1,), "scalar"))
            jobs.extend(wpiece_jobs("wv_l", wS4, buf("wS4"), "scalar"))
            jobs.extend(piece_jobs("xT", x16, X16, (2,), "scalar"))
            jobs.extend(piece_jobs("xT", x16, X16, (3,), "scalar"))
            if DBG >= 3:
                qrr = [0]
                for tile in range(NT):
                    while xtcnt.get(("xT", tile), 0) < FC:
                        jobs.popleft()()
                    ffn_phase(tile, final=False, fill=jobs)
                    if DBG >= 4:
                        ln2_block(tile)
                        for f in range(FC):
                            bnd_out(x16, x1_out, X16[tile], f, tile, qrr[0] % 2)
                            qrr[0] += 1
                            bnd_out(ln2, ln2_out, LN2[tile], f, tile, qrr[0] % 2)
                            qrr[0] += 1
                        if DBG >= 5:
                            jobs.extend(kv_jobs(tile))
                while jobs:
                    jobs.popleft()()

        if launch == 1:
            for p_ in range(FC if DBG >= 5 else 0):
                bank = pb[6] if p_ < 4 else pb[7]
                pbuf = PB[6] if p_ < 4 else PB[7]
                colo = (p_ % 4) * 128
                rd = (pbuf, SH[0], SH[1]) if p_ == 0 else (pbuf,)
                P.op("vector", lambda v, p_=p_, bank=bank, colo=colo: v.tensor_copy(
                    kvsb[0:64, p_ * 64:(p_ + 1) * 64], bank[0:64, colo:colo + 64]),
                    reads=rd, writes=(buf("kvsb"), SH[0], SH[1]))
                P.op("vector", lambda v, p_=p_, bank=bank, colo=colo: v.tensor_copy(
                    kvsb[64:128, p_ * 64:(p_ + 1) * 64], bank[64:128, colo + 64:colo + 128]),
                    reads=(pbuf,), writes=(buf("kvsb"), SH[0], SH[1]))
            if DBG >= 5:
                P.op("vector", lambda v: v.tensor_copy(kvsb[:, 512:520], ksum[:, 0:8]),
                     reads=(buf("ksum"),), writes=(buf("kvsb"), SH[0], SH[1]))
                P.dma("gpsimd", "dout", lambda gp: gp.dma_start(out=kv_out.ap(), in_=kvsb[:, :]),
                      reads=(buf("kvsb"),))

        def bjobs(tile):
            t0 = tile * TS
            QB = buf("qt")
            jobs = []

            def jq(oc):
                bk = 2 + (oc % 2)

                def mmq(t, oc=oc, t0=t0, bk=bk):
                    last = None
                    for kc in range(FC):
                        last = t.matmul(pb[bk][:, 0:512],
                                        wS3[:, kc * D + oc * 128: kc * D + (oc + 1) * 128],
                                        ln2[:, kc * TPC + t0: kc * TPC + t0 + TS],
                                        start=(kc == 0), stop=(kc == FC - 1))
                    return last
                P.op("tensor", mmq, reads=(buf("wS3"), LN2[tile], PB[bk]), writes=(PB[bk],))
                P.op("scalar", lambda s, oc=oc, bk=bk: s.activation(
                    qt[:, oc * TS:(oc + 1) * TS], pb[bk][:, 0:512], AF.Sigmoid),
                    reads=(PB[bk],), writes=(QB,))
            for oc in range(FC):
                jobs.append(lambda oc=oc: jq(oc))

            def jden():
                def mmden(t):
                    last = None
                    for p_ in range(FC):
                        last = t.matmul(pb[2][0:16, 0:512], dbd[:, p_ * 16:(p_ + 1) * 16],
                                        qt[:, p_ * TS:(p_ + 1) * TS],
                                        start=(p_ == 0), stop=(p_ == FC - 1))
                    return last
                P.op("tensor", mmden, reads=(buf("dbd"), QB, PB[2]), writes=(PB[2],))
                P.op("vector", lambda v: v.tensor_scalar_add(rdrow[:, :], pb[2][0:16, 0:512], 1e-8),
                     reads=(PB[2],), writes=(buf("aux32"),))
                P.op("vector", lambda v: v.reciprocal(rdr2[0:16, :], rdrow[:, :]),
                     reads=(buf("aux32"),), writes=(buf("rdr2"),))
            jobs.append(jden)

            def jnum(p_):
                P.op("tensor", lambda t, p_=p_: t.matmul(
                    pb[2][:, 0:512], kvbd[:, p_ * 128:(p_ + 1) * 128],
                    qt[:, p_ * TS:(p_ + 1) * TS], start=True, stop=True),
                    reads=(buf("kvbd"), QB, PB[2]), writes=(PB[2],))
                P.op("tensor", lambda t, p_=p_: t.matmul(
                    pb[3][:, 0:512], sel16s[:, p_ * 128:(p_ + 1) * 128], rdr2[0:16, :],
                    start=True, stop=True),
                    reads=(buf("rdr2"), buf("consts"), PB[3]), writes=(PB[3],))
                P.op("vector", lambda v: v.tensor_copy(rdb32[:, :], pb[3][:, 0:512]),
                     reads=(PB[3], buf("kvsb")), writes=(buf("rdb32"),))
                P.op("vector", lambda v, p_=p_: v.tensor_mul(
                    ao16[:, p_ * TS:(p_ + 1) * TS], pb[2][:, 0:512], rdb32[:, :]),
                    reads=(PB[2], buf("rdb32")), writes=(buf("ao16"),))
            for p_ in range(FC):
                jobs.append(lambda p_=p_: jnum(p_))

            def jfc(oc):
                bk = 6 + (oc % 2)

                def mmfc(t, oc=oc, bk=bk):
                    last = None
                    for hc in range(FC):
                        last = t.matmul(pb[bk][:, 0:512],
                                        wS4[:, hc * D + oc * 128: hc * D + (oc + 1) * 128],
                                        ao16[:, hc * TS:(hc + 1) * TS],
                                        start=(hc == 0), stop=(hc == FC - 1))
                    return last
                P.op("tensor", mmfc, reads=(buf("wS4"), buf("ao16"), PB[bk]), writes=(PB[bk],))
                P.op("vector", lambda v, oc=oc, t0=t0, bk=bk: v.tensor_tensor(
                    qt[:, oc * TS:(oc + 1) * TS], pb[bk][:, 0:512],
                    ln2[:, oc * TPC + t0: oc * TPC + t0 + TS], ALU.add),
                    reads=(PB[bk], LN2[tile]), writes=(QB,))
            for oc in range(FC):
                jobs.append(lambda oc=oc: jfc(oc))

            def jstat():
                def mmsy(t):
                    last = None
                    for f in range(FC):
                        last = t.matmul(pb[2][0:1, 0:512], ones1s[:, 0:1],
                                        qt[:, f * TS:(f + 1) * TS],
                                        start=(f == 0), stop=(f == FC - 1))
                    return last
                P.op("tensor", mmsy, reads=(QB, buf("consts"), PB[2]), writes=(PB[2],))
                for f in range(FC):
                    P.op("vector", lambda v, f=f: v.tensor_tensor(
                        ao16[:, f * TS:(f + 1) * TS], qt[:, f * TS:(f + 1) * TS],
                        qt[:, f * TS:(f + 1) * TS], ALU.mult),
                        reads=(QB,), writes=(buf("ao16"),))

                def mmsy2(t):
                    last = None
                    for f in range(FC):
                        last = t.matmul(pb[3][0:1, 0:512], ones1s[:, 0:1],
                                        ao16[:, f * TS:(f + 1) * TS],
                                        start=(f == 0), stop=(f == FC - 1))
                    return last
                P.op("tensor", mmsy2, reads=(buf("ao16"), buf("consts"), PB[3]), writes=(PB[3],))
                P.op("vector", lambda v: v.tensor_scalar_mul(rowm, pb[2][0:1, 0:512], 1.0 / D),
                     reads=(PB[2],), writes=(buf("rowm"),))
                P.op("vector", lambda v: v.tensor_scalar_mul(rowq, pb[3][0:1, 0:512], 1.0 / D),
                     reads=(PB[3],), writes=(buf("rowq"),))
                P.op("tensor", lambda t: t.matmul(pb[2][:, 0:512], onesrs[0:1, :], rowm, start=True, stop=True),
                     reads=(buf("rowm"), buf("consts"), PB[2]), writes=(PB[2],))
                P.op("vector", lambda v: v.tensor_tensor(rowm, rowm, rowm, ALU.mult),
                     reads=(buf("rowm"),), writes=(buf("rowm"),), small=True)
                P.op("vector", lambda v: v.tensor_sub(rowq, rowq, rowm),
                     reads=(buf("rowq"), buf("rowm")), writes=(buf("rowq"),))
                P.op("scalar", lambda s: s.activation(rowm, rowq, AF.Ln, bias=cfs[0:1, 0:1]),
                     reads=(buf("rowq"), buf("consts")), writes=(buf("rowm"),))
                P.op("scalar", lambda s: s.activation(rowq, rowm, AF.Exp, scale=cfs[0:1, 1:2]),
                     reads=(buf("rowm"), buf("consts")), writes=(buf("rowq"),))
                P.op("tensor", lambda t: t.matmul(pb[3][:, 0:512], onesrs[0:1, :], rowq, start=True, stop=True),
                     reads=(buf("rowq"), buf("consts"), PB[3]), writes=(PB[3],))
                P.op("vector", lambda v: v.tensor_copy(r2b32[:, :], pb[3][:, 0:512]),
                     reads=(PB[3],), writes=(buf("r2b32"),))
                P.op("vector", lambda v: v.tensor_copy(mbf[:, :], pb[2][:, 0:512]),
                     reads=(PB[2],), writes=(buf("aux32"),))
            jobs.append(jstat)

            def jret(fpair):
                for f in (2 * fpair, 2 * fpair + 1):
                    eng = "vector" if f < 5 else "gpsimd"
                    tmp = tmpv if f < 5 else tmpg
                    tb = buf("tmpv") if f < 5 else buf("tmpg")
                    P.op(eng, lambda e2, f=f, tmp=tmp: e2.tensor_sub(
                        tmp[:, :], qt[:, f * TS:(f + 1) * TS], mbf[:, :]),
                        reads=(QB, buf("aux32")), writes=(tb,))
                    P.op(eng, lambda e2, tmp=tmp: e2.tensor_mul(
                        tmp[:, :], tmp[:, :], r2b32[:, :]),
                        reads=(tb, buf("r2b32")), writes=(tb,))
                    P.op(eng, lambda e2, f=f, tmp=tmp: e2.tensor_scalar(
                        tmp[:, :], tmp[:, :], gbs[:, f:f + 1], gbs[:, 8 + f:9 + f], ALU.mult, ALU.add),
                        reads=(tb, buf("consts")), writes=(tb,))
                    P.op(eng, lambda e2, f=f, t0=t0, tmp=tmp: e2.tensor_tensor(
                        x16[:, f * TPC + t0: f * TPC + t0 + TS], tmp[:, :],
                        x16[:, f * TPC + t0: f * TPC + t0 + TS], ALU.add),
                        reads=(tb, X16[tile]), writes=(X16[tile],))
            for fp in range(4):
                jobs.append(lambda fp=fp: jret(fp))
            return jobs

        if launch == 2:
            for j in piece_jobs("ln2f", ln2, LN2, (0,), "scalar"):
                j()
            for j in wpiece_jobs("wq_l", wS3, buf("wS3")):
                j()
            kv_ingest()
            for j in piece_jobs("x1f", x16, X16, (0,), "gpsimd"):
                j()
            P.dma("sync", "din", lambda s: s.dma_start(out=h16[:, :], in_=g["h16"].ap()),
                  writes=(buf("h16"),))
            for j in wpiece_jobs("wfc_l", wS4, buf("wS4")):
                j()
            for j in bjobs(0):
                j()
            for j in wpiece_jobs("wqt_f", wqt, buf("wqt")):
                j()
            jobs2 = deque()
            for tile in range(NT):
                if tile + 1 < NT:
                    jobs2.extend(piece_jobs("ln2f", ln2, LN2, (tile + 1,), "scalar"))
                    bj = bjobs(tile + 1)
                    jobs2.extend(bj[:8])
                    jobs2.extend(piece_jobs("x1f", x16, X16, (tile + 1,), "scalar"))
                    jobs2.extend(bj[8:])
                ffn_phase(tile, final=True, fill=jobs2)
                while jobs2:
                    jobs2.popleft()()

        P.wait("gpsimd", ("dout", P.cnt["dout"], False))
        if P.cnt["dout1"] > 0:
            P.wait("gpsimd", ("dout1", P.cnt["dout1"], False))

        with nc.Block() as block:
            @block.tensor
            def _(t):
                P.emit(t, "tensor", sems)

            @block.vector
            def _(v):
                P.emit(v, "vector", sems)

            @block.scalar
            def _(s):
                P.emit(s, "scalar", sems)

            @block.gpsimd
            def _(gp):
                P.emit(gp, "gpsimd", sems)

            @block.sync
            def _(s):
                P.emit(s, "sync", sems)

    return nc


def kernel(L0_attn_inputs, L0_attn_W_Q, L0_attn_W_K, L0_attn_W_V, L0_attn_W_fc,
           L0_attn_gamma, L0_attn_beta, L0_ffn_weight):
    x = np.asarray(L0_attn_inputs)[0]
    xT = np.ascontiguousarray(x.T)
    h16 = _hadamard_bf16()
    i128 = np.eye(128, dtype=ml_dtypes.bfloat16)
    ones1 = np.ones((128, 1), dtype=ml_dtypes.bfloat16)
    onesrow = np.ones((1, 128), dtype=np.float32)
    sel16 = _sel16_f32()
    gb = np.concatenate([
        np.asarray(L0_attn_gamma).reshape(8, 128).T,
        np.asarray(L0_attn_beta).reshape(8, 128).T], axis=1).astype(np.float32)
    gb = np.ascontiguousarray(gb)
    constf = np.zeros((128, 8), np.float32)
    constf[:, 0] = EPS
    constf[:, 1] = -0.5

    bust1 = 16 + (VERSION * 7) % 48
    bust2 = 17 + (VERSION * 5) % 48
    common = {"h16": h16, "i128": i128, "ones1": ones1, "onesrow": onesrow,
              "constf": constf}

    nc1 = build_nc(bust1, 1)
    wffn = np.ascontiguousarray(np.asarray(L0_ffn_weight, dtype=np.float32))
    base1 = dict(common)
    base1.update({
        "wk_l": np.ascontiguousarray(np.asarray(L0_attn_W_K, dtype=np.float32)),
        "wv_l": np.ascontiguousarray(np.asarray(L0_attn_W_V, dtype=np.float32)),
        "wffn_l": wffn,
        "bust": np.zeros((1, bust1), np.float32),
    })
    in_maps1 = []
    for c in range(NCORE):
        m = dict(base1)
        m["xT"] = np.ascontiguousarray(xT[:, c * TPC:(c + 1) * TPC])
        in_maps1.append(m)
    res1 = run_bass_kernel_spmd(nc1, in_maps1, core_ids=list(range(NCORE)))
    kvparts = np.concatenate(
        [np.asarray(res1.results[c]["kv_out"]) for c in range(NCORE)], axis=0)

    nc2 = build_nc(bust2, 2)
    base2 = dict(common)
    base2.update({
        "wq_l": np.ascontiguousarray(np.asarray(L0_attn_W_Q, dtype=np.float32)),
        "wfc_l": np.ascontiguousarray(np.asarray(L0_attn_W_fc, dtype=np.float32)),
        "gb": gb, "sel16": sel16,
        "wqt_f": np.ascontiguousarray(np.asarray(res1.results[0]["wqt_out"], dtype=np.float32)),
        "kvparts": np.ascontiguousarray(kvparts.astype(np.float32)),
        "bust": np.zeros((1, bust2), np.float32),
    })
    in_maps2 = []
    for c in range(NCORE):
        m = dict(base2)
        m["x1f"] = np.asarray(res1.results[c]["x1_out"])
        m["ln2f"] = np.asarray(res1.results[c]["ln2_out"])
        in_maps2.append(m)
    res2 = run_bass_kernel_spmd(nc2, in_maps2, core_ids=list(range(NCORE)))
    out = np.concatenate([r["yT"] for r in res2.results], axis=1)
    return np.ascontiguousarray(out.T).reshape(1, 16384, 1024).astype(np.float32)


# version 18
# speedup vs baseline: 1.1492x; 1.1492x over previous
VERSION = 20

import os
import numpy as np
import ml_dtypes
import concourse.bass as bass
import concourse.mybir as mybir
from concourse.bass_utils import run_bass_kernel_spmd
from contextlib import ExitStack
from collections import deque

F32 = mybir.dt.float32
BF16 = mybir.dt.bfloat16
AF = mybir.ActivationFunctionType
ALU = mybir.AluOpType
AX = mybir.AxisListType

D = 1024
NCORE = 8
TPC = 2048
NT = 4
TS = 512
NCH = 16
FC = 8
NH = 16
EPS = 1e-5
DBG = int(os.environ.get("KDBG", "9"))
DBG2 = int(os.environ.get("KDBG2", "9"))
FULLSYNC = os.environ.get("BASS_SIM_FP64") == "1"


def _hadamard_bf16():
    h = np.array([[1.0]], dtype=np.float64)
    while h.shape[0] < D:
        h = np.block([[h, h], [h, -h]])
    h = (h / 32.0).astype(np.float32)
    hp = h.reshape(FC, 128, D).transpose(1, 0, 2).reshape(128, FC * D)
    return hp.astype(ml_dtypes.bfloat16)


def _sel16_f32():
    s = np.zeros((16, FC * 128), np.float32)
    for p in range(FC):
        s[2 * p, p * 128: p * 128 + 64] = 1.0
        s[2 * p + 1, p * 128 + 64: p * 128 + 128] = 1.0
    return s.astype(ml_dtypes.bfloat16)


class Buf:
    __slots__ = ("name", "writer", "readers")

    def __init__(self, name):
        self.name = name
        self.writer = None
        self.readers = []


class Prog:
    ENGS = ("tensor", "vector", "scalar", "gpsimd", "sync")
    DMAQ = ("din", "din1", "dout", "dout1", "dcc")

    def __init__(self):
        self.ops = {e: [] for e in self.ENGS}
        self.cnt = {e: 0 for e in list(self.ENGS) + list(self.DMAQ)}
        self.hw = {e: {} for e in self.ENGS}
        self.selfhw = {e: 0 for e in self.ENGS}

    def _collect(self, eng, reads, writes):
        waits = {}

        def need(tok):
            if tok is None:
                return
            te, tv, sm = tok
            if te == eng and not FULLSYNC:
                if not (sm and (self.cnt[eng] - tv) < 3):
                    return
            if waits.get(te, -1) < tv:
                waits[te] = tv

        for b in reads:
            need(b.writer)
        for b in writes:
            need(b.writer)
            for t2 in b.readers:
                need(t2)
        out = []
        for e2, v in sorted(waits.items()):
            if e2 == eng:
                if v > self.selfhw[eng]:
                    out.append((e2, v))
                    self.selfhw[eng] = v
            elif self.hw[eng].get(e2, 0) < v:
                out.append((e2, v))
                self.hw[eng][e2] = v
        return out

    def op(self, eng, fn, reads=(), writes=(), small=False, after=()):
        waits = self._collect(eng, reads, writes)
        for tok in after:
            te, tv, _sm = tok
            if te == eng:
                continue
            if self.hw[eng].get(te, 0) < tv:
                waits.append((te, tv))
                self.hw[eng][te] = tv
        self.cnt[eng] += 1
        tok = (eng, self.cnt[eng], small)
        self.ops[eng].append((waits, fn, eng, 1))
        for b in writes:
            b.writer = tok
            b.readers = []
        for b in reads:
            b.readers.append(tok)
        return tok

    def dma(self, eng, q, fn, reads=(), writes=()):
        waits = self._collect(eng, reads, writes)
        if self.cnt[q] > 0 and self.hw[eng].get(q, 0) < self.cnt[q]:
            waits.append((q, self.cnt[q]))
            self.hw[eng][q] = self.cnt[q]
        self.cnt[q] += 16
        tok = (q, self.cnt[q], False)
        self.ops[eng].append((waits, fn, q, 16))
        for b in writes:
            b.writer = tok
            b.readers = []
        for b in reads:
            b.readers.append(tok)
        return tok

    def wait(self, eng, tok):
        te, tv, _ = tok
        if te == eng:
            if tv > self.selfhw[eng]:
                self.selfhw[eng] = tv
                self.ops[eng].append(([(te, tv)], None, None, 0))
            return
        if self.hw[eng].get(te, 0) < tv:
            self.hw[eng][te] = tv
            self.ops[eng].append(([(te, tv)], None, None, 0))

    def emit(self, engobj, eng, sems):
        for waits, fn, incsem, incval in self.ops[eng]:
            for (se, v) in waits:
                engobj.wait_ge(sems[se], v)
            if fn is None:
                continue
            inst = fn(engobj)
            if incval:
                inst.then_inc(sems[incsem], incval)


def build_nc(bust_len, launch):
    nc = bass.Bass("TRN2", target_bir_lowering=False)
    g = {}

    def din(name, shape, dt=F32):
        g[name] = nc.dram_tensor(name, shape, dt, kind="ExternalInput")

    din("h16", [128, FC * D], BF16)
    din("i128", [128, 128], BF16)
    din("ones1", [128, 1], BF16)
    din("onesrow", [1, 128])
    din("constf", [128, 8])
    din("bust", [1, bust_len])
    if launch == 1:
        din("xT", [D, TPC])
        din("wk_l", [D, D]); din("wv_l", [D, D]); din("wffn_l", [D, D])
        kv_out = nc.dram_tensor("kv_out", [128, 520], F32, kind="ExternalOutput")
        x1_out = nc.dram_tensor("x1_out", [D, TPC], F32, kind="ExternalOutput")
        ln2_out = nc.dram_tensor("ln2_out", [D, TPC], F32, kind="ExternalOutput")
        wqt_out = nc.dram_tensor("wqt_out", [128, FC * D], F32, kind="ExternalOutput")
    else:
        din("wq_l", [D, D]); din("wfc_l", [D, D])
        din("wqt_f", [128, FC * D])
        din("gb", [128, 16])
        din("sel16", [16, FC * 128], BF16)
        din("x1f", [D, TPC])
        din("ln2f", [D, TPC])
        din("kvparts", [8 * 128, 520])
        yT = nc.dram_tensor("yT", [D, TPC], F32, kind="ExternalOutput")

    P = Prog()
    with ExitStack() as ctx:
        e = ctx.enter_context
        ctx.enter_context(nc.allow_low_precision(
            reason="bf16 residual stream / operand casts are deliberate; "
                   "matmuls accumulate fp32 in PSUM"))

        def sb(name, shape, dt):
            return e(nc.sbuf_tensor(name, shape, dt))

        x16 = sb("x16", [128, FC * TPC], BF16)
        ln2 = sb("ln2s", [128, FC * TPC], BF16)
        h16 = sb("h16s", [128, FC * D], BF16)
        wqt = sb("wqt", [128, FC * D], BF16)
        wS3 = sb("wS3", [128, FC * D], BF16)
        wS4 = sb("wS4", [128, FC * D], BF16)
        SW = 2048 if launch == 1 else 1024
        stage = sb("stage", [128, SW], F32)
        LW = 2 if launch == 1 else 1
        PRW = 2 * D
        u32b = sb("u32b", [128, PRW], BF16)
        bb32 = sb("bb32", [128, PRW], BF16)
        v32 = sb("v32", [128, LW * PRW], BF16)
        sqb = sb("sqb", [128, 1040], F32)
        rs16 = sb("rs16", [128, PRW], BF16)
        xqT = sb("xqT", [128, FC * TS], BF16)
        if launch == 2:
            qt = sb("qt", [128, FC * TS], BF16)
            ao16 = sb("ao16", [128, FC * TS], BF16)
        if launch == 1:
            k16 = sb("k16", [128, D], BF16)
            v16 = sb("v16", [128, D], BF16)
            wbld = sb("wbld", [128, D], F32)
        statc = sb("statc", [128, 40], F32)
        ksum = sb("ksum", [128, 16], F32) if launch == 1 else None
        r2b32 = sb("r2b32", [128, TS], F32 if launch == 1 else BF16)
        aux32 = sb("aux32", [128, TS], F32)
        tmpv = sb("tmpv", [128, TS], BF16)
        tmpg = sb("tmpg", [128, TS], BF16)
        if launch == 2:
            rdrow = aux32[0:16, :]
            mbf = aux32
        kvsb = sb("kvsb", [128, 520], F32) if launch == 2 else stage[:, 0:520]
        rdb32 = kvsb[:, 0:512] if launch == 2 else None
        if launch == 2:
            kvbd = sb("kvbd", [128, FC * 128], BF16)
            dbd = sb("dbd", [128, FC * 16], BF16)
            rdr2 = sb("rdr2", [16, TS], BF16)
        rowmt = sb("rowmt", [1, TS], F32)
        rowqt = sb("rowqt", [1, TS], F32)
        rowm = rowmt[0:1, :]
        rowq = rowqt[0:1, :]
        i128s = sb("i128s", [128, 128], BF16)
        ones1s = sb("ones1s", [128, 1], BF16)
        onesrs = sb("onesrs", [1, 128], F32)
        if launch == 2:
            sel16s = sb("sel16s", [16, FC * 128], BF16)
            gbs = sb("gbs", [128, 16], F32)
        cfs = sb("cfs", [128, 8], F32)
        busts = sb("busts", [1, bust_len], F32)

        pb = [e(nc.psum_tensor(f"pb{i}", [128, 512], F32)) for i in range(8)]

        semnames = list(Prog.ENGS) + list(Prog.DMAQ)
        sems = {n: e(nc.semaphore("sem_" + n)) for n in semnames}

        B = {}

        def buf(nm):
            if nm not in B:
                B[nm] = Buf(nm)
            return B[nm]

        PB = [buf(f"pb{i}") for i in range(8)]
        X16 = [buf(f"x16_{t}") for t in range(NT)]
        LN2 = [buf(f"ln2_{t}") for t in range(NT)]
        SH = [buf("stage_0"), buf("stage_1"), buf("stage_2"), buf("stage_3")]
        _ldpc = [0]
        xtcnt = {}

        def piece_jobs(srcn, dstt, tilebufs, tiles, cpeng):
            ar = g[srcn].ap().rearrange("(f p) t -> f p t", p=128)
            pieces = [(t, f) for t in tiles for f in range(FC)]
            qs = []

            def issue(k):
                t, f = pieces[k]
                i = _ldpc[0] % 2
                _ldpc[0] += 1
                qs.append(i)
                P.dma("sync", "din" if i == 0 else "din1",
                      lambda s, f=f, t=t, i=i, ar=ar: s.dma_start(
                          out=stage[:, i * 512:(i + 1) * 512],
                          in_=ar[f][:, t * TS:(t + 1) * TS]),
                      reads=(SH[i],), writes=(SH[i],))

            def copy(k):
                t, f = pieces[k]
                i = qs[k]

                def fn(eng, f=f, t=t, i=i):
                    src = stage[:, i * 512:(i + 1) * 512]
                    dst = dstt[:, f * TPC + t * TS: f * TPC + (t + 1) * TS]
                    return eng.copy(dst, src) if cpeng == "scalar" \
                        else eng.tensor_copy(dst, src)
                P.op(cpeng, fn, reads=(SH[i],), writes=(tilebufs[t],))
                xtcnt[(srcn, t)] = xtcnt.get((srcn, t), 0) + 1

            n = len(pieces)
            jobs = [lambda: issue(0)]
            for k in range(n):
                def jk(k=k):
                    if k + 1 < n:
                        issue(k + 1)
                    copy(k)
                jobs.append(jk)
            for j in jobs:
                j.light = True
            return jobs

        def wpiece_jobs(srcn, slot, slotbuf, cpeng="gpsimd"):
            if srcn == "wqt_f":
                ar = None
            else:
                ar = g[srcn].ap().rearrange("(c p) o -> c p o", p=128)
            qs = []

            def issue(k):
                i = _ldpc[0] % 2
                _ldpc[0] += 1
                qs.append(i)
                if ar is None:
                    P.dma("sync", "din" if i == 0 else "din1",
                          lambda s, k=k, i=i: s.dma_start(
                              out=stage[:, i * 512:(i + 1) * 512],
                              in_=g["wqt_f"].ap()[:, k * 512:(k + 1) * 512]),
                          reads=(SH[i],), writes=(SH[i],))
                else:
                    c, h = k // 2, k % 2
                    P.dma("sync", "din" if i == 0 else "din1",
                          lambda s, c=c, h=h, i=i, ar=ar: s.dma_start(
                              out=stage[:, i * 512:(i + 1) * 512],
                              in_=ar[c][:, h * 512:(h + 1) * 512]),
                          reads=(SH[i],), writes=(SH[i],))

            def copy(k):
                i = qs[k]
                c, h = k // 2, k % 2
                base = k * 512 if ar is None else c * D + h * 512

                def fn(eng, base=base, i=i):
                    src = stage[:, i * 512:(i + 1) * 512]
                    return eng.copy(slot[:, base:base + 512], src) if cpeng == "scalar"                         else eng.tensor_copy(slot[:, base:base + 512], src)
                P.op(cpeng, fn, reads=(SH[i],), writes=(slotbuf,))

            jobs = [lambda: issue(0)]
            for k in range(16):
                def jk(k=k):
                    if k + 1 < 16:
                        issue(k + 1)
                    copy(k)
                jobs.append(jk)
            for j in jobs:
                j.light = True
            return jobs

        constlist = [(i128s, "i128"), (ones1s, "ones1"), (onesrs, "onesrow"),
                     (cfs, "constf"), (busts, "bust")]
        if launch == 2:
            constlist += [(sel16s, "sel16"), (gbs, "gb")]
        for (dst, src) in constlist:
            P.dma("sync", "din", lambda s, d=dst, so=src: s.dma_start(out=d[:, :], in_=g[so].ap()),
                  writes=(buf("consts"),))

        bleft = [FC]

        def mk_build():
            wr = g["wffn_l"].ap().rearrange("(c p) i -> c p i", p=128)
            wqtr = wqt_out.ap().rearrange("p (i o c) -> o i p c", i=8, o=8)
            WH = (SH[2], SH[3])

            def bdma(oc):
                P.dma("sync", "din1", lambda s, oc=oc: s.dma_start(
                    out=stage[:, 1024:2048], in_=wr[oc]), reads=WH, writes=WH)

            def bjob(oc):
                P.op("vector", lambda v: v.tensor_reduce(
                    statc[:, 0:1], stage[:, 1024:2048], AX.X, ALU.add,
                    apply_absolute_value=True),
                    reads=WH, writes=(buf("sc0"),), small=True)
                P.op("vector", lambda v: v.tensor_scalar(
                    statc[:, 1:2], statc[:, 0:1], 1.0 / D, 1e-5, ALU.mult, ALU.max),
                    reads=(buf("sc0"),), writes=(buf("sc1"),), small=True)
                P.op("vector", lambda v: v.reciprocal(statc[:, 2:3], statc[:, 1:2]),
                     reads=(buf("sc1"),), writes=(buf("sc2"),), small=True)
                P.op("vector", lambda v: v.tensor_single_scalar(
                    statc[:, 3:4], statc[:, 1:2], 0.5, ALU.mult),
                    reads=(buf("sc1"),), writes=(buf("sc3"),), small=True)
                P.op("scalar", lambda s: s.activation(
                    wbld[:, :], stage[:, 1024:2048], AF.Copy, scale=statc[:, 2:3]),
                    reads=WH + (buf("sc2"),), writes=(buf("wbld"),))
                P.op("vector", lambda v: v.tensor_scalar(
                    stage[:, 1024:2048], wbld[:, :], 0.5, 1.0, ALU.is_gt, ALU.mult),
                    reads=(buf("wbld"),), writes=WH)
                P.op("vector", lambda v: v.tensor_scalar(
                    wbld[:, :], wbld[:, :], -0.5, 1.0, ALU.is_lt, ALU.mult),
                    reads=(buf("wbld"),), writes=(buf("wbld"),))
                P.op("vector", lambda v: v.tensor_sub(
                    stage[:, 1024:2048], stage[:, 1024:2048], wbld[:, :]),
                    reads=WH + (buf("wbld"),), writes=WH)
                P.op("scalar", lambda s: s.activation(
                    tmpg[:, 0:512], stage[:, 1024:1536], AF.Copy, scale=statc[:, 3:4]),
                    reads=WH + (buf("sc3"),), writes=(buf("tmpg"),))
                P.op("scalar", lambda s: s.activation(
                    tmpv[:, 0:512], stage[:, 1536:2048], AF.Copy, scale=statc[:, 3:4]),
                    reads=WH + (buf("sc3"),), writes=(buf("tmpv"),))
                if oc + 1 < FC:
                    bdma(oc + 1)
                for ic in range(FC):
                    bk = 4 + (ic % 2)
                    srcop = tmpg if ic < 4 else tmpv
                    srcb = buf("tmpg") if ic < 4 else buf("tmpv")
                    co = (ic % 4) * 128
                    P.op("tensor", lambda t, co=co, bk=bk, srcop=srcop: t.matmul(
                        pb[bk][:, 0:128], srcop[:, co:co + 128], i128s[:, :],
                        start=True, stop=True),
                        reads=(srcb, buf("consts"), PB[bk]), writes=(PB[bk],))
                    P.op("scalar", lambda s, ic=ic, oc=oc, bk=bk: s.copy(
                        wqt[:, ic * D + oc * 128: ic * D + (oc + 1) * 128], pb[bk][:, 0:128]),
                        reads=(PB[bk],), writes=(buf("wqt"),))
                    rdst = r2b32 if ic < 4 else aux32
                    rb = buf("r2b32") if ic < 4 else buf("aux32")
                    j = ic % 4
                    P.op("scalar", lambda s, j=j, bk=bk, rdst=rdst: s.copy(
                        rdst[:, j * 128:(j + 1) * 128], pb[bk][:, 0:128]),
                        reads=(PB[bk],), writes=(rb,))
                for j in range(4):
                    P.dma("gpsimd", "dout", lambda gp, oc=oc, j=j: gp.dma_start(
                        out=wqtr[oc][j], in_=r2b32[:, j * 128:(j + 1) * 128]),
                        reads=(buf("r2b32"),))
                    P.dma("gpsimd", "dout1", lambda gp, oc=oc, j=j: gp.dma_start(
                        out=wqtr[oc][4 + j], in_=aux32[:, j * 128:(j + 1) * 128]),
                        reads=(buf("aux32"),))
                bleft[0] -= 1
            bdma(0)
            return [lambda oc=oc: bjob(oc) for oc in range(FC)]

        def kv_ingest():
            kvr = g["kvparts"].ap().rearrange("(r p) c -> r p c", p=128)
            P.op("vector", lambda v: v.memset(kvbd[:, :], 0.0), writes=(buf("kvbd"),))
            P.op("vector", lambda v: v.memset(dbd[:, :], 0.0), writes=(buf("dbd"),))

            def slotb(r):
                return (buf("sqbA"), buf("sqbB")) if r % 2 == 0 else (buf("sqbB"),)

            def dmain(r):
                off = (r % 2) * 520
                P.dma("sync", "din" if r % 2 == 0 else "din1",
                      lambda s, r=r, off=off: s.dma_start(
                          out=sqb[:, off:off + 520], in_=kvr[r]),
                      reads=slotb(r), writes=slotb(r))
            dmain(0)
            for r in range(8):
                if r + 1 < 8:
                    dmain(r + 1)
                off = (r % 2) * 520
                if r == 0:
                    P.op("gpsimd", lambda g2, off=off: g2.tensor_copy(
                        kvsb[:, :], sqb[:, off:off + 520]),
                        reads=slotb(r), writes=(buf("kvsb"),))
                else:
                    P.op("gpsimd", lambda g2, off=off, r=r: g2.tensor_tensor(
                        kvsb[:, :], sqb[:, off:off + 520], kvsb[:, :], ALU.add),
                        reads=slotb(r) + (buf("kvsb"),), writes=(buf("kvsb"),))
            for p_ in range(FC):
                P.op("gpsimd", lambda g2, p_=p_: g2.tensor_copy(
                    kvbd[0:64, p_ * 128: p_ * 128 + 64], kvsb[0:64, p_ * 64:(p_ + 1) * 64]),
                    reads=(buf("kvsb"),), writes=(buf("kvbd"),))
                P.op("gpsimd", lambda g2, p_=p_: g2.tensor_copy(
                    kvbd[64:128, p_ * 128 + 64: p_ * 128 + 128], kvsb[64:128, p_ * 64:(p_ + 1) * 64]),
                    reads=(buf("kvsb"),), writes=(buf("kvbd"),))
                P.op("gpsimd", lambda g2, p_=p_: g2.tensor_copy(
                    dbd[0:64, p_ * 16 + 2 * p_: p_ * 16 + 2 * p_ + 1], kvsb[0:64, 512 + p_:513 + p_]),
                    reads=(buf("kvsb"),), writes=(buf("dbd"),), small=True)
                P.op("gpsimd", lambda g2, p_=p_: g2.tensor_copy(
                    dbd[64:128, p_ * 16 + 2 * p_ + 1: p_ * 16 + 2 * p_ + 2], kvsb[64:128, 512 + p_:513 + p_]),
                    reads=(buf("kvsb"),), writes=(buf("dbd"),), small=True)

        def ffn_phase(tile, final, fill=None):
            def F():
                if fill:
                    fill.popleft()()
                    if fill and len(fill) > 16 and getattr(fill[0], "light", False):
                        fill.popleft()()
            t0 = tile * TS

            def emit_xh_g(gch):
                for half, bk in ((0, 0), (1, 1)):
                    def mmg(t, bk=bk, gch=gch, half=half):
                        last = None
                        for kc in range(FC):
                            last = t.matmul(pb[bk][:, 0:512],
                                            x16[:, kc * TPC + gch * 128: kc * TPC + (gch + 1) * 128],
                                            h16[:, kc * D + half * 512: kc * D + half * 512 + 512],
                                            start=(kc == 0), stop=(kc == FC - 1))
                        return last
                    P.op("tensor", mmg, reads=(X16[gch // 4], buf("h16"), PB[bk]), writes=(PB[bk],))
                    F()

            def emit_xh(cc):
                emit_xh_g(tile * 4 + cc)

            if final or tile == 0:
                emit_xh(0)
            for pp in range(2):
                if DBG2 < 1:
                    continue
                pph = (pp % 2) * PRW if LW == 2 else 0
                pbuf = str(pp % 2) if LW == 2 else "0"
                vb, bbb = buf("v32" + pbuf), buf("bb32")
                ubb = buf("u32b")
                for ci in range(2):
                    cc = 2 * pp + ci
                    co = pph + ci * D
                    t_m1 = P.op("vector", lambda v, ci=ci: v.tensor_scalar_mul(
                        statc[:, 8 + ci:9 + ci], pb[0][:, 0:1], 1.0 / 32),
                        reads=(PB[0],), writes=(buf(f"sc8{ci}"),), small=True)
                    t_a0 = P.op("vector", lambda v, ci=ci: v.tensor_reduce(
                        statc[:, 32 + ci:33 + ci], pb[0][:, 1:512], AX.X, ALU.max,
                        apply_absolute_value=True),
                        reads=(PB[0],), writes=(buf(f"sc32{ci}"),), small=True)
                    t_a1 = P.op("vector", lambda v, ci=ci: v.tensor_reduce(
                        statc[:, 34 + ci:35 + ci], pb[1][:, 0:512], AX.X, ALU.max,
                        apply_absolute_value=True),
                        reads=(PB[1],), writes=(buf(f"sc34{ci}"),), small=True)
                    t_sqa = P.op("scalar", lambda s: s.activation(
                        sqb[:, 0:512], pb[0][:, 0:512], AF.Square),
                        reads=(PB[0],), writes=(buf("sqbA"),), after=(t_m1, t_a0))
                    t_sqb = P.op("scalar", lambda s: s.activation(
                        sqb[:, 512:1024], pb[1][:, 0:512], AF.Square),
                        reads=(PB[1],), writes=(buf("sqbB"),), after=(t_a1,))
                    P.op("vector", lambda v, ci=ci: v.tensor_tensor(
                        statc[:, 36 + ci:37 + ci], statc[:, 32 + ci:33 + ci],
                        statc[:, 34 + ci:35 + ci], ALU.max),
                        reads=(buf(f"sc32{ci}"), buf(f"sc34{ci}")), writes=(buf(f"sc36{ci}"),), small=True)
                    P.op("vector", lambda v, ci=ci: v.tensor_single_scalar(
                        statc[:, 12 + ci:13 + ci], statc[:, 36 + ci:37 + ci], 1e-5, ALU.max),
                        reads=(buf(f"sc36{ci}"),), writes=(buf(f"sc12{ci}"),), small=True)
                    P.op("vector", lambda v, ci=ci: v.reciprocal(
                        statc[:, 14 + ci:15 + ci], statc[:, 12 + ci:13 + ci]),
                        reads=(buf(f"sc12{ci}"),), writes=(buf(f"sc14{ci}"),), small=True)
                    P.op("vector", lambda v, ci=ci: v.tensor_scalar_mul(
                        statc[:, 16 + ci:17 + ci], statc[:, 14 + ci:15 + ci], 7.0),
                        reads=(buf(f"sc14{ci}"),), writes=(buf(f"sc16{ci}"),), small=True)
                    P.op("vector", lambda v, co=co, ci=ci: v.tensor_scalar_mul(
                        v32[:, co:co + 512], pb[0][:, 0:512], statc[:, 16 + ci:17 + ci]),
                        reads=(PB[0], buf(f"sc16{ci}")), writes=(vb,), after=(t_sqa,))
                    P.op("vector", lambda v, co=co, ci=ci: v.tensor_scalar_mul(
                        v32[:, co + 512:co + 1024], pb[1][:, 0:512], statc[:, 16 + ci:17 + ci]),
                        reads=(PB[1], buf(f"sc16{ci}")), writes=(vb,), after=(t_sqb,))
                    if cc + 1 < 4:
                        emit_xh(cc + 1)
                    P.op("vector", lambda v, ci=ci: v.reduce_sum(
                        statc[:, 10 + ci:11 + ci], sqb[:, 0:D], AX.X),
                        reads=(buf("sqbA"), buf("sqbB")), writes=(buf(f"sc10{ci}"),), small=True)
                if DBG2 < 2:
                    continue
                P.op("vector", lambda v, pph=pph: v.memset(
                    v32[:, pph:pph + PRW].rearrange("p (c d) -> p c d", d=D)[:, :, 0:1], 0.0),
                    writes=(vb,), small=True)
                P.op("vector", lambda v: v.tensor_tensor(
                    statc[:, 18:20], statc[:, 8:10], statc[:, 8:10], ALU.mult),
                    reads=(buf("sc80"), buf("sc81")), writes=(buf("sc18"),), small=True)
                P.op("vector", lambda v: v.tensor_single_scalar(
                    statc[:, 20:22], statc[:, 10:12], 1.0 / D, ALU.mult),
                    reads=(buf("sc100"), buf("sc101")), writes=(buf("sc20"),), small=True)
                P.op("vector", lambda v: v.tensor_sub(statc[:, 22:24], statc[:, 20:22], statc[:, 18:20]),
                     reads=(buf("sc20"), buf("sc18")), writes=(buf("sc22"),), small=True)
                P.op("scalar", lambda s: s.activation(statc[:, 24:26], statc[:, 22:24], AF.Ln, bias=cfs[:, 0:1]),
                     reads=(buf("sc22"), buf("consts")), writes=(buf("sc24"),), small=True)
                P.op("scalar", lambda s: s.activation(statc[:, 26:28], statc[:, 24:26], AF.Exp, scale=cfs[:, 1:2]),
                     reads=(buf("sc24"), buf("consts")), writes=(buf("sc26"),), small=True)
                P.op("vector", lambda v: v.tensor_tensor(
                    statc[:, 28:30], statc[:, 12:14], statc[:, 26:28], ALU.mult),
                    reads=(buf("sc120"), buf("sc121"), buf("sc26")), writes=(buf("sc28"),), small=True)
                P.op("vector", lambda v: v.tensor_single_scalar(
                    statc[:, 28:30], statc[:, 28:30], 1.0 / 7, ALU.mult),
                    reads=(buf("sc28"),), writes=(buf("sc28"),), small=True)
                if DBG2 < 3:
                    continue
                P.op("scalar", lambda s, pph=pph: s.activation(
                    u32b[:, :], v32[:, pph:pph + PRW], AF.Abs),
                    reads=(vb,), writes=(ubb,))
                P.op("vector", lambda v, pph=pph: v.tensor_scalar(
                    rs16[:, :], v32[:, pph:pph + PRW], 0.0, 2.0, ALU.is_ge, ALU.mult),
                    reads=(vb,), writes=(buf("rs16"),))
                P.op("vector", lambda v: v.tensor_single_scalar(
                    rs16[:, :], rs16[:, :], -1.0, ALU.add),
                    reads=(buf("rs16"),), writes=(buf("rs16"),))
                P.op("vector", lambda v: v.tensor_scalar(
                    bb32[:, :], u32b[:, :], 3.5, 4.0, ALU.is_ge, ALU.mult),
                    reads=(ubb,), writes=(bbb,))
                P.op("vector", lambda v: v.tensor_sub(u32b[:, :], u32b[:, :], bb32[:, :]),
                     reads=(ubb, bbb), writes=(ubb,))
                P.op("vector", lambda v: v.tensor_scalar(
                    bb32[:, :], u32b[:, :], 1.5, 2.0, ALU.is_ge, ALU.mult),
                    reads=(ubb,), writes=(bbb,))
                P.op("vector", lambda v: v.tensor_sub(u32b[:, :], u32b[:, :], bb32[:, :]),
                     reads=(ubb, bbb), writes=(ubb,))
                P.op("vector", lambda v: v.tensor_scalar(
                    bb32[:, :], u32b[:, :], 0.5, 1.0, ALU.is_ge, ALU.mult),
                    reads=(ubb,), writes=(bbb,))
                P.op("vector", lambda v: v.tensor_sub(u32b[:, :], u32b[:, :], bb32[:, :]),
                     reads=(ubb, bbb), writes=(ubb,))
                P.op("vector", lambda v: v.tensor_mul(bb32[:, :], rs16[:, :], u32b[:, :]),
                     reads=(buf("rs16"), ubb), writes=(bbb,))
                P.op("vector", lambda v, pph=pph: v.tensor_sub(
                    v32[:, pph:pph + PRW], v32[:, pph:pph + PRW], bb32[:, :]),
                    reads=(vb, bbb), writes=(vb,))
                if DBG2 < 4:
                    continue
                for ci in range(2):
                    cc = 2 * pp + ci
                    co = pph + ci * D
                    P.op("vector", lambda v, co=co, ci=ci: v.tensor_scalar_mul(
                        rs16[:, ci * D:(ci + 1) * D], v32[:, co:co + D], statc[:, 28 + ci:29 + ci]),
                        reads=(vb, buf("sc28")), writes=(buf("rs16"),))
                for f in range(FC):
                    bk = 4 + (f % 2)
                    for ci in range(2):
                        cc = 2 * pp + ci
                        P.op("tensor", lambda t, f=f, ci=ci, bk=bk: t.matmul(
                            pb[bk][:, ci * 128:(ci + 1) * 128], rs16[:, ci * D + f * 128: ci * D + (f + 1) * 128],
                            i128s[:, :], start=True, stop=True),
                            reads=(buf("rs16"), buf("consts"), PB[bk]), writes=(PB[bk],))
                    P.op("scalar", lambda s, f=f, pp=pp, bk=bk: s.copy(
                        xqT[:, f * TS + pp * 256: f * TS + (pp + 1) * 256], pb[bk][:, 0:256]),
                        reads=(PB[bk],), writes=(buf("xqT"),))
                    F()
            if not final and tile + 1 < NT:
                while xtcnt.get(("xT", tile + 1), 0) < FC and fill:
                    fill.popleft()()
                emit_xh_g((tile + 1) * 4)
            if launch == 1 and tile == 0:
                while bleft[0] > 0 and fill:
                    fill.popleft()()
            for oc in range(FC if DBG2 >= 5 else 0):
                bk = 4 + (oc % 2)

                def mmo(t, oc=oc, bk=bk):
                    last = None
                    for ic in range(FC):
                        last = t.matmul(pb[bk][:, 0:512], wqt[:, ic * D + oc * 128: ic * D + (oc + 1) * 128],
                                        xqT[:, ic * TS:(ic + 1) * TS],
                                        start=(ic == 0), stop=(ic == FC - 1))
                    return last
                P.op("tensor", mmo, reads=(buf("wqt"), buf("xqT"), PB[bk]), writes=(PB[bk],))
                F()
                if not final:
                    P.op("vector", lambda v, oc=oc, bk=bk, t0=t0: v.tensor_tensor(
                        x16[:, oc * TPC + t0: oc * TPC + t0 + TS], pb[bk][:, 0:512],
                        x16[:, oc * TPC + t0: oc * TPC + t0 + TS], ALU.add),
                        reads=(PB[bk], X16[tile]), writes=(X16[tile],))
                else:
                    i = oc % 2
                    hb = buf("sqbA") if i == 0 else buf("sqbB")
                    P.op("vector", lambda v, oc=oc, bk=bk, t0=t0, i=i: v.tensor_tensor(
                        sqb[:, i * TS:(i + 1) * TS], pb[bk][:, 0:512],
                        x16[:, oc * TPC + t0: oc * TPC + t0 + TS], ALU.add),
                        reads=(PB[bk], X16[tile], hb), writes=(hb,))
                    P.dma("gpsimd", "dout" if oc % 2 == 0 else "dout1",
                          lambda gp, oc=oc, tile=tile, i=i: gp.dma_start(
                              out=yT.ap()[oc * 128:(oc + 1) * 128, tile * TS:(tile + 1) * TS],
                              in_=sqb[:, i * TS:(i + 1) * TS]),
                          reads=(hb,))

        def ln2_block(tile):
            t0 = tile * TS

            def mmsum(t, t0=t0):
                last = None
                for f in range(FC):
                    last = t.matmul(pb[4][0:1, 0:512], ones1s[:, 0:1],
                                    x16[:, f * TPC + t0: f * TPC + t0 + TS],
                                    start=(f == 0), stop=(f == FC - 1))
                return last
            P.op("tensor", mmsum, reads=(X16[tile], buf("consts"), PB[4]), writes=(PB[4],))
            for f in range(FC):
                P.op("scalar", lambda s, f=f, t0=t0: s.activation(
                    xqT[:, f * TS:(f + 1) * TS], x16[:, f * TPC + t0: f * TPC + t0 + TS], AF.Square),
                    reads=(X16[tile],), writes=(buf("xqT"),))

            def mmsq(t):
                last = None
                for f in range(FC):
                    last = t.matmul(pb[5][0:1, 0:512], ones1s[:, 0:1],
                                    xqT[:, f * TS:(f + 1) * TS],
                                    start=(f == 0), stop=(f == FC - 1))
                return last
            P.op("tensor", mmsq, reads=(buf("xqT"), buf("consts"), PB[5]), writes=(PB[5],))
            P.op("vector", lambda v: v.tensor_scalar_mul(rowm, pb[4][0:1, 0:512], 1.0 / D),
                 reads=(PB[4],), writes=(buf("rowm"),))
            P.op("vector", lambda v: v.tensor_scalar_mul(rowq, pb[5][0:1, 0:512], 1.0 / D),
                 reads=(PB[5],), writes=(buf("rowq"),))
            P.op("tensor", lambda t: t.matmul(pb[4][:, 0:512], onesrs[0:1, :], rowm, start=True, stop=True),
                 reads=(buf("rowm"), buf("consts"), PB[4]), writes=(PB[4],))
            P.op("vector", lambda v: v.tensor_tensor(rowm, rowm, rowm, ALU.mult),
                 reads=(buf("rowm"),), writes=(buf("rowm"),), small=True)
            P.op("vector", lambda v: v.tensor_sub(rowq, rowq, rowm),
                 reads=(buf("rowq"), buf("rowm")), writes=(buf("rowq"),))
            P.op("scalar", lambda s: s.activation(rowm, rowq, AF.Ln, bias=cfs[0:1, 0:1]),
                 reads=(buf("rowq"), buf("consts")), writes=(buf("rowm"),))
            P.op("scalar", lambda s: s.activation(rowq, rowm, AF.Exp, scale=cfs[0:1, 1:2]),
                 reads=(buf("rowm"), buf("consts")), writes=(buf("rowq"),))
            P.op("tensor", lambda t: t.matmul(pb[5][:, 0:512], onesrs[0:1, :], rowq, start=True, stop=True),
                 reads=(buf("rowq"), buf("consts"), PB[5]), writes=(PB[5],))
            P.op("vector", lambda v: v.tensor_copy(r2b32[:, :], pb[5][:, 0:512]),
                 reads=(PB[5],), writes=(buf("r2b32"),))
            P.op("vector", lambda v: v.tensor_copy(aux32[:, :], pb[4][:, 0:512]),
                 reads=(PB[4],), writes=(buf("aux32"),))
            for f in range(FC):
                eng = "vector" if f < 5 else "gpsimd"
                tmp = tmpv if f < 5 else tmpg
                tb = buf("tmpv") if f < 5 else buf("tmpg")
                P.op(eng, lambda e2, f=f, t0=t0, tmp=tmp: e2.tensor_sub(
                    tmp[:, :], x16[:, f * TPC + t0: f * TPC + t0 + TS], aux32[:, :]),
                    reads=(X16[tile], buf("aux32")), writes=(tb,))
                P.op(eng, lambda e2, f=f, t0=t0, tmp=tmp: e2.tensor_mul(
                    ln2[:, f * TPC + t0: f * TPC + t0 + TS], tmp[:, :], r2b32[:, :]),
                    reads=(tb, buf("r2b32")), writes=(LN2[tile],))

        def bnd_out(srct, outt, sbuf_, f, t, qi):
            hb = buf("sqbA") if qi == 0 else buf("sqbB")
            P.op("scalar",
                 lambda s, f=f, t=t, qi=qi, srct=srct: s.copy(
                     sqb[:, qi * 512:(qi + 1) * 512],
                     srct[:, f * TPC + t * TS: f * TPC + (t + 1) * TS]),
                 reads=(sbuf_, hb), writes=(hb,))
            P.dma("gpsimd", "dout" if qi % 2 == 0 else "dout1",
                  lambda gp, f=f, t=t, qi=qi, outt=outt: gp.dma_start(
                      out=outt.ap()[f * 128:(f + 1) * 128, t * TS:(t + 1) * TS],
                      in_=sqb[:, qi * 512:(qi + 1) * 512]),
                  reads=(hb,))

        def kv_jobs(tile):
            jobs = []
            for cc in range(4):
                gch = tile * 4 + cc

                def ja(gch=gch, tile=tile):
                    for half in range(2):
                        def mmk(t, gch=gch, half=half):
                            last = None
                            for kc in range(FC):
                                last = t.matmul(pb[2 + half][:, 0:512],
                                                ln2[:, kc * TPC + gch * 128: kc * TPC + (gch + 1) * 128],
                                                wS3[:, kc * D + half * 512: kc * D + half * 512 + 512],
                                                start=(kc == 0), stop=(kc == FC - 1))
                            return last
                        P.op("tensor", mmk, reads=(LN2[tile], buf("wS3"), PB[2 + half]), writes=(PB[2 + half],))
                    P.op("scalar", lambda s: s.activation(k16[:, 0:512], pb[2][:, 0:512], AF.Sigmoid),
                         reads=(PB[2],), writes=(buf("k16"),))
                    P.op("scalar", lambda s: s.activation(k16[:, 512:1024], pb[3][:, 0:512], AF.Sigmoid),
                         reads=(PB[3],), writes=(buf("k16"),))

                def jb(gch=gch, tile=tile):
                    for half in range(2):
                        def mmv(t, gch=gch, half=half):
                            last = None
                            for kc in range(FC):
                                last = t.matmul(pb[2 + half][:, 0:512],
                                                ln2[:, kc * TPC + gch * 128: kc * TPC + (gch + 1) * 128],
                                                wS4[:, kc * D + half * 512: kc * D + half * 512 + 512],
                                                start=(kc == 0), stop=(kc == FC - 1))
                            return last
                        P.op("tensor", mmv, reads=(LN2[tile], buf("wS4"), PB[2 + half]), writes=(PB[2 + half],))
                    P.op("scalar", lambda s: s.copy(v16[:, 0:512], pb[2][:, 0:512]),
                         reads=(PB[2],), writes=(buf("v16"),))
                    P.op("scalar", lambda s: s.copy(v16[:, 512:1024], pb[3][:, 0:512]),
                         reads=(PB[3],), writes=(buf("v16"),))

                def jc(gch=gch):
                    def mmkv(t, gch=gch):
                        last = None
                        for p_ in range(FC):
                            bank = pb[6] if p_ < 4 else pb[7]
                            colo = (p_ % 4) * 128
                            last = t.matmul(bank[:, colo:colo + 128],
                                            k16[:, p_ * 128:(p_ + 1) * 128],
                                            v16[:, p_ * 128:(p_ + 1) * 128],
                                            start=(gch == 0 and (p_ % 4) == 0),
                                            stop=(gch == NCH - 1 and (p_ % 4) == 3))
                        return last
                    P.op("tensor", mmkv, reads=(buf("k16"), buf("v16"), PB[6], PB[7]),
                         writes=(PB[6], PB[7]))

                    def mmks(t):
                        last = None
                        for p_ in range(FC):
                            last = t.matmul(pb[2][:, p_:p_ + 1], k16[:, p_ * 128:(p_ + 1) * 128],
                                            ones1s[:, 0:1], start=(p_ == 0), stop=(p_ == FC - 1))
                        return last
                    P.op("tensor", mmks, reads=(buf("k16"), buf("consts"), PB[2]), writes=(PB[2],))
                    sks, dks = (gch % 2) * 8, ((gch + 1) % 2) * 8
                    if gch == 0:
                        P.op("vector", lambda v, dks=dks: v.tensor_copy(ksum[:, dks:dks + 8], pb[2][:, 0:8]),
                             reads=(PB[2],), writes=(buf("ksum"),), small=True)
                    else:
                        P.op("vector", lambda v, sks=sks, dks=dks: v.tensor_tensor(
                            ksum[:, dks:dks + 8], pb[2][:, 0:8], ksum[:, sks:sks + 8], ALU.add),
                            reads=(PB[2], buf("ksum")), writes=(buf("ksum"),), small=True)
                jobs += [ja, jb, jc]
            return jobs

        if launch == 1:
            P.dma("sync", "din", lambda s: s.dma_start(out=h16[:, :], in_=g["h16"].ap()),
                  writes=(buf("h16"),))
            for j in piece_jobs("xT", x16, X16, (0,), "scalar"):
                j()
            jobs = deque()
            if DBG >= 2:
                jobs.extend(mk_build())
            else:
                bleft[0] = 0
            jobs.extend(wpiece_jobs("wk_l", wS3, buf("wS3"), "scalar"))
            jobs.extend(piece_jobs("xT", x16, X16, (1,), "scalar"))
            jobs.extend(wpiece_jobs("wv_l", wS4, buf("wS4"), "scalar"))
            jobs.extend(piece_jobs("xT", x16, X16, (2,), "scalar"))
            jobs.extend(piece_jobs("xT", x16, X16, (3,), "scalar"))
            if DBG >= 3:
                qrr = [0]
                for tile in range(NT):
                    while xtcnt.get(("xT", tile), 0) < FC:
                        jobs.popleft()()
                    ffn_phase(tile, final=False, fill=jobs)
                    if DBG >= 4:
                        ln2_block(tile)
                        for f in range(FC):
                            bnd_out(x16, x1_out, X16[tile], f, tile, qrr[0] % 2)
                            qrr[0] += 1
                            bnd_out(ln2, ln2_out, LN2[tile], f, tile, qrr[0] % 2)
                            qrr[0] += 1
                        if DBG >= 5:
                            jobs.extend(kv_jobs(tile))
                while jobs:
                    jobs.popleft()()

        if launch == 1:
            for p_ in range(FC if DBG >= 5 else 0):
                bank = pb[6] if p_ < 4 else pb[7]
                pbuf = PB[6] if p_ < 4 else PB[7]
                colo = (p_ % 4) * 128
                rd = (pbuf, SH[0], SH[1]) if p_ == 0 else (pbuf,)
                P.op("vector", lambda v, p_=p_, bank=bank, colo=colo: v.tensor_copy(
                    kvsb[0:64, p_ * 64:(p_ + 1) * 64], bank[0:64, colo:colo + 64]),
                    reads=rd, writes=(buf("kvsb"), SH[0], SH[1]))
                P.op("vector", lambda v, p_=p_, bank=bank, colo=colo: v.tensor_copy(
                    kvsb[64:128, p_ * 64:(p_ + 1) * 64], bank[64:128, colo + 64:colo + 128]),
                    reads=(pbuf,), writes=(buf("kvsb"), SH[0], SH[1]))
            if DBG >= 5:
                P.op("vector", lambda v: v.tensor_copy(kvsb[:, 512:520], ksum[:, 0:8]),
                     reads=(buf("ksum"),), writes=(buf("kvsb"), SH[0], SH[1]))
                P.dma("gpsimd", "dout", lambda gp: gp.dma_start(out=kv_out.ap(), in_=kvsb[:, :]),
                      reads=(buf("kvsb"),))

        def bjobs(tile):
            t0 = tile * TS
            QB = buf("qt")
            jobs = []

            def jq(oc):
                bk = 2 + (oc % 2)

                def mmq(t, oc=oc, t0=t0, bk=bk):
                    last = None
                    for kc in range(FC):
                        last = t.matmul(pb[bk][:, 0:512],
                                        wS3[:, kc * D + oc * 128: kc * D + (oc + 1) * 128],
                                        ln2[:, kc * TPC + t0: kc * TPC + t0 + TS],
                                        start=(kc == 0), stop=(kc == FC - 1))
                    return last
                P.op("tensor", mmq, reads=(buf("wS3"), LN2[tile], PB[bk]), writes=(PB[bk],))
                P.op("scalar", lambda s, oc=oc, bk=bk: s.activation(
                    qt[:, oc * TS:(oc + 1) * TS], pb[bk][:, 0:512], AF.Sigmoid),
                    reads=(PB[bk],), writes=(QB,))
            for oc in range(FC):
                jobs.append(lambda oc=oc: jq(oc))

            def jden():
                def mmden(t):
                    last = None
                    for p_ in range(FC):
                        last = t.matmul(pb[2][0:16, 0:512], dbd[:, p_ * 16:(p_ + 1) * 16],
                                        qt[:, p_ * TS:(p_ + 1) * TS],
                                        start=(p_ == 0), stop=(p_ == FC - 1))
                    return last
                P.op("tensor", mmden, reads=(buf("dbd"), QB, PB[2]), writes=(PB[2],))
                P.op("vector", lambda v: v.tensor_scalar_add(rdrow[:, :], pb[2][0:16, 0:512], 1e-8),
                     reads=(PB[2],), writes=(buf("aux32"),))
                P.op("vector", lambda v: v.reciprocal(rdr2[0:16, :], rdrow[:, :]),
                     reads=(buf("aux32"),), writes=(buf("rdr2"),))
            jobs.append(jden)

            def jnum(p_):
                P.op("tensor", lambda t, p_=p_: t.matmul(
                    pb[2][:, 0:512], kvbd[:, p_ * 128:(p_ + 1) * 128],
                    qt[:, p_ * TS:(p_ + 1) * TS], start=True, stop=True),
                    reads=(buf("kvbd"), QB, PB[2]), writes=(PB[2],))
                P.op("tensor", lambda t, p_=p_: t.matmul(
                    pb[3][:, 0:512], sel16s[:, p_ * 128:(p_ + 1) * 128], rdr2[0:16, :],
                    start=True, stop=True),
                    reads=(buf("rdr2"), buf("consts"), PB[3]), writes=(PB[3],))
                P.op("vector", lambda v: v.tensor_copy(rdb32[:, :], pb[3][:, 0:512]),
                     reads=(PB[3], buf("kvsb")), writes=(buf("rdb32"),))
                P.op("vector", lambda v, p_=p_: v.tensor_mul(
                    ao16[:, p_ * TS:(p_ + 1) * TS], pb[2][:, 0:512], rdb32[:, :]),
                    reads=(PB[2], buf("rdb32")), writes=(buf("ao16"),))
            for p_ in range(FC):
                jobs.append(lambda p_=p_: jnum(p_))

            def jfc(oc):
                bk = 6 + (oc % 2)

                def mmfc(t, oc=oc, bk=bk):
                    last = None
                    for hc in range(FC):
                        last = t.matmul(pb[bk][:, 0:512],
                                        wS4[:, hc * D + oc * 128: hc * D + (oc + 1) * 128],
                                        ao16[:, hc * TS:(hc + 1) * TS],
                                        start=(hc == 0), stop=(hc == FC - 1))
                    return last
                P.op("tensor", mmfc, reads=(buf("wS4"), buf("ao16"), PB[bk]), writes=(PB[bk],))
                P.op("vector", lambda v, oc=oc, t0=t0, bk=bk: v.tensor_tensor(
                    qt[:, oc * TS:(oc + 1) * TS], pb[bk][:, 0:512],
                    ln2[:, oc * TPC + t0: oc * TPC + t0 + TS], ALU.add),
                    reads=(PB[bk], LN2[tile]), writes=(QB,))
            for oc in range(FC):
                jobs.append(lambda oc=oc: jfc(oc))

            def jstat():
                def mmsy(t):
                    last = None
                    for f in range(FC):
                        last = t.matmul(pb[2][0:1, 0:512], ones1s[:, 0:1],
                                        qt[:, f * TS:(f + 1) * TS],
                                        start=(f == 0), stop=(f == FC - 1))
                    return last
                P.op("tensor", mmsy, reads=(QB, buf("consts"), PB[2]), writes=(PB[2],))
                for f in range(FC):
                    P.op("scalar", lambda s, f=f: s.activation(
                        ao16[:, f * TS:(f + 1) * TS], qt[:, f * TS:(f + 1) * TS], AF.Square),
                        reads=(QB,), writes=(buf("ao16"),))

                def mmsy2(t):
                    last = None
                    for f in range(FC):
                        last = t.matmul(pb[3][0:1, 0:512], ones1s[:, 0:1],
                                        ao16[:, f * TS:(f + 1) * TS],
                                        start=(f == 0), stop=(f == FC - 1))
                    return last
                P.op("tensor", mmsy2, reads=(buf("ao16"), buf("consts"), PB[3]), writes=(PB[3],))
                P.op("vector", lambda v: v.tensor_scalar_mul(rowm, pb[2][0:1, 0:512], 1.0 / D),
                     reads=(PB[2],), writes=(buf("rowm"),))
                P.op("vector", lambda v: v.tensor_scalar_mul(rowq, pb[3][0:1, 0:512], 1.0 / D),
                     reads=(PB[3],), writes=(buf("rowq"),))
                P.op("tensor", lambda t: t.matmul(pb[2][:, 0:512], onesrs[0:1, :], rowm, start=True, stop=True),
                     reads=(buf("rowm"), buf("consts"), PB[2]), writes=(PB[2],))
                P.op("vector", lambda v: v.tensor_tensor(rowm, rowm, rowm, ALU.mult),
                     reads=(buf("rowm"),), writes=(buf("rowm"),), small=True)
                P.op("vector", lambda v: v.tensor_sub(rowq, rowq, rowm),
                     reads=(buf("rowq"), buf("rowm")), writes=(buf("rowq"),))
                P.op("scalar", lambda s: s.activation(rowm, rowq, AF.Ln, bias=cfs[0:1, 0:1]),
                     reads=(buf("rowq"), buf("consts")), writes=(buf("rowm"),))
                P.op("scalar", lambda s: s.activation(rowq, rowm, AF.Exp, scale=cfs[0:1, 1:2]),
                     reads=(buf("rowm"), buf("consts")), writes=(buf("rowq"),))
                P.op("tensor", lambda t: t.matmul(pb[3][:, 0:512], onesrs[0:1, :], rowq, start=True, stop=True),
                     reads=(buf("rowq"), buf("consts"), PB[3]), writes=(PB[3],))
                P.op("vector", lambda v: v.tensor_copy(r2b32[:, :], pb[3][:, 0:512]),
                     reads=(PB[3],), writes=(buf("r2b32"),))
                P.op("vector", lambda v: v.tensor_copy(mbf[:, :], pb[2][:, 0:512]),
                     reads=(PB[2],), writes=(buf("aux32"),))
            jobs.append(jstat)

            def jret(fpair):
                for f in (2 * fpair, 2 * fpair + 1):
                    eng = "vector" if f < 5 else "gpsimd"
                    tmp = tmpv if f < 5 else tmpg
                    tb = buf("tmpv") if f < 5 else buf("tmpg")
                    P.op(eng, lambda e2, f=f, tmp=tmp: e2.tensor_sub(
                        tmp[:, :], qt[:, f * TS:(f + 1) * TS], mbf[:, :]),
                        reads=(QB, buf("aux32")), writes=(tb,))
                    P.op(eng, lambda e2, tmp=tmp: e2.tensor_mul(
                        tmp[:, :], tmp[:, :], r2b32[:, :]),
                        reads=(tb, buf("r2b32")), writes=(tb,))
                    P.op(eng, lambda e2, f=f, tmp=tmp: e2.tensor_scalar(
                        tmp[:, :], tmp[:, :], gbs[:, f:f + 1], gbs[:, 8 + f:9 + f], ALU.mult, ALU.add),
                        reads=(tb, buf("consts")), writes=(tb,))
                    P.op(eng, lambda e2, f=f, t0=t0, tmp=tmp: e2.tensor_tensor(
                        x16[:, f * TPC + t0: f * TPC + t0 + TS], tmp[:, :],
                        x16[:, f * TPC + t0: f * TPC + t0 + TS], ALU.add),
                        reads=(tb, X16[tile]), writes=(X16[tile],))
            for fp in range(4):
                jobs.append(lambda fp=fp: jret(fp))
            return jobs

        if launch == 2:
            for j in piece_jobs("ln2f", ln2, LN2, (0,), "scalar"):
                j()
            for j in wpiece_jobs("wq_l", wS3, buf("wS3")):
                j()
            kv_ingest()
            for j in piece_jobs("x1f", x16, X16, (0,), "gpsimd"):
                j()
            P.dma("sync", "din", lambda s: s.dma_start(out=h16[:, :], in_=g["h16"].ap()),
                  writes=(buf("h16"),))
            for j in wpiece_jobs("wfc_l", wS4, buf("wS4")):
                j()
            for j in bjobs(0):
                j()
            for j in wpiece_jobs("wqt_f", wqt, buf("wqt")):
                j()
            jobs2 = deque()
            for tile in range(NT):
                if tile + 1 < NT:
                    jobs2.extend(piece_jobs("ln2f", ln2, LN2, (tile + 1,), "scalar"))
                    bj = bjobs(tile + 1)
                    jobs2.extend(bj[:8])
                    jobs2.extend(piece_jobs("x1f", x16, X16, (tile + 1,), "scalar"))
                    jobs2.extend(bj[8:])
                ffn_phase(tile, final=True, fill=jobs2)
                while jobs2:
                    jobs2.popleft()()

        P.wait("gpsimd", ("dout", P.cnt["dout"], False))
        if P.cnt["dout1"] > 0:
            P.wait("gpsimd", ("dout1", P.cnt["dout1"], False))

        with nc.Block() as block:
            @block.tensor
            def _(t):
                P.emit(t, "tensor", sems)

            @block.vector
            def _(v):
                P.emit(v, "vector", sems)

            @block.scalar
            def _(s):
                P.emit(s, "scalar", sems)

            @block.gpsimd
            def _(gp):
                P.emit(gp, "gpsimd", sems)

            @block.sync
            def _(s):
                P.emit(s, "sync", sems)

    return nc


def kernel(L0_attn_inputs, L0_attn_W_Q, L0_attn_W_K, L0_attn_W_V, L0_attn_W_fc,
           L0_attn_gamma, L0_attn_beta, L0_ffn_weight):
    x = np.asarray(L0_attn_inputs)[0]
    xT = np.ascontiguousarray(x.T)
    h16 = _hadamard_bf16()
    i128 = np.eye(128, dtype=ml_dtypes.bfloat16)
    ones1 = np.ones((128, 1), dtype=ml_dtypes.bfloat16)
    onesrow = np.ones((1, 128), dtype=np.float32)
    sel16 = _sel16_f32()
    gb = np.concatenate([
        np.asarray(L0_attn_gamma).reshape(8, 128).T,
        np.asarray(L0_attn_beta).reshape(8, 128).T], axis=1).astype(np.float32)
    gb = np.ascontiguousarray(gb)
    constf = np.zeros((128, 8), np.float32)
    constf[:, 0] = EPS
    constf[:, 1] = -0.5

    bust1 = 16 + (VERSION * 7) % 48
    bust2 = 17 + (VERSION * 5) % 48
    common = {"h16": h16, "i128": i128, "ones1": ones1, "onesrow": onesrow,
              "constf": constf}

    nc1 = build_nc(bust1, 1)
    wffn = np.ascontiguousarray(np.asarray(L0_ffn_weight, dtype=np.float32))
    base1 = dict(common)
    base1.update({
        "wk_l": np.ascontiguousarray(np.asarray(L0_attn_W_K, dtype=np.float32)),
        "wv_l": np.ascontiguousarray(np.asarray(L0_attn_W_V, dtype=np.float32)),
        "wffn_l": wffn,
        "bust": np.zeros((1, bust1), np.float32),
    })
    in_maps1 = []
    for c in range(NCORE):
        m = dict(base1)
        m["xT"] = np.ascontiguousarray(xT[:, c * TPC:(c + 1) * TPC])
        in_maps1.append(m)
    res1 = run_bass_kernel_spmd(nc1, in_maps1, core_ids=list(range(NCORE)))
    kvparts = np.concatenate(
        [np.asarray(res1.results[c]["kv_out"]) for c in range(NCORE)], axis=0)

    nc2 = build_nc(bust2, 2)
    base2 = dict(common)
    base2.update({
        "wq_l": np.ascontiguousarray(np.asarray(L0_attn_W_Q, dtype=np.float32)),
        "wfc_l": np.ascontiguousarray(np.asarray(L0_attn_W_fc, dtype=np.float32)),
        "gb": gb, "sel16": sel16,
        "wqt_f": np.ascontiguousarray(np.asarray(res1.results[0]["wqt_out"], dtype=np.float32)),
        "kvparts": np.ascontiguousarray(kvparts.astype(np.float32)),
        "bust": np.zeros((1, bust2), np.float32),
    })
    in_maps2 = []
    for c in range(NCORE):
        m = dict(base2)
        m["x1f"] = np.asarray(res1.results[c]["x1_out"])
        m["ln2f"] = np.asarray(res1.results[c]["ln2_out"])
        in_maps2.append(m)
    res2 = run_bass_kernel_spmd(nc2, in_maps2, core_ids=list(range(NCORE)))
    out = np.concatenate([r["yT"] for r in res2.results], axis=1)
    return np.ascontiguousarray(out.T).reshape(1, 16384, 1024).astype(np.float32)


# version 19
# speedup vs baseline: 1.1542x; 1.0044x over previous
VERSION = 21

import os
import numpy as np
import ml_dtypes
import concourse.bass as bass
import concourse.mybir as mybir
from concourse.bass_utils import run_bass_kernel_spmd
from contextlib import ExitStack
from collections import deque

F32 = mybir.dt.float32
BF16 = mybir.dt.bfloat16
AF = mybir.ActivationFunctionType
ALU = mybir.AluOpType
AX = mybir.AxisListType

D = 1024
NCORE = 8
TPC = 2048
NT = 4
TS = 512
NCH = 16
FC = 8
NH = 16
EPS = 1e-5
DBG = int(os.environ.get("KDBG", "9"))
DBG2 = int(os.environ.get("KDBG2", "9"))
FULLSYNC = os.environ.get("BASS_SIM_FP64") == "1"


def _hadamard_bf16():
    h = np.array([[1.0]], dtype=np.float64)
    while h.shape[0] < D:
        h = np.block([[h, h], [h, -h]])
    h = (h / 32.0).astype(np.float32)
    hp = h.reshape(FC, 128, D).transpose(1, 0, 2).reshape(128, FC * D)
    return hp.astype(ml_dtypes.bfloat16)


def _sel16_f32():
    s = np.zeros((16, FC * 128), np.float32)
    for p in range(FC):
        s[2 * p, p * 128: p * 128 + 64] = 1.0
        s[2 * p + 1, p * 128 + 64: p * 128 + 128] = 1.0
    return s.astype(ml_dtypes.bfloat16)


class Buf:
    __slots__ = ("name", "writer", "readers")

    def __init__(self, name):
        self.name = name
        self.writer = None
        self.readers = []


class Prog:
    ENGS = ("tensor", "vector", "scalar", "gpsimd", "sync")
    DMAQ = ("din", "din1", "dout", "dout1", "dcc")

    def __init__(self):
        self.ops = {e: [] for e in self.ENGS}
        self.cnt = {e: 0 for e in list(self.ENGS) + list(self.DMAQ)}
        self.hw = {e: {} for e in self.ENGS}
        self.selfhw = {e: 0 for e in self.ENGS}

    def _collect(self, eng, reads, writes):
        waits = {}

        def need(tok):
            if tok is None:
                return
            te, tv, sm = tok
            if te == eng and not FULLSYNC:
                if not (sm and (self.cnt[eng] - tv) < 3):
                    return
            if waits.get(te, -1) < tv:
                waits[te] = tv

        for b in reads:
            need(b.writer)
        for b in writes:
            need(b.writer)
            for t2 in b.readers:
                need(t2)
        out = []
        for e2, v in sorted(waits.items()):
            if e2 == eng:
                if v > self.selfhw[eng]:
                    out.append((e2, v))
                    self.selfhw[eng] = v
            elif self.hw[eng].get(e2, 0) < v:
                out.append((e2, v))
                self.hw[eng][e2] = v
        return out

    def op(self, eng, fn, reads=(), writes=(), small=False, after=()):
        waits = self._collect(eng, reads, writes)
        for tok in after:
            te, tv, _sm = tok
            if te == eng:
                continue
            if self.hw[eng].get(te, 0) < tv:
                waits.append((te, tv))
                self.hw[eng][te] = tv
        self.cnt[eng] += 1
        tok = (eng, self.cnt[eng], small)
        self.ops[eng].append((waits, fn, eng, 1))
        for b in writes:
            b.writer = tok
            b.readers = []
        for b in reads:
            b.readers.append(tok)
        return tok

    def dma(self, eng, q, fn, reads=(), writes=()):
        waits = self._collect(eng, reads, writes)
        if self.cnt[q] > 0 and self.hw[eng].get(q, 0) < self.cnt[q]:
            waits.append((q, self.cnt[q]))
            self.hw[eng][q] = self.cnt[q]
        self.cnt[q] += 16
        tok = (q, self.cnt[q], False)
        self.ops[eng].append((waits, fn, q, 16))
        for b in writes:
            b.writer = tok
            b.readers = []
        for b in reads:
            b.readers.append(tok)
        return tok

    def wait(self, eng, tok):
        te, tv, _ = tok
        if te == eng:
            if tv > self.selfhw[eng]:
                self.selfhw[eng] = tv
                self.ops[eng].append(([(te, tv)], None, None, 0))
            return
        if self.hw[eng].get(te, 0) < tv:
            self.hw[eng][te] = tv
            self.ops[eng].append(([(te, tv)], None, None, 0))

    def emit(self, engobj, eng, sems):
        for waits, fn, incsem, incval in self.ops[eng]:
            for (se, v) in waits:
                engobj.wait_ge(sems[se], v)
            if fn is None:
                continue
            inst = fn(engobj)
            if incval:
                inst.then_inc(sems[incsem], incval)


def build_nc(bust_len, launch):
    nc = bass.Bass("TRN2", target_bir_lowering=False)
    g = {}

    def din(name, shape, dt=F32):
        g[name] = nc.dram_tensor(name, shape, dt, kind="ExternalInput")

    din("h16", [128, FC * D], BF16)
    din("i128", [128, 128], BF16)
    din("ones1", [128, 1], BF16)
    din("onesrow", [1, 128])
    din("constf", [128, 8])
    din("bust", [1, bust_len])
    if launch == 1:
        din("xT", [D, TPC])
        din("wk_l", [D, D]); din("wv_l", [D, D]); din("wffn_l", [D, D])
        kv_out = nc.dram_tensor("kv_out", [128, 520], F32, kind="ExternalOutput")
        x1_out = nc.dram_tensor("x1_out", [D, TPC], F32, kind="ExternalOutput")
        ln2_out = nc.dram_tensor("ln2_out", [D, TPC], F32, kind="ExternalOutput")
        wqt_out = nc.dram_tensor("wqt_out", [128, FC * D], F32, kind="ExternalOutput")
    else:
        din("wq_l", [D, D]); din("wfc_l", [D, D])
        din("wqt_f", [128, FC * D])
        din("gb", [128, 16])
        din("sel16", [16, FC * 128], BF16)
        din("x1f", [D, TPC])
        din("ln2f", [D, TPC])
        din("kvparts", [8 * 128, 520])
        yT = nc.dram_tensor("yT", [D, TPC], F32, kind="ExternalOutput")

    P = Prog()
    with ExitStack() as ctx:
        e = ctx.enter_context
        ctx.enter_context(nc.allow_low_precision(
            reason="bf16 residual stream / operand casts are deliberate; "
                   "matmuls accumulate fp32 in PSUM"))

        def sb(name, shape, dt):
            return e(nc.sbuf_tensor(name, shape, dt))

        x16 = sb("x16", [128, FC * TPC], BF16)
        ln2 = sb("ln2s", [128, FC * TPC], BF16)
        h16 = sb("h16s", [128, FC * D], BF16)
        wqt = sb("wqt", [128, FC * D], BF16)
        wS3 = sb("wS3", [128, FC * D], BF16)
        wS4 = sb("wS4", [128, FC * D], BF16)
        SW = 2048 if launch == 1 else 1024
        stage = sb("stage", [128, SW], F32)
        LW = 2 if launch == 1 else 1
        PRW = 2 * D
        u32b = sb("u32b", [128, PRW], BF16)
        bb32 = sb("bb32", [128, PRW], BF16)
        v32 = sb("v32", [128, LW * PRW], BF16)
        sqb = sb("sqb", [128, 1040], F32)
        rs16 = sb("rs16", [128, PRW], BF16)
        xqT = sb("xqT", [128, FC * TS], BF16)
        if launch == 2:
            qt = sb("qt", [128, FC * TS], BF16)
            ao16 = sb("ao16", [128, FC * TS], BF16)
        if launch == 1:
            k16 = sb("k16", [128, D], BF16)
            v16 = sb("v16", [128, D], BF16)
            wbld = sb("wbld", [128, D], F32)
        statc = sb("statc", [128, 40], F32)
        ksum = sb("ksum", [128, 16], F32) if launch == 1 else None
        r2b32 = sb("r2b32", [128, TS], F32 if launch == 1 else BF16)
        aux32 = sb("aux32", [128, TS], F32)
        tmpv = sb("tmpv", [128, TS], BF16)
        tmpg = sb("tmpg", [128, TS], BF16)
        if launch == 2:
            rdrow = aux32[0:16, :]
            mbf = aux32
        kvsb = sb("kvsb", [128, 520], F32) if launch == 2 else stage[:, 0:520]
        rdb32 = kvsb[:, 0:512] if launch == 2 else None
        if launch == 2:
            kvbd = sb("kvbd", [128, FC * 128], BF16)
            dbd = sb("dbd", [128, FC * 16], BF16)
            rdr2 = sb("rdr2", [16, TS], BF16)
        rowmt = sb("rowmt", [1, TS], F32)
        rowqt = sb("rowqt", [1, TS], F32)
        rowm = rowmt[0:1, :]
        rowq = rowqt[0:1, :]
        i128s = sb("i128s", [128, 128], BF16)
        ones1s = sb("ones1s", [128, 1], BF16)
        onesrs = sb("onesrs", [1, 128], F32)
        if launch == 2:
            sel16s = sb("sel16s", [16, FC * 128], BF16)
            gbs = sb("gbs", [128, 16], F32)
        cfs = sb("cfs", [128, 8], F32)
        busts = sb("busts", [1, bust_len], F32)

        pb = [e(nc.psum_tensor(f"pb{i}", [128, 512], F32)) for i in range(8)]

        semnames = list(Prog.ENGS) + list(Prog.DMAQ)
        sems = {n: e(nc.semaphore("sem_" + n)) for n in semnames}

        B = {}

        def buf(nm):
            if nm not in B:
                B[nm] = Buf(nm)
            return B[nm]

        PB = [buf(f"pb{i}") for i in range(8)]
        X16 = [buf(f"x16_{t}") for t in range(NT)]
        LN2 = [buf(f"ln2_{t}") for t in range(NT)]
        SH = [buf("stage_0"), buf("stage_1"), buf("stage_2"), buf("stage_3")]
        _ldpc = [0]
        xtcnt = {}

        def piece_jobs(srcn, dstt, tilebufs, tiles, cpeng):
            ar = g[srcn].ap().rearrange("(f p) t -> f p t", p=128)
            pieces = [(t, f) for t in tiles for f in range(FC)]
            qs = []

            def issue(k):
                t, f = pieces[k]
                i = _ldpc[0] % 2
                _ldpc[0] += 1
                qs.append(i)
                P.dma("sync", "din" if i == 0 else "din1",
                      lambda s, f=f, t=t, i=i, ar=ar: s.dma_start(
                          out=stage[:, i * 512:(i + 1) * 512],
                          in_=ar[f][:, t * TS:(t + 1) * TS]),
                      reads=(SH[i],), writes=(SH[i],))

            def copy(k):
                t, f = pieces[k]
                i = qs[k]

                def fn(eng, f=f, t=t, i=i):
                    src = stage[:, i * 512:(i + 1) * 512]
                    dst = dstt[:, f * TPC + t * TS: f * TPC + (t + 1) * TS]
                    return eng.copy(dst, src) if cpeng == "scalar" \
                        else eng.tensor_copy(dst, src)
                P.op(cpeng, fn, reads=(SH[i],), writes=(tilebufs[t],))
                xtcnt[(srcn, t)] = xtcnt.get((srcn, t), 0) + 1

            n = len(pieces)
            jobs = [lambda: issue(0)]
            for k in range(n):
                def jk(k=k):
                    if k + 1 < n:
                        issue(k + 1)
                    copy(k)
                jobs.append(jk)
            for j in jobs:
                j.light = True
            return jobs

        def wpiece_jobs(srcn, slot, slotbuf, cpeng="gpsimd"):
            if srcn == "wqt_f":
                ar = None
            else:
                ar = g[srcn].ap().rearrange("(c p) o -> c p o", p=128)
            qs = []

            def issue(k):
                i = _ldpc[0] % 2
                _ldpc[0] += 1
                qs.append(i)
                if ar is None:
                    P.dma("sync", "din" if i == 0 else "din1",
                          lambda s, k=k, i=i: s.dma_start(
                              out=stage[:, i * 512:(i + 1) * 512],
                              in_=g["wqt_f"].ap()[:, k * 512:(k + 1) * 512]),
                          reads=(SH[i],), writes=(SH[i],))
                else:
                    c, h = k // 2, k % 2
                    P.dma("sync", "din" if i == 0 else "din1",
                          lambda s, c=c, h=h, i=i, ar=ar: s.dma_start(
                              out=stage[:, i * 512:(i + 1) * 512],
                              in_=ar[c][:, h * 512:(h + 1) * 512]),
                          reads=(SH[i],), writes=(SH[i],))

            def copy(k):
                i = qs[k]
                c, h = k // 2, k % 2
                base = k * 512 if ar is None else c * D + h * 512

                def fn(eng, base=base, i=i):
                    src = stage[:, i * 512:(i + 1) * 512]
                    return eng.copy(slot[:, base:base + 512], src) if cpeng == "scalar"                         else eng.tensor_copy(slot[:, base:base + 512], src)
                P.op(cpeng, fn, reads=(SH[i],), writes=(slotbuf,))

            jobs = [lambda: issue(0)]
            for k in range(16):
                def jk(k=k):
                    if k + 1 < 16:
                        issue(k + 1)
                    copy(k)
                jobs.append(jk)
            for j in jobs:
                j.light = True
            return jobs

        constlist = [(i128s, "i128"), (ones1s, "ones1"), (onesrs, "onesrow"),
                     (cfs, "constf"), (busts, "bust")]
        if launch == 2:
            constlist += [(sel16s, "sel16"), (gbs, "gb")]
        for (dst, src) in constlist:
            P.dma("sync", "din", lambda s, d=dst, so=src: s.dma_start(out=d[:, :], in_=g[so].ap()),
                  writes=(buf("consts"),))

        bleft = [FC]

        def mk_build():
            wr = g["wffn_l"].ap().rearrange("(c p) i -> c p i", p=128)
            wqtr = wqt_out.ap().rearrange("p (i o c) -> o i p c", i=8, o=8)
            WH = (SH[2], SH[3])

            def bdma(oc):
                P.dma("sync", "din1", lambda s, oc=oc: s.dma_start(
                    out=stage[:, 1024:2048], in_=wr[oc]), reads=WH, writes=WH)

            def bjob(oc):
                P.op("vector", lambda v: v.tensor_reduce(
                    statc[:, 0:1], stage[:, 1024:2048], AX.X, ALU.add,
                    apply_absolute_value=True),
                    reads=WH, writes=(buf("sc0"),), small=True)
                P.op("vector", lambda v: v.tensor_scalar(
                    statc[:, 1:2], statc[:, 0:1], 1.0 / D, 1e-5, ALU.mult, ALU.max),
                    reads=(buf("sc0"),), writes=(buf("sc1"),), small=True)
                P.op("vector", lambda v: v.reciprocal(statc[:, 2:3], statc[:, 1:2]),
                     reads=(buf("sc1"),), writes=(buf("sc2"),), small=True)
                P.op("vector", lambda v: v.tensor_single_scalar(
                    statc[:, 3:4], statc[:, 1:2], 0.5, ALU.mult),
                    reads=(buf("sc1"),), writes=(buf("sc3"),), small=True)
                P.op("scalar", lambda s: s.activation(
                    wbld[:, :], stage[:, 1024:2048], AF.Copy, scale=statc[:, 2:3]),
                    reads=WH + (buf("sc2"),), writes=(buf("wbld"),))
                P.op("vector", lambda v: v.tensor_scalar(
                    stage[:, 1024:2048], wbld[:, :], 0.5, 1.0, ALU.is_gt, ALU.mult),
                    reads=(buf("wbld"),), writes=WH)
                P.op("vector", lambda v: v.tensor_scalar(
                    wbld[:, :], wbld[:, :], -0.5, 1.0, ALU.is_lt, ALU.mult),
                    reads=(buf("wbld"),), writes=(buf("wbld"),))
                P.op("vector", lambda v: v.tensor_sub(
                    stage[:, 1024:2048], stage[:, 1024:2048], wbld[:, :]),
                    reads=WH + (buf("wbld"),), writes=WH)
                P.op("scalar", lambda s: s.activation(
                    tmpg[:, 0:512], stage[:, 1024:1536], AF.Copy, scale=statc[:, 3:4]),
                    reads=WH + (buf("sc3"),), writes=(buf("tmpg"),))
                P.op("scalar", lambda s: s.activation(
                    tmpv[:, 0:512], stage[:, 1536:2048], AF.Copy, scale=statc[:, 3:4]),
                    reads=WH + (buf("sc3"),), writes=(buf("tmpv"),))
                if oc + 1 < FC:
                    bdma(oc + 1)
                for ic in range(FC):
                    bk = 4 + (ic % 2)
                    srcop = tmpg if ic < 4 else tmpv
                    srcb = buf("tmpg") if ic < 4 else buf("tmpv")
                    co = (ic % 4) * 128
                    P.op("tensor", lambda t, co=co, bk=bk, srcop=srcop: t.matmul(
                        pb[bk][:, 0:128], srcop[:, co:co + 128], i128s[:, :],
                        start=True, stop=True),
                        reads=(srcb, buf("consts"), PB[bk]), writes=(PB[bk],))
                    P.op("scalar", lambda s, ic=ic, oc=oc, bk=bk: s.copy(
                        wqt[:, ic * D + oc * 128: ic * D + (oc + 1) * 128], pb[bk][:, 0:128]),
                        reads=(PB[bk],), writes=(buf("wqt"),))
                    rdst = r2b32 if ic < 4 else aux32
                    rb = buf("r2b32") if ic < 4 else buf("aux32")
                    j = ic % 4
                    P.op("scalar", lambda s, j=j, bk=bk, rdst=rdst: s.copy(
                        rdst[:, j * 128:(j + 1) * 128], pb[bk][:, 0:128]),
                        reads=(PB[bk],), writes=(rb,))
                for j in range(4):
                    P.dma("gpsimd", "dout", lambda gp, oc=oc, j=j: gp.dma_start(
                        out=wqtr[oc][j], in_=r2b32[:, j * 128:(j + 1) * 128]),
                        reads=(buf("r2b32"),))
                    P.dma("gpsimd", "dout1", lambda gp, oc=oc, j=j: gp.dma_start(
                        out=wqtr[oc][4 + j], in_=aux32[:, j * 128:(j + 1) * 128]),
                        reads=(buf("aux32"),))
                bleft[0] -= 1
            bdma(0)
            return [lambda oc=oc: bjob(oc) for oc in range(FC)]

        def kv_ingest():
            kvr = g["kvparts"].ap().rearrange("(r p) c -> r p c", p=128)
            P.op("vector", lambda v: v.memset(kvbd[:, :], 0.0), writes=(buf("kvbd"),))
            P.op("vector", lambda v: v.memset(dbd[:, :], 0.0), writes=(buf("dbd"),))

            def slotb(r):
                return (buf("sqbA"), buf("sqbB")) if r % 2 == 0 else (buf("sqbB"),)

            def dmain(r):
                off = (r % 2) * 520
                P.dma("sync", "din" if r % 2 == 0 else "din1",
                      lambda s, r=r, off=off: s.dma_start(
                          out=sqb[:, off:off + 520], in_=kvr[r]),
                      reads=slotb(r), writes=slotb(r))
            dmain(0)
            for r in range(8):
                if r + 1 < 8:
                    dmain(r + 1)
                off = (r % 2) * 520
                if r == 0:
                    P.op("gpsimd", lambda g2, off=off: g2.tensor_copy(
                        kvsb[:, :], sqb[:, off:off + 520]),
                        reads=slotb(r), writes=(buf("kvsb"),))
                else:
                    P.op("gpsimd", lambda g2, off=off, r=r: g2.tensor_tensor(
                        kvsb[:, :], sqb[:, off:off + 520], kvsb[:, :], ALU.add),
                        reads=slotb(r) + (buf("kvsb"),), writes=(buf("kvsb"),))
            for p_ in range(FC):
                P.op("gpsimd", lambda g2, p_=p_: g2.tensor_copy(
                    kvbd[0:64, p_ * 128: p_ * 128 + 64], kvsb[0:64, p_ * 64:(p_ + 1) * 64]),
                    reads=(buf("kvsb"),), writes=(buf("kvbd"),))
                P.op("gpsimd", lambda g2, p_=p_: g2.tensor_copy(
                    kvbd[64:128, p_ * 128 + 64: p_ * 128 + 128], kvsb[64:128, p_ * 64:(p_ + 1) * 64]),
                    reads=(buf("kvsb"),), writes=(buf("kvbd"),))
                P.op("gpsimd", lambda g2, p_=p_: g2.tensor_copy(
                    dbd[0:64, p_ * 16 + 2 * p_: p_ * 16 + 2 * p_ + 1], kvsb[0:64, 512 + p_:513 + p_]),
                    reads=(buf("kvsb"),), writes=(buf("dbd"),), small=True)
                P.op("gpsimd", lambda g2, p_=p_: g2.tensor_copy(
                    dbd[64:128, p_ * 16 + 2 * p_ + 1: p_ * 16 + 2 * p_ + 2], kvsb[64:128, 512 + p_:513 + p_]),
                    reads=(buf("kvsb"),), writes=(buf("dbd"),), small=True)

        def ffn_phase(tile, final, fill=None):
            def F():
                if fill:
                    fill.popleft()()
                    if fill and len(fill) > 16 and getattr(fill[0], "light", False):
                        fill.popleft()()
            t0 = tile * TS

            def emit_xh_g(gch):
                for half, bk in ((0, 0), (1, 1)):
                    def mmg(t, bk=bk, gch=gch, half=half):
                        last = None
                        for kc in range(FC):
                            last = t.matmul(pb[bk][:, 0:512],
                                            x16[:, kc * TPC + gch * 128: kc * TPC + (gch + 1) * 128],
                                            h16[:, kc * D + half * 512: kc * D + half * 512 + 512],
                                            start=(kc == 0), stop=(kc == FC - 1))
                        return last
                    P.op("tensor", mmg, reads=(X16[gch // 4], buf("h16"), PB[bk]), writes=(PB[bk],))
                    F()

            def emit_xh(cc):
                emit_xh_g(tile * 4 + cc)

            if final or tile == 0:
                emit_xh(0)
            for pp in range(2):
                if DBG2 < 1:
                    continue
                pph = (pp % 2) * PRW if LW == 2 else 0
                pbuf = str(pp % 2) if LW == 2 else "0"
                vb, bbb = buf("v32" + pbuf), buf("bb32")
                ubb = buf("u32b")
                for ci in range(2):
                    cc = 2 * pp + ci
                    co = pph + ci * D
                    t_m1 = P.op("vector", lambda v, ci=ci: v.tensor_scalar_mul(
                        statc[:, 8 + ci:9 + ci], pb[0][:, 0:1], 1.0 / 32),
                        reads=(PB[0],), writes=(buf(f"sc8{ci}"),), small=True)
                    t_a0 = P.op("vector", lambda v, ci=ci: v.tensor_reduce(
                        statc[:, 32 + ci:33 + ci], pb[0][:, 1:512], AX.X, ALU.max,
                        apply_absolute_value=True),
                        reads=(PB[0],), writes=(buf(f"sc32{ci}"),), small=True)
                    t_a1 = P.op("vector", lambda v, ci=ci: v.tensor_reduce(
                        statc[:, 34 + ci:35 + ci], pb[1][:, 0:512], AX.X, ALU.max,
                        apply_absolute_value=True),
                        reads=(PB[1],), writes=(buf(f"sc34{ci}"),), small=True)
                    t_sqa = P.op("scalar", lambda s: s.activation(
                        sqb[:, 0:512], pb[0][:, 0:512], AF.Square),
                        reads=(PB[0],), writes=(buf("sqbA"),), after=(t_m1, t_a0))
                    t_sqb = P.op("scalar", lambda s: s.activation(
                        sqb[:, 512:1024], pb[1][:, 0:512], AF.Square),
                        reads=(PB[1],), writes=(buf("sqbB"),), after=(t_a1,))
                    P.op("vector", lambda v, ci=ci: v.tensor_tensor(
                        statc[:, 36 + ci:37 + ci], statc[:, 32 + ci:33 + ci],
                        statc[:, 34 + ci:35 + ci], ALU.max),
                        reads=(buf(f"sc32{ci}"), buf(f"sc34{ci}")), writes=(buf(f"sc36{ci}"),), small=True)
                    P.op("vector", lambda v, ci=ci: v.tensor_single_scalar(
                        statc[:, 12 + ci:13 + ci], statc[:, 36 + ci:37 + ci], 1e-5, ALU.max),
                        reads=(buf(f"sc36{ci}"),), writes=(buf(f"sc12{ci}"),), small=True)
                    P.op("vector", lambda v, ci=ci: v.reciprocal(
                        statc[:, 14 + ci:15 + ci], statc[:, 12 + ci:13 + ci]),
                        reads=(buf(f"sc12{ci}"),), writes=(buf(f"sc14{ci}"),), small=True)
                    P.op("vector", lambda v, ci=ci: v.tensor_scalar_mul(
                        statc[:, 16 + ci:17 + ci], statc[:, 14 + ci:15 + ci], 7.0),
                        reads=(buf(f"sc14{ci}"),), writes=(buf(f"sc16{ci}"),), small=True)
                    P.op("vector", lambda v, co=co, ci=ci: v.tensor_scalar_mul(
                        v32[:, co:co + 512], pb[0][:, 0:512], statc[:, 16 + ci:17 + ci]),
                        reads=(PB[0], buf(f"sc16{ci}")), writes=(vb,), after=(t_sqa,))
                    P.op("vector", lambda v, co=co, ci=ci: v.tensor_scalar_mul(
                        v32[:, co + 512:co + 1024], pb[1][:, 0:512], statc[:, 16 + ci:17 + ci]),
                        reads=(PB[1], buf(f"sc16{ci}")), writes=(vb,), after=(t_sqb,))
                    if cc + 1 < 4:
                        emit_xh(cc + 1)
                    P.op("vector", lambda v, ci=ci: v.reduce_sum(
                        statc[:, 10 + ci:11 + ci], sqb[:, 0:D], AX.X),
                        reads=(buf("sqbA"), buf("sqbB")), writes=(buf(f"sc10{ci}"),), small=True)
                if DBG2 < 2:
                    continue
                P.op("vector", lambda v, pph=pph: v.memset(
                    v32[:, pph:pph + PRW].rearrange("p (c d) -> p c d", d=D)[:, :, 0:1], 0.0),
                    writes=(vb,), small=True)
                P.op("vector", lambda v: v.tensor_tensor(
                    statc[:, 18:20], statc[:, 8:10], statc[:, 8:10], ALU.mult),
                    reads=(buf("sc80"), buf("sc81")), writes=(buf("sc18"),), small=True)
                P.op("vector", lambda v: v.tensor_single_scalar(
                    statc[:, 20:22], statc[:, 10:12], 1.0 / D, ALU.mult),
                    reads=(buf("sc100"), buf("sc101")), writes=(buf("sc20"),), small=True)
                P.op("vector", lambda v: v.tensor_sub(statc[:, 22:24], statc[:, 20:22], statc[:, 18:20]),
                     reads=(buf("sc20"), buf("sc18")), writes=(buf("sc22"),), small=True)
                P.op("scalar", lambda s: s.activation(statc[:, 24:26], statc[:, 22:24], AF.Ln, bias=cfs[:, 0:1]),
                     reads=(buf("sc22"), buf("consts")), writes=(buf("sc24"),), small=True)
                P.op("scalar", lambda s: s.activation(statc[:, 26:28], statc[:, 24:26], AF.Exp, scale=cfs[:, 1:2]),
                     reads=(buf("sc24"), buf("consts")), writes=(buf("sc26"),), small=True)
                P.op("vector", lambda v: v.tensor_tensor(
                    statc[:, 28:30], statc[:, 12:14], statc[:, 26:28], ALU.mult),
                    reads=(buf("sc120"), buf("sc121"), buf("sc26")), writes=(buf("sc28"),), small=True)
                P.op("vector", lambda v: v.tensor_single_scalar(
                    statc[:, 28:30], statc[:, 28:30], 1.0 / 7, ALU.mult),
                    reads=(buf("sc28"),), writes=(buf("sc28"),), small=True)
                if DBG2 < 3:
                    continue
                P.op("scalar", lambda s, pph=pph: s.activation(
                    u32b[:, :], v32[:, pph:pph + PRW], AF.Abs),
                    reads=(vb,), writes=(ubb,))
                P.op("vector", lambda v, pph=pph: v.tensor_scalar(
                    rs16[:, :], v32[:, pph:pph + PRW], 0.0, 2.0, ALU.is_ge, ALU.mult),
                    reads=(vb,), writes=(buf("rs16"),))
                P.op("vector", lambda v: v.tensor_single_scalar(
                    rs16[:, :], rs16[:, :], -1.0, ALU.add),
                    reads=(buf("rs16"),), writes=(buf("rs16"),))
                P.op("vector", lambda v: v.tensor_scalar(
                    bb32[:, :], u32b[:, :], 3.5, 4.0, ALU.is_ge, ALU.mult),
                    reads=(ubb,), writes=(bbb,))
                P.op("vector", lambda v: v.tensor_sub(u32b[:, :], u32b[:, :], bb32[:, :]),
                     reads=(ubb, bbb), writes=(ubb,))
                P.op("vector", lambda v: v.tensor_scalar(
                    bb32[:, :], u32b[:, :], 1.5, 2.0, ALU.is_ge, ALU.mult),
                    reads=(ubb,), writes=(bbb,))
                P.op("vector", lambda v: v.tensor_sub(u32b[:, :], u32b[:, :], bb32[:, :]),
                     reads=(ubb, bbb), writes=(ubb,))
                P.op("vector", lambda v: v.tensor_scalar(
                    bb32[:, :], u32b[:, :], 0.5, 1.0, ALU.is_ge, ALU.mult),
                    reads=(ubb,), writes=(bbb,))
                P.op("vector", lambda v: v.tensor_sub(u32b[:, :], u32b[:, :], bb32[:, :]),
                     reads=(ubb, bbb), writes=(ubb,))
                P.op("vector", lambda v: v.tensor_mul(bb32[:, :], rs16[:, :], u32b[:, :]),
                     reads=(buf("rs16"), ubb), writes=(bbb,))
                P.op("vector", lambda v, pph=pph: v.tensor_sub(
                    v32[:, pph:pph + PRW], v32[:, pph:pph + PRW], bb32[:, :]),
                    reads=(vb, bbb), writes=(vb,))
                if DBG2 < 4:
                    continue
                for ci in range(2):
                    cc = 2 * pp + ci
                    co = pph + ci * D
                    P.op("vector", lambda v, co=co, ci=ci: v.tensor_scalar_mul(
                        rs16[:, ci * D:(ci + 1) * D], v32[:, co:co + D], statc[:, 28 + ci:29 + ci]),
                        reads=(vb, buf("sc28")), writes=(buf("rs16"),))
                for f in range(FC):
                    bk = 4 + (f % 2)
                    for ci in range(2):
                        cc = 2 * pp + ci
                        P.op("tensor", lambda t, f=f, ci=ci, bk=bk: t.matmul(
                            pb[bk][:, ci * 128:(ci + 1) * 128], rs16[:, ci * D + f * 128: ci * D + (f + 1) * 128],
                            i128s[:, :], start=True, stop=True),
                            reads=(buf("rs16"), buf("consts"), PB[bk]), writes=(PB[bk],))
                    P.op("scalar", lambda s, f=f, pp=pp, bk=bk: s.copy(
                        xqT[:, f * TS + pp * 256: f * TS + (pp + 1) * 256], pb[bk][:, 0:256]),
                        reads=(PB[bk],), writes=(buf("xqT"),))
                    F()
            if not final and tile + 1 < NT:
                while xtcnt.get(("xT", tile + 1), 0) < FC and fill:
                    fill.popleft()()
                emit_xh_g((tile + 1) * 4)
            if launch == 1 and tile == 0:
                while bleft[0] > 0 and fill:
                    fill.popleft()()
            for oc in range(FC if DBG2 >= 5 else 0):
                bk = 4 + (oc % 2)

                def mmo(t, oc=oc, bk=bk):
                    last = None
                    for ic in range(FC):
                        last = t.matmul(pb[bk][:, 0:512], wqt[:, ic * D + oc * 128: ic * D + (oc + 1) * 128],
                                        xqT[:, ic * TS:(ic + 1) * TS],
                                        start=(ic == 0), stop=(ic == FC - 1))
                    return last
                P.op("tensor", mmo, reads=(buf("wqt"), buf("xqT"), PB[bk]), writes=(PB[bk],))
                F()
                if not final:
                    P.op("vector", lambda v, oc=oc, bk=bk, t0=t0: v.tensor_tensor(
                        x16[:, oc * TPC + t0: oc * TPC + t0 + TS], pb[bk][:, 0:512],
                        x16[:, oc * TPC + t0: oc * TPC + t0 + TS], ALU.add),
                        reads=(PB[bk], X16[tile]), writes=(X16[tile],))
                else:
                    i = oc % 2
                    hb = buf("sqbA") if i == 0 else buf("sqbB")
                    P.op("vector", lambda v, oc=oc, bk=bk, t0=t0, i=i: v.tensor_tensor(
                        sqb[:, i * TS:(i + 1) * TS], pb[bk][:, 0:512],
                        x16[:, oc * TPC + t0: oc * TPC + t0 + TS], ALU.add),
                        reads=(PB[bk], X16[tile], hb), writes=(hb,))
                    P.dma("gpsimd", "dout" if oc % 2 == 0 else "dout1",
                          lambda gp, oc=oc, tile=tile, i=i: gp.dma_start(
                              out=yT.ap()[oc * 128:(oc + 1) * 128, tile * TS:(tile + 1) * TS],
                              in_=sqb[:, i * TS:(i + 1) * TS]),
                          reads=(hb,))

        def ln2_block(tile):
            t0 = tile * TS

            def mmsum(t, t0=t0):
                last = None
                for f in range(FC):
                    last = t.matmul(pb[4][0:1, 0:512], ones1s[:, 0:1],
                                    x16[:, f * TPC + t0: f * TPC + t0 + TS],
                                    start=(f == 0), stop=(f == FC - 1))
                return last
            P.op("tensor", mmsum, reads=(X16[tile], buf("consts"), PB[4]), writes=(PB[4],))
            for f in range(FC):
                P.op("scalar", lambda s, f=f, t0=t0: s.activation(
                    xqT[:, f * TS:(f + 1) * TS], x16[:, f * TPC + t0: f * TPC + t0 + TS], AF.Square),
                    reads=(X16[tile],), writes=(buf("xqT"),))

            def mmsq(t):
                last = None
                for f in range(FC):
                    last = t.matmul(pb[5][0:1, 0:512], ones1s[:, 0:1],
                                    xqT[:, f * TS:(f + 1) * TS],
                                    start=(f == 0), stop=(f == FC - 1))
                return last
            P.op("tensor", mmsq, reads=(buf("xqT"), buf("consts"), PB[5]), writes=(PB[5],))
            P.op("vector", lambda v: v.tensor_scalar_mul(rowm, pb[4][0:1, 0:512], 1.0 / D),
                 reads=(PB[4],), writes=(buf("rowm"),))
            P.op("vector", lambda v: v.tensor_scalar_mul(rowq, pb[5][0:1, 0:512], 1.0 / D),
                 reads=(PB[5],), writes=(buf("rowq"),))
            P.op("tensor", lambda t: t.matmul(pb[4][:, 0:512], onesrs[0:1, :], rowm, start=True, stop=True),
                 reads=(buf("rowm"), buf("consts"), PB[4]), writes=(PB[4],))
            P.op("vector", lambda v: v.tensor_tensor(rowm, rowm, rowm, ALU.mult),
                 reads=(buf("rowm"),), writes=(buf("rowm"),), small=True)
            P.op("vector", lambda v: v.tensor_sub(rowq, rowq, rowm),
                 reads=(buf("rowq"), buf("rowm")), writes=(buf("rowq"),))
            P.op("scalar", lambda s: s.activation(rowm, rowq, AF.Ln, bias=cfs[0:1, 0:1]),
                 reads=(buf("rowq"), buf("consts")), writes=(buf("rowm"),))
            P.op("scalar", lambda s: s.activation(rowq, rowm, AF.Exp, scale=cfs[0:1, 1:2]),
                 reads=(buf("rowm"), buf("consts")), writes=(buf("rowq"),))
            P.op("tensor", lambda t: t.matmul(pb[5][:, 0:512], onesrs[0:1, :], rowq, start=True, stop=True),
                 reads=(buf("rowq"), buf("consts"), PB[5]), writes=(PB[5],))
            P.op("vector", lambda v: v.tensor_copy(r2b32[:, :], pb[5][:, 0:512]),
                 reads=(PB[5],), writes=(buf("r2b32"),))
            P.op("vector", lambda v: v.tensor_copy(aux32[:, :], pb[4][:, 0:512]),
                 reads=(PB[4],), writes=(buf("aux32"),))
            for f in range(FC):
                eng = "vector" if f < 5 else "gpsimd"
                tmp = tmpv if f < 5 else tmpg
                tb = buf("tmpv") if f < 5 else buf("tmpg")
                P.op(eng, lambda e2, f=f, t0=t0, tmp=tmp: e2.tensor_sub(
                    tmp[:, :], x16[:, f * TPC + t0: f * TPC + t0 + TS], aux32[:, :]),
                    reads=(X16[tile], buf("aux32")), writes=(tb,))
                P.op(eng, lambda e2, f=f, t0=t0, tmp=tmp: e2.tensor_mul(
                    ln2[:, f * TPC + t0: f * TPC + t0 + TS], tmp[:, :], r2b32[:, :]),
                    reads=(tb, buf("r2b32")), writes=(LN2[tile],))

        def bnd_out(srct, outt, sbuf_, f, t, qi, eng="scalar"):
            hb = buf("sqbA") if qi == 0 else buf("sqbB")

            def fn(e2, f=f, t=t, qi=qi, srct=srct):
                src = srct[:, f * TPC + t * TS: f * TPC + (t + 1) * TS]
                dst = sqb[:, qi * 512:(qi + 1) * 512]
                return e2.copy(dst, src) if eng == "scalar" else e2.tensor_copy(dst, src)
            P.op(eng, fn, reads=(sbuf_, hb), writes=(hb,))
            P.dma("gpsimd", "dout" if qi % 2 == 0 else "dout1",
                  lambda gp, f=f, t=t, qi=qi, outt=outt: gp.dma_start(
                      out=outt.ap()[f * 128:(f + 1) * 128, t * TS:(t + 1) * TS],
                      in_=sqb[:, qi * 512:(qi + 1) * 512]),
                  reads=(hb,))

        def kv_jobs(tile):
            jobs = []
            for cc in range(4):
                gch = tile * 4 + cc

                def ja(gch=gch, tile=tile):
                    for half in range(2):
                        def mmk(t, gch=gch, half=half):
                            last = None
                            for kc in range(FC):
                                last = t.matmul(pb[2 + half][:, 0:512],
                                                ln2[:, kc * TPC + gch * 128: kc * TPC + (gch + 1) * 128],
                                                wS3[:, kc * D + half * 512: kc * D + half * 512 + 512],
                                                start=(kc == 0), stop=(kc == FC - 1))
                            return last
                        P.op("tensor", mmk, reads=(LN2[tile], buf("wS3"), PB[2 + half]), writes=(PB[2 + half],))
                    P.op("scalar", lambda s: s.activation(k16[:, 0:512], pb[2][:, 0:512], AF.Sigmoid),
                         reads=(PB[2],), writes=(buf("k16"),))
                    P.op("scalar", lambda s: s.activation(k16[:, 512:1024], pb[3][:, 0:512], AF.Sigmoid),
                         reads=(PB[3],), writes=(buf("k16"),))

                def jb(gch=gch, tile=tile):
                    for half in range(2):
                        def mmv(t, gch=gch, half=half):
                            last = None
                            for kc in range(FC):
                                last = t.matmul(pb[2 + half][:, 0:512],
                                                ln2[:, kc * TPC + gch * 128: kc * TPC + (gch + 1) * 128],
                                                wS4[:, kc * D + half * 512: kc * D + half * 512 + 512],
                                                start=(kc == 0), stop=(kc == FC - 1))
                            return last
                        P.op("tensor", mmv, reads=(LN2[tile], buf("wS4"), PB[2 + half]), writes=(PB[2 + half],))
                    P.op("scalar", lambda s: s.copy(v16[:, 0:512], pb[2][:, 0:512]),
                         reads=(PB[2],), writes=(buf("v16"),))
                    P.op("scalar", lambda s: s.copy(v16[:, 512:1024], pb[3][:, 0:512]),
                         reads=(PB[3],), writes=(buf("v16"),))

                def jc(gch=gch):
                    def mmkv(t, gch=gch):
                        last = None
                        for p_ in range(FC):
                            bank = pb[6] if p_ < 4 else pb[7]
                            colo = (p_ % 4) * 128
                            last = t.matmul(bank[:, colo:colo + 128],
                                            k16[:, p_ * 128:(p_ + 1) * 128],
                                            v16[:, p_ * 128:(p_ + 1) * 128],
                                            start=(gch == 0 and (p_ % 4) == 0),
                                            stop=(gch == NCH - 1 and (p_ % 4) == 3))
                        return last
                    P.op("tensor", mmkv, reads=(buf("k16"), buf("v16"), PB[6], PB[7]),
                         writes=(PB[6], PB[7]))

                    def mmks(t):
                        last = None
                        for p_ in range(FC):
                            last = t.matmul(pb[2][:, p_:p_ + 1], k16[:, p_ * 128:(p_ + 1) * 128],
                                            ones1s[:, 0:1], start=(p_ == 0), stop=(p_ == FC - 1))
                        return last
                    P.op("tensor", mmks, reads=(buf("k16"), buf("consts"), PB[2]), writes=(PB[2],))
                    sks, dks = (gch % 2) * 8, ((gch + 1) % 2) * 8
                    if gch == 0:
                        P.op("vector", lambda v, dks=dks: v.tensor_copy(ksum[:, dks:dks + 8], pb[2][:, 0:8]),
                             reads=(PB[2],), writes=(buf("ksum"),), small=True)
                    else:
                        P.op("vector", lambda v, sks=sks, dks=dks: v.tensor_tensor(
                            ksum[:, dks:dks + 8], pb[2][:, 0:8], ksum[:, sks:sks + 8], ALU.add),
                            reads=(PB[2], buf("ksum")), writes=(buf("ksum"),), small=True)
                jobs += [ja, jb, jc]
            return jobs

        if launch == 1:
            P.dma("sync", "din", lambda s: s.dma_start(out=h16[:, :], in_=g["h16"].ap()),
                  writes=(buf("h16"),))
            for j in piece_jobs("xT", x16, X16, (0,), "scalar"):
                j()
            jobs = deque()
            wkj = wpiece_jobs("wk_l", wS3, buf("wS3"), "scalar")
            if DBG >= 2:
                bj = mk_build()
                for i, b in enumerate(bj):
                    jobs.append(b)
                    jobs.extend(wkj[2 * i: 2 * i + 2])
                jobs.extend(wkj[16:])
            else:
                bleft[0] = 0
                jobs.extend(wkj)
            jobs.extend(piece_jobs("xT", x16, X16, (1,), "scalar"))
            jobs.extend(wpiece_jobs("wv_l", wS4, buf("wS4"), "scalar"))
            jobs.extend(piece_jobs("xT", x16, X16, (2,), "scalar"))
            jobs.extend(piece_jobs("xT", x16, X16, (3,), "scalar"))
            if DBG >= 3:
                qrr = [0]
                for tile in range(NT):
                    while xtcnt.get(("xT", tile), 0) < FC:
                        jobs.popleft()()
                    ffn_phase(tile, final=False, fill=jobs)
                    if DBG >= 4:
                        ln2_block(tile)
                        for f in range(FC):
                            bnd_out(x16, x1_out, X16[tile], f, tile, qrr[0] % 2)
                            qrr[0] += 1
                            bnd_out(ln2, ln2_out, LN2[tile], f, tile, qrr[0] % 2, "vector")
                            qrr[0] += 1
                        if DBG >= 5:
                            jobs.extend(kv_jobs(tile))
                while jobs:
                    jobs.popleft()()

        if launch == 1:
            for p_ in range(FC if DBG >= 5 else 0):
                bank = pb[6] if p_ < 4 else pb[7]
                pbuf = PB[6] if p_ < 4 else PB[7]
                colo = (p_ % 4) * 128
                rd = (pbuf, SH[0], SH[1]) if p_ == 0 else (pbuf,)
                P.op("vector", lambda v, p_=p_, bank=bank, colo=colo: v.tensor_copy(
                    kvsb[0:64, p_ * 64:(p_ + 1) * 64], bank[0:64, colo:colo + 64]),
                    reads=rd, writes=(buf("kvsb"), SH[0], SH[1]))
                P.op("vector", lambda v, p_=p_, bank=bank, colo=colo: v.tensor_copy(
                    kvsb[64:128, p_ * 64:(p_ + 1) * 64], bank[64:128, colo + 64:colo + 128]),
                    reads=(pbuf,), writes=(buf("kvsb"), SH[0], SH[1]))
            if DBG >= 5:
                P.op("vector", lambda v: v.tensor_copy(kvsb[:, 512:520], ksum[:, 0:8]),
                     reads=(buf("ksum"),), writes=(buf("kvsb"), SH[0], SH[1]))
                P.dma("gpsimd", "dout", lambda gp: gp.dma_start(out=kv_out.ap(), in_=kvsb[:, :]),
                      reads=(buf("kvsb"),))

        def bjobs(tile):
            t0 = tile * TS
            QB = buf("qt")
            jobs = []

            def jq(oc):
                bk = 2 + (oc % 2)

                def mmq(t, oc=oc, t0=t0, bk=bk):
                    last = None
                    for kc in range(FC):
                        last = t.matmul(pb[bk][:, 0:512],
                                        wS3[:, kc * D + oc * 128: kc * D + (oc + 1) * 128],
                                        ln2[:, kc * TPC + t0: kc * TPC + t0 + TS],
                                        start=(kc == 0), stop=(kc == FC - 1))
                    return last
                P.op("tensor", mmq, reads=(buf("wS3"), LN2[tile], PB[bk]), writes=(PB[bk],))
                P.op("scalar", lambda s, oc=oc, bk=bk: s.activation(
                    qt[:, oc * TS:(oc + 1) * TS], pb[bk][:, 0:512], AF.Sigmoid),
                    reads=(PB[bk],), writes=(QB,))
            for oc in range(FC):
                jobs.append(lambda oc=oc: jq(oc))

            def jden():
                def mmden(t):
                    last = None
                    for p_ in range(FC):
                        last = t.matmul(pb[2][0:16, 0:512], dbd[:, p_ * 16:(p_ + 1) * 16],
                                        qt[:, p_ * TS:(p_ + 1) * TS],
                                        start=(p_ == 0), stop=(p_ == FC - 1))
                    return last
                P.op("tensor", mmden, reads=(buf("dbd"), QB, PB[2]), writes=(PB[2],))
                P.op("vector", lambda v: v.tensor_scalar_add(rdrow[:, :], pb[2][0:16, 0:512], 1e-8),
                     reads=(PB[2],), writes=(buf("aux32"),))
                P.op("vector", lambda v: v.reciprocal(rdr2[0:16, :], rdrow[:, :]),
                     reads=(buf("aux32"),), writes=(buf("rdr2"),))
            jobs.append(jden)

            def jnum(p_):
                P.op("tensor", lambda t, p_=p_: t.matmul(
                    pb[2][:, 0:512], kvbd[:, p_ * 128:(p_ + 1) * 128],
                    qt[:, p_ * TS:(p_ + 1) * TS], start=True, stop=True),
                    reads=(buf("kvbd"), QB, PB[2]), writes=(PB[2],))
                P.op("tensor", lambda t, p_=p_: t.matmul(
                    pb[3][:, 0:512], sel16s[:, p_ * 128:(p_ + 1) * 128], rdr2[0:16, :],
                    start=True, stop=True),
                    reads=(buf("rdr2"), buf("consts"), PB[3]), writes=(PB[3],))
                P.op("vector", lambda v: v.tensor_copy(rdb32[:, :], pb[3][:, 0:512]),
                     reads=(PB[3], buf("kvsb")), writes=(buf("rdb32"),))
                P.op("vector", lambda v, p_=p_: v.tensor_mul(
                    ao16[:, p_ * TS:(p_ + 1) * TS], pb[2][:, 0:512], rdb32[:, :]),
                    reads=(PB[2], buf("rdb32")), writes=(buf("ao16"),))
            for p_ in range(FC):
                jobs.append(lambda p_=p_: jnum(p_))

            def jfc(oc):
                bk = 6 + (oc % 2)

                def mmfc(t, oc=oc, bk=bk):
                    last = None
                    for hc in range(FC):
                        last = t.matmul(pb[bk][:, 0:512],
                                        wS4[:, hc * D + oc * 128: hc * D + (oc + 1) * 128],
                                        ao16[:, hc * TS:(hc + 1) * TS],
                                        start=(hc == 0), stop=(hc == FC - 1))
                    return last
                P.op("tensor", mmfc, reads=(buf("wS4"), buf("ao16"), PB[bk]), writes=(PB[bk],))
                P.op("vector", lambda v, oc=oc, t0=t0, bk=bk: v.tensor_tensor(
                    qt[:, oc * TS:(oc + 1) * TS], pb[bk][:, 0:512],
                    ln2[:, oc * TPC + t0: oc * TPC + t0 + TS], ALU.add),
                    reads=(PB[bk], LN2[tile]), writes=(QB,))
            for oc in range(FC):
                jobs.append(lambda oc=oc: jfc(oc))

            def jstat():
                def mmsy(t):
                    last = None
                    for f in range(FC):
                        last = t.matmul(pb[2][0:1, 0:512], ones1s[:, 0:1],
                                        qt[:, f * TS:(f + 1) * TS],
                                        start=(f == 0), stop=(f == FC - 1))
                    return last
                P.op("tensor", mmsy, reads=(QB, buf("consts"), PB[2]), writes=(PB[2],))
                for f in range(FC):
                    P.op("scalar", lambda s, f=f: s.activation(
                        ao16[:, f * TS:(f + 1) * TS], qt[:, f * TS:(f + 1) * TS], AF.Square),
                        reads=(QB,), writes=(buf("ao16"),))

                def mmsy2(t):
                    last = None
                    for f in range(FC):
                        last = t.matmul(pb[3][0:1, 0:512], ones1s[:, 0:1],
                                        ao16[:, f * TS:(f + 1) * TS],
                                        start=(f == 0), stop=(f == FC - 1))
                    return last
                P.op("tensor", mmsy2, reads=(buf("ao16"), buf("consts"), PB[3]), writes=(PB[3],))
                P.op("vector", lambda v: v.tensor_scalar_mul(rowm, pb[2][0:1, 0:512], 1.0 / D),
                     reads=(PB[2],), writes=(buf("rowm"),))
                P.op("vector", lambda v: v.tensor_scalar_mul(rowq, pb[3][0:1, 0:512], 1.0 / D),
                     reads=(PB[3],), writes=(buf("rowq"),))
                P.op("tensor", lambda t: t.matmul(pb[2][:, 0:512], onesrs[0:1, :], rowm, start=True, stop=True),
                     reads=(buf("rowm"), buf("consts"), PB[2]), writes=(PB[2],))
                P.op("vector", lambda v: v.tensor_tensor(rowm, rowm, rowm, ALU.mult),
                     reads=(buf("rowm"),), writes=(buf("rowm"),), small=True)
                P.op("vector", lambda v: v.tensor_sub(rowq, rowq, rowm),
                     reads=(buf("rowq"), buf("rowm")), writes=(buf("rowq"),))
                P.op("scalar", lambda s: s.activation(rowm, rowq, AF.Ln, bias=cfs[0:1, 0:1]),
                     reads=(buf("rowq"), buf("consts")), writes=(buf("rowm"),))
                P.op("scalar", lambda s: s.activation(rowq, rowm, AF.Exp, scale=cfs[0:1, 1:2]),
                     reads=(buf("rowm"), buf("consts")), writes=(buf("rowq"),))
                P.op("tensor", lambda t: t.matmul(pb[3][:, 0:512], onesrs[0:1, :], rowq, start=True, stop=True),
                     reads=(buf("rowq"), buf("consts"), PB[3]), writes=(PB[3],))
                P.op("vector", lambda v: v.tensor_copy(r2b32[:, :], pb[3][:, 0:512]),
                     reads=(PB[3],), writes=(buf("r2b32"),))
                P.op("vector", lambda v: v.tensor_copy(mbf[:, :], pb[2][:, 0:512]),
                     reads=(PB[2],), writes=(buf("aux32"),))
            jobs.append(jstat)

            def jret(fpair):
                for f in (2 * fpair, 2 * fpair + 1):
                    eng = "vector" if f < 5 else "gpsimd"
                    tmp = tmpv if f < 5 else tmpg
                    tb = buf("tmpv") if f < 5 else buf("tmpg")
                    P.op(eng, lambda e2, f=f, tmp=tmp: e2.tensor_sub(
                        tmp[:, :], qt[:, f * TS:(f + 1) * TS], mbf[:, :]),
                        reads=(QB, buf("aux32")), writes=(tb,))
                    P.op(eng, lambda e2, tmp=tmp: e2.tensor_mul(
                        tmp[:, :], tmp[:, :], r2b32[:, :]),
                        reads=(tb, buf("r2b32")), writes=(tb,))
                    P.op(eng, lambda e2, f=f, tmp=tmp: e2.tensor_scalar(
                        tmp[:, :], tmp[:, :], gbs[:, f:f + 1], gbs[:, 8 + f:9 + f], ALU.mult, ALU.add),
                        reads=(tb, buf("consts")), writes=(tb,))
                    P.op(eng, lambda e2, f=f, t0=t0, tmp=tmp: e2.tensor_tensor(
                        x16[:, f * TPC + t0: f * TPC + t0 + TS], tmp[:, :],
                        x16[:, f * TPC + t0: f * TPC + t0 + TS], ALU.add),
                        reads=(tb, X16[tile]), writes=(X16[tile],))
            for fp in range(4):
                jobs.append(lambda fp=fp: jret(fp))
            return jobs

        if launch == 2:
            for j in piece_jobs("ln2f", ln2, LN2, (0,), "scalar"):
                j()
            for j in wpiece_jobs("wq_l", wS3, buf("wS3")):
                j()
            kv_ingest()
            for j in piece_jobs("x1f", x16, X16, (0,), "gpsimd"):
                j()
            P.dma("sync", "din", lambda s: s.dma_start(out=h16[:, :], in_=g["h16"].ap()),
                  writes=(buf("h16"),))
            for j in wpiece_jobs("wfc_l", wS4, buf("wS4")):
                j()
            for j in bjobs(0):
                j()
            for j in wpiece_jobs("wqt_f", wqt, buf("wqt")):
                j()
            jobs2 = deque()
            for tile in range(NT):
                if tile + 1 < NT:
                    jobs2.extend(piece_jobs("ln2f", ln2, LN2, (tile + 1,), "scalar"))
                    bj = bjobs(tile + 1)
                    jobs2.extend(bj[:8])
                    jobs2.extend(piece_jobs("x1f", x16, X16, (tile + 1,), "scalar"))
                    jobs2.extend(bj[8:])
                ffn_phase(tile, final=True, fill=jobs2)
                while jobs2:
                    jobs2.popleft()()

        P.wait("gpsimd", ("dout", P.cnt["dout"], False))
        if P.cnt["dout1"] > 0:
            P.wait("gpsimd", ("dout1", P.cnt["dout1"], False))

        with nc.Block() as block:
            @block.tensor
            def _(t):
                P.emit(t, "tensor", sems)

            @block.vector
            def _(v):
                P.emit(v, "vector", sems)

            @block.scalar
            def _(s):
                P.emit(s, "scalar", sems)

            @block.gpsimd
            def _(gp):
                P.emit(gp, "gpsimd", sems)

            @block.sync
            def _(s):
                P.emit(s, "sync", sems)

    return nc


def kernel(L0_attn_inputs, L0_attn_W_Q, L0_attn_W_K, L0_attn_W_V, L0_attn_W_fc,
           L0_attn_gamma, L0_attn_beta, L0_ffn_weight):
    x = np.asarray(L0_attn_inputs)[0]
    xT = np.ascontiguousarray(x.T)
    h16 = _hadamard_bf16()
    i128 = np.eye(128, dtype=ml_dtypes.bfloat16)
    ones1 = np.ones((128, 1), dtype=ml_dtypes.bfloat16)
    onesrow = np.ones((1, 128), dtype=np.float32)
    sel16 = _sel16_f32()
    gb = np.concatenate([
        np.asarray(L0_attn_gamma).reshape(8, 128).T,
        np.asarray(L0_attn_beta).reshape(8, 128).T], axis=1).astype(np.float32)
    gb = np.ascontiguousarray(gb)
    constf = np.zeros((128, 8), np.float32)
    constf[:, 0] = EPS
    constf[:, 1] = -0.5

    bust1 = 16 + (VERSION * 7) % 48
    bust2 = 17 + (VERSION * 5) % 48
    common = {"h16": h16, "i128": i128, "ones1": ones1, "onesrow": onesrow,
              "constf": constf}

    nc1 = build_nc(bust1, 1)
    wffn = np.ascontiguousarray(np.asarray(L0_ffn_weight, dtype=np.float32))
    base1 = dict(common)
    base1.update({
        "wk_l": np.ascontiguousarray(np.asarray(L0_attn_W_K, dtype=np.float32)),
        "wv_l": np.ascontiguousarray(np.asarray(L0_attn_W_V, dtype=np.float32)),
        "wffn_l": wffn,
        "bust": np.zeros((1, bust1), np.float32),
    })
    in_maps1 = []
    for c in range(NCORE):
        m = dict(base1)
        m["xT"] = np.ascontiguousarray(xT[:, c * TPC:(c + 1) * TPC])
        in_maps1.append(m)
    res1 = run_bass_kernel_spmd(nc1, in_maps1, core_ids=list(range(NCORE)))
    kvparts = np.concatenate(
        [np.asarray(res1.results[c]["kv_out"]) for c in range(NCORE)], axis=0)

    nc2 = build_nc(bust2, 2)
    base2 = dict(common)
    base2.update({
        "wq_l": np.ascontiguousarray(np.asarray(L0_attn_W_Q, dtype=np.float32)),
        "wfc_l": np.ascontiguousarray(np.asarray(L0_attn_W_fc, dtype=np.float32)),
        "gb": gb, "sel16": sel16,
        "wqt_f": np.ascontiguousarray(np.asarray(res1.results[0]["wqt_out"], dtype=np.float32)),
        "kvparts": np.ascontiguousarray(kvparts.astype(np.float32)),
        "bust": np.zeros((1, bust2), np.float32),
    })
    in_maps2 = []
    for c in range(NCORE):
        m = dict(base2)
        m["x1f"] = np.asarray(res1.results[c]["x1_out"])
        m["ln2f"] = np.asarray(res1.results[c]["ln2_out"])
        in_maps2.append(m)
    res2 = run_bass_kernel_spmd(nc2, in_maps2, core_ids=list(range(NCORE)))
    out = np.concatenate([r["yT"] for r in res2.results], axis=1)
    return np.ascontiguousarray(out.T).reshape(1, 16384, 1024).astype(np.float32)


# version 20
# speedup vs baseline: 1.1851x; 1.0268x over previous
VERSION = 22

import os
import numpy as np
import ml_dtypes
import concourse.bass as bass
import concourse.mybir as mybir
from concourse.bass_utils import run_bass_kernel_spmd
from contextlib import ExitStack
from collections import deque

F32 = mybir.dt.float32
BF16 = mybir.dt.bfloat16
AF = mybir.ActivationFunctionType
ALU = mybir.AluOpType
AX = mybir.AxisListType

D = 1024
NCORE = 8
TPC = 2048
NT = 4
TS = 512
NCH = 16
FC = 8
NH = 16
EPS = 1e-5
DBG = int(os.environ.get("KDBG", "9"))
DBG2 = int(os.environ.get("KDBG2", "9"))
FULLSYNC = os.environ.get("BASS_SIM_FP64") == "1"


def _hadamard_bf16():
    h = np.array([[1.0]], dtype=np.float64)
    while h.shape[0] < D:
        h = np.block([[h, h], [h, -h]])
    h = (h / 32.0).astype(np.float32)
    hp = h.reshape(FC, 128, D).transpose(1, 0, 2).reshape(128, FC * D)
    return hp.astype(ml_dtypes.bfloat16)


def _sel16_f32():
    s = np.zeros((16, FC * 128), np.float32)
    for p in range(FC):
        s[2 * p, p * 128: p * 128 + 64] = 1.0
        s[2 * p + 1, p * 128 + 64: p * 128 + 128] = 1.0
    return s.astype(ml_dtypes.bfloat16)


class Buf:
    __slots__ = ("name", "writer", "readers")

    def __init__(self, name):
        self.name = name
        self.writer = None
        self.readers = []


class Prog:
    ENGS = ("tensor", "vector", "scalar", "gpsimd", "sync")
    DMAQ = ("din", "din1", "dout", "dout1", "dcc")

    def __init__(self):
        self.ops = {e: [] for e in self.ENGS}
        self.cnt = {e: 0 for e in list(self.ENGS) + list(self.DMAQ)}
        self.hw = {e: {} for e in self.ENGS}
        self.selfhw = {e: 0 for e in self.ENGS}

    def _collect(self, eng, reads, writes):
        waits = {}

        def need(tok):
            if tok is None:
                return
            te, tv, sm = tok
            if te == eng and not FULLSYNC:
                if not (sm and (self.cnt[eng] - tv) < 3):
                    return
            if waits.get(te, -1) < tv:
                waits[te] = tv

        for b in reads:
            need(b.writer)
        for b in writes:
            need(b.writer)
            for t2 in b.readers:
                need(t2)
        out = []
        for e2, v in sorted(waits.items()):
            if e2 == eng:
                if v > self.selfhw[eng]:
                    out.append((e2, v))
                    self.selfhw[eng] = v
            elif self.hw[eng].get(e2, 0) < v:
                out.append((e2, v))
                self.hw[eng][e2] = v
        return out

    def op(self, eng, fn, reads=(), writes=(), small=False, after=()):
        waits = self._collect(eng, reads, writes)
        for tok in after:
            te, tv, _sm = tok
            if te == eng:
                continue
            if self.hw[eng].get(te, 0) < tv:
                waits.append((te, tv))
                self.hw[eng][te] = tv
        self.cnt[eng] += 1
        tok = (eng, self.cnt[eng], small)
        self.ops[eng].append((waits, fn, eng, 1))
        for b in writes:
            b.writer = tok
            b.readers = []
        for b in reads:
            b.readers.append(tok)
        return tok

    def dma(self, eng, q, fn, reads=(), writes=()):
        waits = self._collect(eng, reads, writes)
        if self.cnt[q] > 0 and self.hw[eng].get(q, 0) < self.cnt[q]:
            waits.append((q, self.cnt[q]))
            self.hw[eng][q] = self.cnt[q]
        self.cnt[q] += 16
        tok = (q, self.cnt[q], False)
        self.ops[eng].append((waits, fn, q, 16))
        for b in writes:
            b.writer = tok
            b.readers = []
        for b in reads:
            b.readers.append(tok)
        return tok

    def wait(self, eng, tok):
        te, tv, _ = tok
        if te == eng:
            if tv > self.selfhw[eng]:
                self.selfhw[eng] = tv
                self.ops[eng].append(([(te, tv)], None, None, 0))
            return
        if self.hw[eng].get(te, 0) < tv:
            self.hw[eng][te] = tv
            self.ops[eng].append(([(te, tv)], None, None, 0))

    def emit(self, engobj, eng, sems):
        for waits, fn, incsem, incval in self.ops[eng]:
            for (se, v) in waits:
                engobj.wait_ge(sems[se], v)
            if fn is None:
                continue
            inst = fn(engobj)
            if incval:
                inst.then_inc(sems[incsem], incval)


def build_nc(bust_len, launch):
    nc = bass.Bass("TRN2", target_bir_lowering=False)
    g = {}

    def din(name, shape, dt=F32):
        g[name] = nc.dram_tensor(name, shape, dt, kind="ExternalInput")

    din("h16", [128, FC * D], BF16)
    din("i128", [128, 128], BF16)
    din("ones1", [128, 1], BF16)
    din("onesrow", [1, 128])
    din("constf", [128, 8])
    din("bust", [1, bust_len])
    if launch == 1:
        din("xT", [D, TPC])
        din("wk_l", [D, D]); din("wv_l", [D, D]); din("wffn_l", [D, D])
        kv_out = nc.dram_tensor("kv_out", [128, 520], F32, kind="ExternalOutput")
        x1_out = nc.dram_tensor("x1_out", [D, TPC], F32, kind="ExternalOutput")
        ln2_out = nc.dram_tensor("ln2_out", [D, TPC], F32, kind="ExternalOutput")
        wqt_out = nc.dram_tensor("wqt_out", [128, FC * D], F32, kind="ExternalOutput")
    else:
        din("wq_l", [D, D]); din("wfc_l", [D, D])
        din("wqt_f", [128, FC * D])
        din("gb", [128, 16])
        din("sel16", [16, FC * 128], BF16)
        din("x1f", [D, TPC])
        din("ln2f", [D, TPC])
        din("kvparts", [8 * 128, 520])
        yT = nc.dram_tensor("yT", [D, TPC], F32, kind="ExternalOutput")

    P = Prog()
    with ExitStack() as ctx:
        e = ctx.enter_context
        ctx.enter_context(nc.allow_low_precision(
            reason="bf16 residual stream / operand casts are deliberate; "
                   "matmuls accumulate fp32 in PSUM"))

        def sb(name, shape, dt):
            return e(nc.sbuf_tensor(name, shape, dt))

        x16 = sb("x16", [128, FC * TPC], BF16)
        ln2 = sb("ln2s", [128, FC * TPC], BF16)
        h16 = sb("h16s", [128, FC * D], BF16)
        wqt = sb("wqt", [128, FC * D], BF16)
        wS3 = sb("wS3", [128, FC * D], BF16)
        wS4 = sb("wS4", [128, FC * D], BF16)
        SW = 2048 if launch == 1 else 1024
        stage = sb("stage", [128, SW], F32)
        LW = 2 if launch == 1 else 1
        PRW = 2 * D
        u32b = sb("u32b", [128, PRW], BF16)
        bb32 = sb("bb32", [128, PRW], BF16)
        v32 = sb("v32", [128, LW * PRW], BF16)
        sqb = sb("sqb", [128, 1040], F32)
        rs16 = sb("rs16", [128, PRW], BF16)
        xqT = sb("xqT", [128, FC * TS], BF16)
        if launch == 2:
            qt = sb("qt", [128, FC * TS], BF16)
            ao16 = sb("ao16", [128, FC * TS], BF16)
        if launch == 1:
            k16 = sb("k16", [128, D], BF16)
            v16 = sb("v16", [128, D], BF16)
            wbld = sb("wbld", [128, D], F32)
        statc = sb("statc", [128, 40], F32)
        ksum = sb("ksum", [128, 16], F32) if launch == 1 else None
        r2b32 = sb("r2b32", [128, TS], F32 if launch == 1 else BF16)
        aux32 = sb("aux32", [128, TS], F32)
        tmpv = sb("tmpv", [128, TS], BF16)
        tmpg = sb("tmpg", [128, TS], BF16)
        if launch == 2:
            rdrow = aux32[0:16, :]
            mbf = aux32
        kvsb = sb("kvsb", [128, 520], F32) if launch == 2 else stage[:, 0:520]
        rdb32 = kvsb[:, 0:512] if launch == 2 else None
        if launch == 2:
            kvbd = sb("kvbd", [128, FC * 128], BF16)
            dbd = sb("dbd", [128, FC * 16], BF16)
            rdr2 = sb("rdr2", [16, TS], BF16)
        rowmt = sb("rowmt", [1, TS], F32)
        rowqt = sb("rowqt", [1, TS], F32)
        rowm = rowmt[0:1, :]
        rowq = rowqt[0:1, :]
        i128s = sb("i128s", [128, 128], BF16)
        ones1s = sb("ones1s", [128, 1], BF16)
        onesrs = sb("onesrs", [1, 128], F32)
        if launch == 2:
            sel16s = sb("sel16s", [16, FC * 128], BF16)
            gbs = sb("gbs", [128, 16], F32)
        cfs = sb("cfs", [128, 8], F32)
        busts = sb("busts", [1, bust_len], F32)

        pb = [e(nc.psum_tensor(f"pb{i}", [128, 512], F32)) for i in range(8)]

        semnames = list(Prog.ENGS) + list(Prog.DMAQ)
        sems = {n: e(nc.semaphore("sem_" + n)) for n in semnames}

        B = {}

        def buf(nm):
            if nm not in B:
                B[nm] = Buf(nm)
            return B[nm]

        PB = [buf(f"pb{i}") for i in range(8)]
        X16 = [buf(f"x16_{t}") for t in range(NT)]
        LN2 = [buf(f"ln2_{t}") for t in range(NT)]
        SH = [buf("stage_0"), buf("stage_1"), buf("stage_2"), buf("stage_3")]
        _ldpc = [0]
        xtcnt = {}

        def piece_jobs(srcn, dstt, tilebufs, tiles, cpeng):
            ar = g[srcn].ap().rearrange("(f p) t -> f p t", p=128)
            pieces = [(t, f) for t in tiles for f in range(FC)]
            qs = []

            def issue(k):
                t, f = pieces[k]
                i = _ldpc[0] % 2
                _ldpc[0] += 1
                qs.append(i)
                P.dma("sync", "din" if i == 0 else "din1",
                      lambda s, f=f, t=t, i=i, ar=ar: s.dma_start(
                          out=stage[:, i * 512:(i + 1) * 512],
                          in_=ar[f][:, t * TS:(t + 1) * TS]),
                      reads=(SH[i],), writes=(SH[i],))

            def copy(k):
                t, f = pieces[k]
                i = qs[k]

                def fn(eng, f=f, t=t, i=i):
                    src = stage[:, i * 512:(i + 1) * 512]
                    dst = dstt[:, f * TPC + t * TS: f * TPC + (t + 1) * TS]
                    return eng.copy(dst, src) if cpeng == "scalar" \
                        else eng.tensor_copy(dst, src)
                P.op(cpeng, fn, reads=(SH[i],), writes=(tilebufs[t],))
                xtcnt[(srcn, t)] = xtcnt.get((srcn, t), 0) + 1

            n = len(pieces)
            jobs = [lambda: issue(0)]
            for k in range(n):
                def jk(k=k):
                    if k + 1 < n:
                        issue(k + 1)
                    copy(k)
                jobs.append(jk)
            for j in jobs:
                j.light = True
            return jobs

        def wpiece_jobs(srcn, slot, slotbuf, cpeng="gpsimd"):
            if srcn == "wqt_f":
                ar = None
            else:
                ar = g[srcn].ap().rearrange("(c p) o -> c p o", p=128)
            qs = []

            def issue(k):
                i = _ldpc[0] % 2
                _ldpc[0] += 1
                qs.append(i)
                if ar is None:
                    P.dma("sync", "din" if i == 0 else "din1",
                          lambda s, k=k, i=i: s.dma_start(
                              out=stage[:, i * 512:(i + 1) * 512],
                              in_=g["wqt_f"].ap()[:, k * 512:(k + 1) * 512]),
                          reads=(SH[i],), writes=(SH[i],))
                else:
                    c, h = k // 2, k % 2
                    P.dma("sync", "din" if i == 0 else "din1",
                          lambda s, c=c, h=h, i=i, ar=ar: s.dma_start(
                              out=stage[:, i * 512:(i + 1) * 512],
                              in_=ar[c][:, h * 512:(h + 1) * 512]),
                          reads=(SH[i],), writes=(SH[i],))

            def copy(k):
                i = qs[k]
                c, h = k // 2, k % 2
                base = k * 512 if ar is None else c * D + h * 512

                def fn(eng, base=base, i=i):
                    src = stage[:, i * 512:(i + 1) * 512]
                    return eng.copy(slot[:, base:base + 512], src) if cpeng == "scalar"                         else eng.tensor_copy(slot[:, base:base + 512], src)
                P.op(cpeng, fn, reads=(SH[i],), writes=(slotbuf,))

            jobs = [lambda: issue(0)]
            for k in range(16):
                def jk(k=k):
                    if k + 1 < 16:
                        issue(k + 1)
                    copy(k)
                jobs.append(jk)
            for j in jobs:
                j.light = True
            return jobs

        constlist = [(i128s, "i128"), (ones1s, "ones1"), (onesrs, "onesrow"),
                     (cfs, "constf"), (busts, "bust")]
        if launch == 2:
            constlist += [(sel16s, "sel16"), (gbs, "gb")]
        for (dst, src) in constlist:
            P.dma("sync", "din", lambda s, d=dst, so=src: s.dma_start(out=d[:, :], in_=g[so].ap()),
                  writes=(buf("consts"),))

        bleft = [FC]

        def mk_build():
            wr = g["wffn_l"].ap().rearrange("(c p) i -> c p i", p=128)
            wqtr = wqt_out.ap().rearrange("p (i o c) -> o i p c", i=8, o=8)
            WH = (SH[2], SH[3])

            def bdma(oc):
                P.dma("sync", "din1", lambda s, oc=oc: s.dma_start(
                    out=stage[:, 1024:2048], in_=wr[oc]), reads=WH, writes=WH)

            def bjob(oc):
                P.op("vector", lambda v: v.tensor_reduce(
                    statc[:, 0:1], stage[:, 1024:2048], AX.X, ALU.add,
                    apply_absolute_value=True),
                    reads=WH, writes=(buf("sc0"),), small=True)
                P.op("vector", lambda v: v.tensor_scalar(
                    statc[:, 1:2], statc[:, 0:1], 1.0 / D, 1e-5, ALU.mult, ALU.max),
                    reads=(buf("sc0"),), writes=(buf("sc1"),), small=True)
                P.op("vector", lambda v: v.reciprocal(statc[:, 2:3], statc[:, 1:2]),
                     reads=(buf("sc1"),), writes=(buf("sc2"),), small=True)
                P.op("vector", lambda v: v.tensor_single_scalar(
                    statc[:, 3:4], statc[:, 1:2], 0.5, ALU.mult),
                    reads=(buf("sc1"),), writes=(buf("sc3"),), small=True)
                P.op("scalar", lambda s: s.activation(
                    wbld[:, :], stage[:, 1024:2048], AF.Copy, scale=statc[:, 2:3]),
                    reads=WH + (buf("sc2"),), writes=(buf("wbld"),))
                P.op("vector", lambda v: v.tensor_scalar(
                    stage[:, 1024:2048], wbld[:, :], 0.5, 1.0, ALU.is_gt, ALU.mult),
                    reads=(buf("wbld"),), writes=WH)
                P.op("vector", lambda v: v.tensor_scalar(
                    wbld[:, :], wbld[:, :], -0.5, 1.0, ALU.is_lt, ALU.mult),
                    reads=(buf("wbld"),), writes=(buf("wbld"),))
                P.op("vector", lambda v: v.tensor_sub(
                    stage[:, 1024:2048], stage[:, 1024:2048], wbld[:, :]),
                    reads=WH + (buf("wbld"),), writes=WH)
                P.op("scalar", lambda s: s.activation(
                    tmpg[:, 0:512], stage[:, 1024:1536], AF.Copy, scale=statc[:, 3:4]),
                    reads=WH + (buf("sc3"),), writes=(buf("tmpg"),))
                P.op("scalar", lambda s: s.activation(
                    tmpv[:, 0:512], stage[:, 1536:2048], AF.Copy, scale=statc[:, 3:4]),
                    reads=WH + (buf("sc3"),), writes=(buf("tmpv"),))
                if oc + 1 < FC:
                    bdma(oc + 1)
                for ic in range(FC):
                    bk = 4 + (ic % 2)
                    srcop = tmpg if ic < 4 else tmpv
                    srcb = buf("tmpg") if ic < 4 else buf("tmpv")
                    co = (ic % 4) * 128
                    P.op("tensor", lambda t, co=co, bk=bk, srcop=srcop: t.matmul(
                        pb[bk][:, 0:128], srcop[:, co:co + 128], i128s[:, :],
                        start=True, stop=True),
                        reads=(srcb, buf("consts"), PB[bk]), writes=(PB[bk],))
                    P.op("scalar", lambda s, ic=ic, oc=oc, bk=bk: s.copy(
                        wqt[:, ic * D + oc * 128: ic * D + (oc + 1) * 128], pb[bk][:, 0:128]),
                        reads=(PB[bk],), writes=(buf("wqt"),))
                    rdst = r2b32 if ic < 4 else aux32
                    rb = buf("r2b32") if ic < 4 else buf("aux32")
                    j = ic % 4
                    P.op("scalar", lambda s, j=j, bk=bk, rdst=rdst: s.copy(
                        rdst[:, j * 128:(j + 1) * 128], pb[bk][:, 0:128]),
                        reads=(PB[bk],), writes=(rb,))
                for j in range(4):
                    P.dma("gpsimd", "dout", lambda gp, oc=oc, j=j: gp.dma_start(
                        out=wqtr[oc][j], in_=r2b32[:, j * 128:(j + 1) * 128]),
                        reads=(buf("r2b32"),))
                    P.dma("gpsimd", "dout1", lambda gp, oc=oc, j=j: gp.dma_start(
                        out=wqtr[oc][4 + j], in_=aux32[:, j * 128:(j + 1) * 128]),
                        reads=(buf("aux32"),))
                bleft[0] -= 1
            bdma(0)
            return [lambda oc=oc: bjob(oc) for oc in range(FC)]

        def kv_ingest():
            kvr = g["kvparts"].ap().rearrange("(r p) c -> r p c", p=128)
            P.op("vector", lambda v: v.memset(kvbd[:, :], 0.0), writes=(buf("kvbd"),))
            P.op("vector", lambda v: v.memset(dbd[:, :], 0.0), writes=(buf("dbd"),))

            def slotb(r):
                return (buf("sqbA"), buf("sqbB")) if r % 2 == 0 else (buf("sqbB"),)

            def dmain(r):
                off = (r % 2) * 520
                P.dma("sync", "din" if r % 2 == 0 else "din1",
                      lambda s, r=r, off=off: s.dma_start(
                          out=sqb[:, off:off + 520], in_=kvr[r]),
                      reads=slotb(r), writes=slotb(r))
            dmain(0)
            for r in range(8):
                if r + 1 < 8:
                    dmain(r + 1)
                off = (r % 2) * 520
                if r == 0:
                    P.op("gpsimd", lambda g2, off=off: g2.tensor_copy(
                        kvsb[:, :], sqb[:, off:off + 520]),
                        reads=slotb(r), writes=(buf("kvsb"),))
                else:
                    P.op("gpsimd", lambda g2, off=off, r=r: g2.tensor_tensor(
                        kvsb[:, :], sqb[:, off:off + 520], kvsb[:, :], ALU.add),
                        reads=slotb(r) + (buf("kvsb"),), writes=(buf("kvsb"),))
            for p_ in range(FC):
                P.op("gpsimd", lambda g2, p_=p_: g2.tensor_copy(
                    kvbd[0:64, p_ * 128: p_ * 128 + 64], kvsb[0:64, p_ * 64:(p_ + 1) * 64]),
                    reads=(buf("kvsb"),), writes=(buf("kvbd"),))
                P.op("gpsimd", lambda g2, p_=p_: g2.tensor_copy(
                    kvbd[64:128, p_ * 128 + 64: p_ * 128 + 128], kvsb[64:128, p_ * 64:(p_ + 1) * 64]),
                    reads=(buf("kvsb"),), writes=(buf("kvbd"),))
                P.op("gpsimd", lambda g2, p_=p_: g2.tensor_copy(
                    dbd[0:64, p_ * 16 + 2 * p_: p_ * 16 + 2 * p_ + 1], kvsb[0:64, 512 + p_:513 + p_]),
                    reads=(buf("kvsb"),), writes=(buf("dbd"),), small=True)
                P.op("gpsimd", lambda g2, p_=p_: g2.tensor_copy(
                    dbd[64:128, p_ * 16 + 2 * p_ + 1: p_ * 16 + 2 * p_ + 2], kvsb[64:128, 512 + p_:513 + p_]),
                    reads=(buf("kvsb"),), writes=(buf("dbd"),), small=True)

        def ffn_phase(tile, final, fill=None):
            def F():
                if fill:
                    fill.popleft()()
                    if fill and len(fill) > 16 and getattr(fill[0], "light", False):
                        fill.popleft()()
            t0 = tile * TS

            def emit_xh_g(gch):
                for half, bk in ((0, 0), (1, 1)):
                    def mmg(t, bk=bk, gch=gch, half=half):
                        last = None
                        for kc in range(FC):
                            last = t.matmul(pb[bk][:, 0:512],
                                            x16[:, kc * TPC + gch * 128: kc * TPC + (gch + 1) * 128],
                                            h16[:, kc * D + half * 512: kc * D + half * 512 + 512],
                                            start=(kc == 0), stop=(kc == FC - 1))
                        return last
                    P.op("tensor", mmg, reads=(X16[gch // 4], buf("h16"), PB[bk]), writes=(PB[bk],))
                    F()

            def emit_xh(cc):
                emit_xh_g(tile * 4 + cc)

            if final or tile == 0:
                emit_xh(0)
            for pp in range(2):
                if DBG2 < 1:
                    continue
                pph = (pp % 2) * PRW if LW == 2 else 0
                pbuf = str(pp % 2) if LW == 2 else "0"
                vb, bbb = buf("v32" + pbuf), buf("bb32")
                ubb = buf("u32b")
                for ci in range(2):
                    cc = 2 * pp + ci
                    co = pph + ci * D
                    t_m1 = P.op("vector", lambda v, ci=ci: v.tensor_scalar_mul(
                        statc[:, 8 + ci:9 + ci], pb[0][:, 0:1], 1.0 / 32),
                        reads=(PB[0],), writes=(buf(f"sc8{ci}"),), small=True)
                    t_a0 = P.op("vector", lambda v, ci=ci: v.tensor_reduce(
                        statc[:, 32 + ci:33 + ci], pb[0][:, 1:512], AX.X, ALU.max,
                        apply_absolute_value=True),
                        reads=(PB[0],), writes=(buf(f"sc32{ci}"),), small=True)
                    t_a1 = P.op("vector", lambda v, ci=ci: v.tensor_reduce(
                        statc[:, 34 + ci:35 + ci], pb[1][:, 0:512], AX.X, ALU.max,
                        apply_absolute_value=True),
                        reads=(PB[1],), writes=(buf(f"sc34{ci}"),), small=True)
                    t_sqa = P.op("scalar", lambda s: s.activation(
                        sqb[:, 0:512], pb[0][:, 0:512], AF.Square),
                        reads=(PB[0],), writes=(buf("sqbA"),), after=(t_m1, t_a0))
                    t_sqb = P.op("scalar", lambda s: s.activation(
                        sqb[:, 512:1024], pb[1][:, 0:512], AF.Square),
                        reads=(PB[1],), writes=(buf("sqbB"),), after=(t_a1,))
                    P.op("vector", lambda v, ci=ci: v.tensor_tensor(
                        statc[:, 36 + ci:37 + ci], statc[:, 32 + ci:33 + ci],
                        statc[:, 34 + ci:35 + ci], ALU.max),
                        reads=(buf(f"sc32{ci}"), buf(f"sc34{ci}")), writes=(buf(f"sc36{ci}"),), small=True)
                    P.op("vector", lambda v, ci=ci: v.tensor_single_scalar(
                        statc[:, 12 + ci:13 + ci], statc[:, 36 + ci:37 + ci], 1e-5, ALU.max),
                        reads=(buf(f"sc36{ci}"),), writes=(buf(f"sc12{ci}"),), small=True)
                    P.op("vector", lambda v, ci=ci: v.reciprocal(
                        statc[:, 14 + ci:15 + ci], statc[:, 12 + ci:13 + ci]),
                        reads=(buf(f"sc12{ci}"),), writes=(buf(f"sc14{ci}"),), small=True)
                    P.op("vector", lambda v, ci=ci: v.tensor_scalar_mul(
                        statc[:, 16 + ci:17 + ci], statc[:, 14 + ci:15 + ci], 7.0),
                        reads=(buf(f"sc14{ci}"),), writes=(buf(f"sc16{ci}"),), small=True)
                    P.op("vector", lambda v, co=co, ci=ci: v.tensor_scalar_mul(
                        v32[:, co:co + 512], pb[0][:, 0:512], statc[:, 16 + ci:17 + ci]),
                        reads=(PB[0], buf(f"sc16{ci}")), writes=(vb,), after=(t_sqa,))
                    P.op("vector", lambda v, co=co, ci=ci: v.tensor_scalar_mul(
                        v32[:, co + 512:co + 1024], pb[1][:, 0:512], statc[:, 16 + ci:17 + ci]),
                        reads=(PB[1], buf(f"sc16{ci}")), writes=(vb,), after=(t_sqb,))
                    if cc + 1 < 4:
                        emit_xh(cc + 1)
                    P.op("vector", lambda v, ci=ci: v.reduce_sum(
                        statc[:, 10 + ci:11 + ci], sqb[:, 0:D], AX.X),
                        reads=(buf("sqbA"), buf("sqbB")), writes=(buf(f"sc10{ci}"),), small=True)
                if DBG2 < 2:
                    continue
                P.op("vector", lambda v, pph=pph: v.memset(
                    v32[:, pph:pph + PRW].rearrange("p (c d) -> p c d", d=D)[:, :, 0:1], 0.0),
                    writes=(vb,), small=True)
                P.op("vector", lambda v: v.tensor_tensor(
                    statc[:, 18:20], statc[:, 8:10], statc[:, 8:10], ALU.mult),
                    reads=(buf("sc80"), buf("sc81")), writes=(buf("sc18"),), small=True)
                P.op("vector", lambda v: v.tensor_single_scalar(
                    statc[:, 20:22], statc[:, 10:12], 1.0 / D, ALU.mult),
                    reads=(buf("sc100"), buf("sc101")), writes=(buf("sc20"),), small=True)
                P.op("vector", lambda v: v.tensor_sub(statc[:, 22:24], statc[:, 20:22], statc[:, 18:20]),
                     reads=(buf("sc20"), buf("sc18")), writes=(buf("sc22"),), small=True)
                P.op("scalar", lambda s: s.activation(statc[:, 24:26], statc[:, 22:24], AF.Ln, bias=cfs[:, 0:1]),
                     reads=(buf("sc22"), buf("consts")), writes=(buf("sc24"),), small=True)
                P.op("scalar", lambda s: s.activation(statc[:, 26:28], statc[:, 24:26], AF.Exp, scale=cfs[:, 1:2]),
                     reads=(buf("sc24"), buf("consts")), writes=(buf("sc26"),), small=True)
                P.op("vector", lambda v: v.tensor_tensor(
                    statc[:, 28:30], statc[:, 12:14], statc[:, 26:28], ALU.mult),
                    reads=(buf("sc120"), buf("sc121"), buf("sc26")), writes=(buf("sc28"),), small=True)
                P.op("vector", lambda v: v.tensor_single_scalar(
                    statc[:, 28:30], statc[:, 28:30], 1.0 / 7, ALU.mult),
                    reads=(buf("sc28"),), writes=(buf("sc28"),), small=True)
                if DBG2 < 3:
                    continue
                P.op("scalar", lambda s, pph=pph: s.activation(
                    u32b[:, :], v32[:, pph:pph + PRW], AF.Abs),
                    reads=(vb,), writes=(ubb,))
                P.op("vector", lambda v, pph=pph: v.tensor_scalar(
                    rs16[:, :], v32[:, pph:pph + PRW], 0.0, 2.0, ALU.is_ge, ALU.mult),
                    reads=(vb,), writes=(buf("rs16"),))
                P.op("vector", lambda v: v.tensor_single_scalar(
                    rs16[:, :], rs16[:, :], -1.0, ALU.add),
                    reads=(buf("rs16"),), writes=(buf("rs16"),))
                P.op("vector", lambda v: v.tensor_scalar(
                    bb32[:, :], u32b[:, :], 3.5, 4.0, ALU.is_ge, ALU.mult),
                    reads=(ubb,), writes=(bbb,))
                P.op("vector", lambda v: v.tensor_sub(u32b[:, :], u32b[:, :], bb32[:, :]),
                     reads=(ubb, bbb), writes=(ubb,))
                P.op("vector", lambda v: v.tensor_scalar(
                    bb32[:, :], u32b[:, :], 1.5, 2.0, ALU.is_ge, ALU.mult),
                    reads=(ubb,), writes=(bbb,))
                P.op("vector", lambda v: v.tensor_sub(u32b[:, :], u32b[:, :], bb32[:, :]),
                     reads=(ubb, bbb), writes=(ubb,))
                P.op("vector", lambda v: v.tensor_scalar(
                    bb32[:, :], u32b[:, :], 0.5, 1.0, ALU.is_ge, ALU.mult),
                    reads=(ubb,), writes=(bbb,))
                P.op("vector", lambda v: v.tensor_sub(u32b[:, :], u32b[:, :], bb32[:, :]),
                     reads=(ubb, bbb), writes=(ubb,))
                P.op("vector", lambda v: v.tensor_mul(bb32[:, :], rs16[:, :], u32b[:, :]),
                     reads=(buf("rs16"), ubb), writes=(bbb,))
                P.op("vector", lambda v, pph=pph: v.tensor_sub(
                    v32[:, pph:pph + PRW], v32[:, pph:pph + PRW], bb32[:, :]),
                    reads=(vb, bbb), writes=(vb,))
                if DBG2 < 4:
                    continue
                for ci in range(2):
                    cc = 2 * pp + ci
                    co = pph + ci * D
                    P.op("vector", lambda v, co=co, ci=ci: v.tensor_scalar_mul(
                        rs16[:, ci * D:(ci + 1) * D], v32[:, co:co + D], statc[:, 28 + ci:29 + ci]),
                        reads=(vb, buf("sc28")), writes=(buf("rs16"),))
                for f in range(FC):
                    bk = 4 + (f % 2)
                    for ci in range(2):
                        cc = 2 * pp + ci
                        P.op("tensor", lambda t, f=f, ci=ci, bk=bk: t.matmul(
                            pb[bk][:, ci * 128:(ci + 1) * 128], rs16[:, ci * D + f * 128: ci * D + (f + 1) * 128],
                            i128s[:, :], start=True, stop=True),
                            reads=(buf("rs16"), buf("consts"), PB[bk]), writes=(PB[bk],))
                    P.op("scalar", lambda s, f=f, pp=pp, bk=bk: s.copy(
                        xqT[:, f * TS + pp * 256: f * TS + (pp + 1) * 256], pb[bk][:, 0:256]),
                        reads=(PB[bk],), writes=(buf("xqT"),))
                    F()
            if not final and tile + 1 < NT:
                while xtcnt.get(("xT", tile + 1), 0) < FC and fill:
                    fill.popleft()()
                emit_xh_g((tile + 1) * 4)
            if launch == 1 and tile == 0:
                while bleft[0] > 0 and fill:
                    fill.popleft()()
            for oc in range(FC if DBG2 >= 5 else 0):
                bk = 4 + (oc % 2)

                def mmo(t, oc=oc, bk=bk):
                    last = None
                    for ic in range(FC):
                        last = t.matmul(pb[bk][:, 0:512], wqt[:, ic * D + oc * 128: ic * D + (oc + 1) * 128],
                                        xqT[:, ic * TS:(ic + 1) * TS],
                                        start=(ic == 0), stop=(ic == FC - 1))
                    return last
                P.op("tensor", mmo, reads=(buf("wqt"), buf("xqT"), PB[bk]), writes=(PB[bk],))
                F()
                if not final:
                    P.op("vector", lambda v, oc=oc, bk=bk, t0=t0: v.tensor_tensor(
                        x16[:, oc * TPC + t0: oc * TPC + t0 + TS], pb[bk][:, 0:512],
                        x16[:, oc * TPC + t0: oc * TPC + t0 + TS], ALU.add),
                        reads=(PB[bk], X16[tile]), writes=(X16[tile],))
                else:
                    i = oc % 2
                    hb = buf("sqbA") if i == 0 else buf("sqbB")
                    P.op("vector", lambda v, oc=oc, bk=bk, t0=t0, i=i: v.tensor_tensor(
                        sqb[:, i * TS:(i + 1) * TS], pb[bk][:, 0:512],
                        x16[:, oc * TPC + t0: oc * TPC + t0 + TS], ALU.add),
                        reads=(PB[bk], X16[tile], hb), writes=(hb,))
                    P.dma("gpsimd", "dout" if oc % 2 == 0 else "dout1",
                          lambda gp, oc=oc, tile=tile, i=i: gp.dma_start(
                              out=yT.ap()[oc * 128:(oc + 1) * 128, tile * TS:(tile + 1) * TS],
                              in_=sqb[:, i * TS:(i + 1) * TS]),
                          reads=(hb,))

        def ln2_block(tile):
            t0 = tile * TS

            def mmsum(t, t0=t0):
                last = None
                for f in range(FC):
                    last = t.matmul(pb[4][0:1, 0:512], ones1s[:, 0:1],
                                    x16[:, f * TPC + t0: f * TPC + t0 + TS],
                                    start=(f == 0), stop=(f == FC - 1))
                return last
            P.op("tensor", mmsum, reads=(X16[tile], buf("consts"), PB[4]), writes=(PB[4],))
            for f in range(FC):
                P.op("scalar", lambda s, f=f, t0=t0: s.activation(
                    xqT[:, f * TS:(f + 1) * TS], x16[:, f * TPC + t0: f * TPC + t0 + TS], AF.Square),
                    reads=(X16[tile],), writes=(buf("xqT"),))

            def mmsq(t):
                last = None
                for f in range(FC):
                    last = t.matmul(pb[5][0:1, 0:512], ones1s[:, 0:1],
                                    xqT[:, f * TS:(f + 1) * TS],
                                    start=(f == 0), stop=(f == FC - 1))
                return last
            P.op("tensor", mmsq, reads=(buf("xqT"), buf("consts"), PB[5]), writes=(PB[5],))
            P.op("vector", lambda v: v.tensor_scalar_mul(rowm, pb[4][0:1, 0:512], 1.0 / D),
                 reads=(PB[4],), writes=(buf("rowm"),))
            P.op("vector", lambda v: v.tensor_scalar_mul(rowq, pb[5][0:1, 0:512], 1.0 / D),
                 reads=(PB[5],), writes=(buf("rowq"),))
            P.op("tensor", lambda t: t.matmul(pb[4][:, 0:512], onesrs[0:1, :], rowm, start=True, stop=True),
                 reads=(buf("rowm"), buf("consts"), PB[4]), writes=(PB[4],))
            P.op("vector", lambda v: v.tensor_tensor(rowm, rowm, rowm, ALU.mult),
                 reads=(buf("rowm"),), writes=(buf("rowm"),), small=True)
            P.op("vector", lambda v: v.tensor_sub(rowq, rowq, rowm),
                 reads=(buf("rowq"), buf("rowm")), writes=(buf("rowq"),))
            P.op("scalar", lambda s: s.activation(rowm, rowq, AF.Ln, bias=cfs[0:1, 0:1]),
                 reads=(buf("rowq"), buf("consts")), writes=(buf("rowm"),))
            P.op("scalar", lambda s: s.activation(rowq, rowm, AF.Exp, scale=cfs[0:1, 1:2]),
                 reads=(buf("rowm"), buf("consts")), writes=(buf("rowq"),))
            P.op("tensor", lambda t: t.matmul(pb[5][:, 0:512], onesrs[0:1, :], rowq, start=True, stop=True),
                 reads=(buf("rowq"), buf("consts"), PB[5]), writes=(PB[5],))
            P.op("vector", lambda v: v.tensor_copy(r2b32[:, :], pb[5][:, 0:512]),
                 reads=(PB[5],), writes=(buf("r2b32"),))
            P.op("vector", lambda v: v.tensor_copy(aux32[:, :], pb[4][:, 0:512]),
                 reads=(PB[4],), writes=(buf("aux32"),))
            for f in range(FC):
                eng = "vector" if f < 5 else "gpsimd"
                tmp = tmpv if f < 5 else tmpg
                tb = buf("tmpv") if f < 5 else buf("tmpg")
                P.op(eng, lambda e2, f=f, t0=t0, tmp=tmp: e2.tensor_sub(
                    tmp[:, :], x16[:, f * TPC + t0: f * TPC + t0 + TS], aux32[:, :]),
                    reads=(X16[tile], buf("aux32")), writes=(tb,))
                P.op(eng, lambda e2, f=f, t0=t0, tmp=tmp: e2.tensor_mul(
                    ln2[:, f * TPC + t0: f * TPC + t0 + TS], tmp[:, :], r2b32[:, :]),
                    reads=(tb, buf("r2b32")), writes=(LN2[tile],))

        def bnd_out(srct, outt, sbuf_, f, t, qi, eng="scalar"):
            hb = buf("sqbA") if qi == 0 else buf("sqbB")

            def fn(e2, f=f, t=t, qi=qi, srct=srct):
                src = srct[:, f * TPC + t * TS: f * TPC + (t + 1) * TS]
                dst = sqb[:, qi * 512:(qi + 1) * 512]
                return e2.copy(dst, src) if eng == "scalar" else e2.tensor_copy(dst, src)
            P.op(eng, fn, reads=(sbuf_, hb), writes=(hb,))
            P.dma("gpsimd", "dout" if qi % 2 == 0 else "dout1",
                  lambda gp, f=f, t=t, qi=qi, outt=outt: gp.dma_start(
                      out=outt.ap()[f * 128:(f + 1) * 128, t * TS:(t + 1) * TS],
                      in_=sqb[:, qi * 512:(qi + 1) * 512]),
                  reads=(hb,))

        def kv_jobs(tile):
            jobs = []
            for cc in range(4):
                gch = tile * 4 + cc

                def ja(gch=gch, tile=tile):
                    for half in range(2):
                        def mmk(t, gch=gch, half=half):
                            last = None
                            for kc in range(FC):
                                last = t.matmul(pb[2 + half][:, 0:512],
                                                ln2[:, kc * TPC + gch * 128: kc * TPC + (gch + 1) * 128],
                                                wS3[:, kc * D + half * 512: kc * D + half * 512 + 512],
                                                start=(kc == 0), stop=(kc == FC - 1))
                            return last
                        P.op("tensor", mmk, reads=(LN2[tile], buf("wS3"), PB[2 + half]), writes=(PB[2 + half],))
                    P.op("scalar", lambda s: s.activation(k16[:, 0:512], pb[2][:, 0:512], AF.Sigmoid),
                         reads=(PB[2],), writes=(buf("k16"),))
                    P.op("scalar", lambda s: s.activation(k16[:, 512:1024], pb[3][:, 0:512], AF.Sigmoid),
                         reads=(PB[3],), writes=(buf("k16"),))

                def jb(gch=gch, tile=tile):
                    for half in range(2):
                        def mmv(t, gch=gch, half=half):
                            last = None
                            for kc in range(FC):
                                last = t.matmul(pb[2 + half][:, 0:512],
                                                ln2[:, kc * TPC + gch * 128: kc * TPC + (gch + 1) * 128],
                                                wS4[:, kc * D + half * 512: kc * D + half * 512 + 512],
                                                start=(kc == 0), stop=(kc == FC - 1))
                            return last
                        P.op("tensor", mmv, reads=(LN2[tile], buf("wS4"), PB[2 + half]), writes=(PB[2 + half],))
                    P.op("scalar", lambda s: s.copy(v16[:, 0:512], pb[2][:, 0:512]),
                         reads=(PB[2],), writes=(buf("v16"),))
                    P.op("scalar", lambda s: s.copy(v16[:, 512:1024], pb[3][:, 0:512]),
                         reads=(PB[3],), writes=(buf("v16"),))

                def jc(gch=gch):
                    def mmkv(t, gch=gch):
                        last = None
                        for p_ in range(FC):
                            bank = pb[6] if p_ < 4 else pb[7]
                            colo = (p_ % 4) * 128
                            last = t.matmul(bank[:, colo:colo + 128],
                                            k16[:, p_ * 128:(p_ + 1) * 128],
                                            v16[:, p_ * 128:(p_ + 1) * 128],
                                            start=(gch == 0 and (p_ % 4) == 0),
                                            stop=(gch == NCH - 1 and (p_ % 4) == 3))
                        return last
                    P.op("tensor", mmkv, reads=(buf("k16"), buf("v16"), PB[6], PB[7]),
                         writes=(PB[6], PB[7]))

                    def mmks(t):
                        last = None
                        for p_ in range(FC):
                            last = t.matmul(pb[2][:, p_:p_ + 1], k16[:, p_ * 128:(p_ + 1) * 128],
                                            ones1s[:, 0:1], start=(p_ == 0), stop=(p_ == FC - 1))
                        return last
                    P.op("tensor", mmks, reads=(buf("k16"), buf("consts"), PB[2]), writes=(PB[2],))
                    sks, dks = (gch % 2) * 8, ((gch + 1) % 2) * 8
                    if gch == 0:
                        P.op("vector", lambda v, dks=dks: v.tensor_copy(ksum[:, dks:dks + 8], pb[2][:, 0:8]),
                             reads=(PB[2],), writes=(buf("ksum"),), small=True)
                    else:
                        P.op("vector", lambda v, sks=sks, dks=dks: v.tensor_tensor(
                            ksum[:, dks:dks + 8], pb[2][:, 0:8], ksum[:, sks:sks + 8], ALU.add),
                            reads=(PB[2], buf("ksum")), writes=(buf("ksum"),), small=True)
                jobs += [ja, jb, jc]
            return jobs

        if launch == 1:
            P.dma("sync", "din", lambda s: s.dma_start(out=h16[:, :], in_=g["h16"].ap()),
                  writes=(buf("h16"),))
            for j in piece_jobs("xT", x16, X16, (0,), "scalar"):
                j()
            jobs = deque()
            wkj = wpiece_jobs("wk_l", wS3, buf("wS3"), "scalar")
            if DBG >= 2:
                bj = mk_build()
                for i, b in enumerate(bj):
                    jobs.append(b)
                    jobs.extend(wkj[2 * i: 2 * i + 2])
                jobs.extend(wkj[16:])
            else:
                bleft[0] = 0
                jobs.extend(wkj)
            jobs.extend(piece_jobs("xT", x16, X16, (1,), "scalar"))
            jobs.extend(wpiece_jobs("wv_l", wS4, buf("wS4"), "scalar"))
            jobs.extend(piece_jobs("xT", x16, X16, (2,), "scalar"))
            jobs.extend(piece_jobs("xT", x16, X16, (3,), "scalar"))
            if DBG >= 3:
                qrr = [0]
                for tile in range(NT):
                    while xtcnt.get(("xT", tile), 0) < FC:
                        jobs.popleft()()
                    ffn_phase(tile, final=False, fill=jobs)
                    if DBG >= 4:
                        ln2_block(tile)
                        if DBG >= 5:
                            jobs.extend(kv_jobs(tile))
                        for f in range(FC):
                            bnd_out(x16, x1_out, X16[tile], f, tile, qrr[0] % 2)
                            qrr[0] += 1
                            if tile == NT - 1 and jobs:
                                jobs.popleft()()
                            bnd_out(ln2, ln2_out, LN2[tile], f, tile, qrr[0] % 2, "vector")
                            qrr[0] += 1
                            if tile == NT - 1 and jobs:
                                jobs.popleft()()
                while jobs:
                    jobs.popleft()()

        if launch == 1:
            for p_ in range(FC if DBG >= 5 else 0):
                bank = pb[6] if p_ < 4 else pb[7]
                pbuf = PB[6] if p_ < 4 else PB[7]
                colo = (p_ % 4) * 128
                rd = (pbuf, SH[0], SH[1]) if p_ == 0 else (pbuf,)
                P.op("vector", lambda v, p_=p_, bank=bank, colo=colo: v.tensor_copy(
                    kvsb[0:64, p_ * 64:(p_ + 1) * 64], bank[0:64, colo:colo + 64]),
                    reads=rd, writes=(buf("kvsb"), SH[0], SH[1]))
                P.op("vector", lambda v, p_=p_, bank=bank, colo=colo: v.tensor_copy(
                    kvsb[64:128, p_ * 64:(p_ + 1) * 64], bank[64:128, colo + 64:colo + 128]),
                    reads=(pbuf,), writes=(buf("kvsb"), SH[0], SH[1]))
            if DBG >= 5:
                P.op("vector", lambda v: v.tensor_copy(kvsb[:, 512:520], ksum[:, 0:8]),
                     reads=(buf("ksum"),), writes=(buf("kvsb"), SH[0], SH[1]))
                P.dma("gpsimd", "dout", lambda gp: gp.dma_start(out=kv_out.ap(), in_=kvsb[:, :]),
                      reads=(buf("kvsb"),))

        def bjobs(tile):
            t0 = tile * TS
            QB = buf("qt")
            jobs = []

            def jq(oc):
                bk = 2 + (oc % 2)

                def mmq(t, oc=oc, t0=t0, bk=bk):
                    last = None
                    for kc in range(FC):
                        last = t.matmul(pb[bk][:, 0:512],
                                        wS3[:, kc * D + oc * 128: kc * D + (oc + 1) * 128],
                                        ln2[:, kc * TPC + t0: kc * TPC + t0 + TS],
                                        start=(kc == 0), stop=(kc == FC - 1))
                    return last
                P.op("tensor", mmq, reads=(buf("wS3"), LN2[tile], PB[bk]), writes=(PB[bk],))
                P.op("scalar", lambda s, oc=oc, bk=bk: s.activation(
                    qt[:, oc * TS:(oc + 1) * TS], pb[bk][:, 0:512], AF.Sigmoid),
                    reads=(PB[bk],), writes=(QB,))
            for oc in range(FC):
                jobs.append(lambda oc=oc: jq(oc))

            def jden():
                def mmden(t):
                    last = None
                    for p_ in range(FC):
                        last = t.matmul(pb[2][0:16, 0:512], dbd[:, p_ * 16:(p_ + 1) * 16],
                                        qt[:, p_ * TS:(p_ + 1) * TS],
                                        start=(p_ == 0), stop=(p_ == FC - 1))
                    return last
                P.op("tensor", mmden, reads=(buf("dbd"), QB, PB[2]), writes=(PB[2],))
                P.op("vector", lambda v: v.tensor_scalar_add(rdrow[:, :], pb[2][0:16, 0:512], 1e-8),
                     reads=(PB[2],), writes=(buf("aux32"),))
                P.op("vector", lambda v: v.reciprocal(rdr2[0:16, :], rdrow[:, :]),
                     reads=(buf("aux32"),), writes=(buf("rdr2"),))
            jobs.append(jden)

            def jnum(p_):
                P.op("tensor", lambda t, p_=p_: t.matmul(
                    pb[2][:, 0:512], kvbd[:, p_ * 128:(p_ + 1) * 128],
                    qt[:, p_ * TS:(p_ + 1) * TS], start=True, stop=True),
                    reads=(buf("kvbd"), QB, PB[2]), writes=(PB[2],))
                P.op("tensor", lambda t, p_=p_: t.matmul(
                    pb[3][:, 0:512], sel16s[:, p_ * 128:(p_ + 1) * 128], rdr2[0:16, :],
                    start=True, stop=True),
                    reads=(buf("rdr2"), buf("consts"), PB[3]), writes=(PB[3],))
                P.op("vector", lambda v: v.tensor_copy(rdb32[:, :], pb[3][:, 0:512]),
                     reads=(PB[3], buf("kvsb")), writes=(buf("rdb32"),))
                P.op("vector", lambda v, p_=p_: v.tensor_mul(
                    ao16[:, p_ * TS:(p_ + 1) * TS], pb[2][:, 0:512], rdb32[:, :]),
                    reads=(PB[2], buf("rdb32")), writes=(buf("ao16"),))
            for p_ in range(FC):
                jobs.append(lambda p_=p_: jnum(p_))

            def jfc(oc):
                bk = 6 + (oc % 2)

                def mmfc(t, oc=oc, bk=bk):
                    last = None
                    for hc in range(FC):
                        last = t.matmul(pb[bk][:, 0:512],
                                        wS4[:, hc * D + oc * 128: hc * D + (oc + 1) * 128],
                                        ao16[:, hc * TS:(hc + 1) * TS],
                                        start=(hc == 0), stop=(hc == FC - 1))
                    return last
                P.op("tensor", mmfc, reads=(buf("wS4"), buf("ao16"), PB[bk]), writes=(PB[bk],))
                P.op("vector", lambda v, oc=oc, t0=t0, bk=bk: v.tensor_tensor(
                    qt[:, oc * TS:(oc + 1) * TS], pb[bk][:, 0:512],
                    ln2[:, oc * TPC + t0: oc * TPC + t0 + TS], ALU.add),
                    reads=(PB[bk], LN2[tile]), writes=(QB,))
            for oc in range(FC):
                jobs.append(lambda oc=oc: jfc(oc))

            def jstat():
                def mmsy(t):
                    last = None
                    for f in range(FC):
                        last = t.matmul(pb[2][0:1, 0:512], ones1s[:, 0:1],
                                        qt[:, f * TS:(f + 1) * TS],
                                        start=(f == 0), stop=(f == FC - 1))
                    return last
                P.op("tensor", mmsy, reads=(QB, buf("consts"), PB[2]), writes=(PB[2],))
                for f in range(FC):
                    P.op("scalar", lambda s, f=f: s.activation(
                        ao16[:, f * TS:(f + 1) * TS], qt[:, f * TS:(f + 1) * TS], AF.Square),
                        reads=(QB,), writes=(buf("ao16"),))

                def mmsy2(t):
                    last = None
                    for f in range(FC):
                        last = t.matmul(pb[3][0:1, 0:512], ones1s[:, 0:1],
                                        ao16[:, f * TS:(f + 1) * TS],
                                        start=(f == 0), stop=(f == FC - 1))
                    return last
                P.op("tensor", mmsy2, reads=(buf("ao16"), buf("consts"), PB[3]), writes=(PB[3],))
                P.op("vector", lambda v: v.tensor_scalar_mul(rowm, pb[2][0:1, 0:512], 1.0 / D),
                     reads=(PB[2],), writes=(buf("rowm"),))
                P.op("vector", lambda v: v.tensor_scalar_mul(rowq, pb[3][0:1, 0:512], 1.0 / D),
                     reads=(PB[3],), writes=(buf("rowq"),))
                P.op("tensor", lambda t: t.matmul(pb[2][:, 0:512], onesrs[0:1, :], rowm, start=True, stop=True),
                     reads=(buf("rowm"), buf("consts"), PB[2]), writes=(PB[2],))
                P.op("vector", lambda v: v.tensor_tensor(rowm, rowm, rowm, ALU.mult),
                     reads=(buf("rowm"),), writes=(buf("rowm"),), small=True)
                P.op("vector", lambda v: v.tensor_sub(rowq, rowq, rowm),
                     reads=(buf("rowq"), buf("rowm")), writes=(buf("rowq"),))
                P.op("scalar", lambda s: s.activation(rowm, rowq, AF.Ln, bias=cfs[0:1, 0:1]),
                     reads=(buf("rowq"), buf("consts")), writes=(buf("rowm"),))
                P.op("scalar", lambda s: s.activation(rowq, rowm, AF.Exp, scale=cfs[0:1, 1:2]),
                     reads=(buf("rowm"), buf("consts")), writes=(buf("rowq"),))
                P.op("tensor", lambda t: t.matmul(pb[3][:, 0:512], onesrs[0:1, :], rowq, start=True, stop=True),
                     reads=(buf("rowq"), buf("consts"), PB[3]), writes=(PB[3],))
                P.op("vector", lambda v: v.tensor_copy(r2b32[:, :], pb[3][:, 0:512]),
                     reads=(PB[3],), writes=(buf("r2b32"),))
                P.op("vector", lambda v: v.tensor_copy(mbf[:, :], pb[2][:, 0:512]),
                     reads=(PB[2],), writes=(buf("aux32"),))
            jobs.append(jstat)

            def jret(fpair):
                for f in (2 * fpair, 2 * fpair + 1):
                    eng = "vector" if f < 5 else "gpsimd"
                    tmp = tmpv if f < 5 else tmpg
                    tb = buf("tmpv") if f < 5 else buf("tmpg")
                    P.op(eng, lambda e2, f=f, tmp=tmp: e2.tensor_sub(
                        tmp[:, :], qt[:, f * TS:(f + 1) * TS], mbf[:, :]),
                        reads=(QB, buf("aux32")), writes=(tb,))
                    P.op(eng, lambda e2, tmp=tmp: e2.tensor_mul(
                        tmp[:, :], tmp[:, :], r2b32[:, :]),
                        reads=(tb, buf("r2b32")), writes=(tb,))
                    P.op(eng, lambda e2, f=f, tmp=tmp: e2.tensor_scalar(
                        tmp[:, :], tmp[:, :], gbs[:, f:f + 1], gbs[:, 8 + f:9 + f], ALU.mult, ALU.add),
                        reads=(tb, buf("consts")), writes=(tb,))
                    P.op(eng, lambda e2, f=f, t0=t0, tmp=tmp: e2.tensor_tensor(
                        x16[:, f * TPC + t0: f * TPC + t0 + TS], tmp[:, :],
                        x16[:, f * TPC + t0: f * TPC + t0 + TS], ALU.add),
                        reads=(tb, X16[tile]), writes=(X16[tile],))
            for fp in range(4):
                jobs.append(lambda fp=fp: jret(fp))
            return jobs

        if launch == 2:
            for j in piece_jobs("ln2f", ln2, LN2, (0,), "scalar"):
                j()
            for j in wpiece_jobs("wq_l", wS3, buf("wS3")):
                j()
            kv_ingest()
            for j in piece_jobs("x1f", x16, X16, (0,), "gpsimd"):
                j()
            P.dma("sync", "din", lambda s: s.dma_start(out=h16[:, :], in_=g["h16"].ap()),
                  writes=(buf("h16"),))
            for j in wpiece_jobs("wfc_l", wS4, buf("wS4")):
                j()
            for j in bjobs(0):
                j()
            for j in wpiece_jobs("wqt_f", wqt, buf("wqt")):
                j()
            jobs2 = deque()
            for tile in range(NT):
                if tile + 1 < NT:
                    jobs2.extend(piece_jobs("ln2f", ln2, LN2, (tile + 1,), "scalar"))
                    bj = bjobs(tile + 1)
                    jobs2.extend(bj[:8])
                    jobs2.extend(piece_jobs("x1f", x16, X16, (tile + 1,), "scalar"))
                    jobs2.extend(bj[8:])
                ffn_phase(tile, final=True, fill=jobs2)
                while jobs2:
                    jobs2.popleft()()

        P.wait("gpsimd", ("dout", P.cnt["dout"], False))
        if P.cnt["dout1"] > 0:
            P.wait("gpsimd", ("dout1", P.cnt["dout1"], False))

        with nc.Block() as block:
            @block.tensor
            def _(t):
                P.emit(t, "tensor", sems)

            @block.vector
            def _(v):
                P.emit(v, "vector", sems)

            @block.scalar
            def _(s):
                P.emit(s, "scalar", sems)

            @block.gpsimd
            def _(gp):
                P.emit(gp, "gpsimd", sems)

            @block.sync
            def _(s):
                P.emit(s, "sync", sems)

    return nc


def kernel(L0_attn_inputs, L0_attn_W_Q, L0_attn_W_K, L0_attn_W_V, L0_attn_W_fc,
           L0_attn_gamma, L0_attn_beta, L0_ffn_weight):
    x = np.asarray(L0_attn_inputs)[0]
    xT = np.ascontiguousarray(x.T)
    h16 = _hadamard_bf16()
    i128 = np.eye(128, dtype=ml_dtypes.bfloat16)
    ones1 = np.ones((128, 1), dtype=ml_dtypes.bfloat16)
    onesrow = np.ones((1, 128), dtype=np.float32)
    sel16 = _sel16_f32()
    gb = np.concatenate([
        np.asarray(L0_attn_gamma).reshape(8, 128).T,
        np.asarray(L0_attn_beta).reshape(8, 128).T], axis=1).astype(np.float32)
    gb = np.ascontiguousarray(gb)
    constf = np.zeros((128, 8), np.float32)
    constf[:, 0] = EPS
    constf[:, 1] = -0.5

    bust1 = 16 + (VERSION * 7) % 48
    bust2 = 17 + (VERSION * 5) % 48
    common = {"h16": h16, "i128": i128, "ones1": ones1, "onesrow": onesrow,
              "constf": constf}

    nc1 = build_nc(bust1, 1)
    wffn = np.ascontiguousarray(np.asarray(L0_ffn_weight, dtype=np.float32))
    base1 = dict(common)
    base1.update({
        "wk_l": np.ascontiguousarray(np.asarray(L0_attn_W_K, dtype=np.float32)),
        "wv_l": np.ascontiguousarray(np.asarray(L0_attn_W_V, dtype=np.float32)),
        "wffn_l": wffn,
        "bust": np.zeros((1, bust1), np.float32),
    })
    in_maps1 = []
    for c in range(NCORE):
        m = dict(base1)
        m["xT"] = np.ascontiguousarray(xT[:, c * TPC:(c + 1) * TPC])
        in_maps1.append(m)
    res1 = run_bass_kernel_spmd(nc1, in_maps1, core_ids=list(range(NCORE)))
    kvparts = np.concatenate(
        [np.asarray(res1.results[c]["kv_out"]) for c in range(NCORE)], axis=0)

    nc2 = build_nc(bust2, 2)
    base2 = dict(common)
    base2.update({
        "wq_l": np.ascontiguousarray(np.asarray(L0_attn_W_Q, dtype=np.float32)),
        "wfc_l": np.ascontiguousarray(np.asarray(L0_attn_W_fc, dtype=np.float32)),
        "gb": gb, "sel16": sel16,
        "wqt_f": np.ascontiguousarray(np.asarray(res1.results[0]["wqt_out"], dtype=np.float32)),
        "kvparts": np.ascontiguousarray(kvparts.astype(np.float32)),
        "bust": np.zeros((1, bust2), np.float32),
    })
    in_maps2 = []
    for c in range(NCORE):
        m = dict(base2)
        m["x1f"] = np.asarray(res1.results[c]["x1_out"])
        m["ln2f"] = np.asarray(res1.results[c]["ln2_out"])
        in_maps2.append(m)
    res2 = run_bass_kernel_spmd(nc2, in_maps2, core_ids=list(range(NCORE)))
    out = np.concatenate([r["yT"] for r in res2.results], axis=1)
    return np.ascontiguousarray(out.T).reshape(1, 16384, 1024).astype(np.float32)


# version 21
# speedup vs baseline: 1.1913x; 1.0052x over previous
VERSION = 23

import os
import numpy as np
import ml_dtypes
import concourse.bass as bass
import concourse.mybir as mybir
from concourse.bass_utils import run_bass_kernel_spmd
from contextlib import ExitStack
from collections import deque

F32 = mybir.dt.float32
BF16 = mybir.dt.bfloat16
AF = mybir.ActivationFunctionType
ALU = mybir.AluOpType
AX = mybir.AxisListType

D = 1024
NCORE = 8
TPC = 2048
NT = 4
TS = 512
NCH = 16
FC = 8
NH = 16
EPS = 1e-5
DBG = int(os.environ.get("KDBG", "9"))
DBG2 = int(os.environ.get("KDBG2", "9"))
FULLSYNC = os.environ.get("BASS_SIM_FP64") == "1"


def _hadamard_bf16():
    h = np.array([[1.0]], dtype=np.float64)
    while h.shape[0] < D:
        h = np.block([[h, h], [h, -h]])
    h = (h / 32.0).astype(np.float32)
    hp = h.reshape(FC, 128, D).transpose(1, 0, 2).reshape(128, FC * D)
    return hp.astype(ml_dtypes.bfloat16)


def _sel16_f32():
    s = np.zeros((16, FC * 128), np.float32)
    for p in range(FC):
        s[2 * p, p * 128: p * 128 + 64] = 1.0
        s[2 * p + 1, p * 128 + 64: p * 128 + 128] = 1.0
    return s.astype(ml_dtypes.bfloat16)


class Buf:
    __slots__ = ("name", "writer", "readers")

    def __init__(self, name):
        self.name = name
        self.writer = None
        self.readers = []


class Prog:
    ENGS = ("tensor", "vector", "scalar", "gpsimd", "sync")
    DMAQ = ("din", "din1", "dout", "dout1", "dcc")

    def __init__(self):
        self.ops = {e: [] for e in self.ENGS}
        self.cnt = {e: 0 for e in list(self.ENGS) + list(self.DMAQ)}
        self.hw = {e: {} for e in self.ENGS}
        self.selfhw = {e: 0 for e in self.ENGS}

    def _collect(self, eng, reads, writes):
        waits = {}

        def need(tok):
            if tok is None:
                return
            te, tv, sm = tok
            if te == eng and not FULLSYNC:
                if not (sm and (self.cnt[eng] - tv) < 3):
                    return
            if waits.get(te, -1) < tv:
                waits[te] = tv

        for b in reads:
            need(b.writer)
        for b in writes:
            need(b.writer)
            for t2 in b.readers:
                need(t2)
        out = []
        for e2, v in sorted(waits.items()):
            if e2 == eng:
                if v > self.selfhw[eng]:
                    out.append((e2, v))
                    self.selfhw[eng] = v
            elif self.hw[eng].get(e2, 0) < v:
                out.append((e2, v))
                self.hw[eng][e2] = v
        return out

    def op(self, eng, fn, reads=(), writes=(), small=False, after=()):
        waits = self._collect(eng, reads, writes)
        for tok in after:
            te, tv, _sm = tok
            if te == eng:
                continue
            if self.hw[eng].get(te, 0) < tv:
                waits.append((te, tv))
                self.hw[eng][te] = tv
        self.cnt[eng] += 1
        tok = (eng, self.cnt[eng], small)
        self.ops[eng].append((waits, fn, eng, 1))
        for b in writes:
            b.writer = tok
            b.readers = []
        for b in reads:
            b.readers.append(tok)
        return tok

    def dma(self, eng, q, fn, reads=(), writes=()):
        waits = self._collect(eng, reads, writes)
        if self.cnt[q] > 0 and self.hw[eng].get(q, 0) < self.cnt[q]:
            waits.append((q, self.cnt[q]))
            self.hw[eng][q] = self.cnt[q]
        self.cnt[q] += 16
        tok = (q, self.cnt[q], False)
        self.ops[eng].append((waits, fn, q, 16))
        for b in writes:
            b.writer = tok
            b.readers = []
        for b in reads:
            b.readers.append(tok)
        return tok

    def wait(self, eng, tok):
        te, tv, _ = tok
        if te == eng:
            if tv > self.selfhw[eng]:
                self.selfhw[eng] = tv
                self.ops[eng].append(([(te, tv)], None, None, 0))
            return
        if self.hw[eng].get(te, 0) < tv:
            self.hw[eng][te] = tv
            self.ops[eng].append(([(te, tv)], None, None, 0))

    def emit(self, engobj, eng, sems):
        for waits, fn, incsem, incval in self.ops[eng]:
            for (se, v) in waits:
                engobj.wait_ge(sems[se], v)
            if fn is None:
                continue
            inst = fn(engobj)
            if incval:
                inst.then_inc(sems[incsem], incval)


def build_nc(bust_len, launch):
    nc = bass.Bass("TRN2", target_bir_lowering=False)
    g = {}

    def din(name, shape, dt=F32):
        g[name] = nc.dram_tensor(name, shape, dt, kind="ExternalInput")

    din("h16", [128, FC * D], BF16)
    din("i128", [128, 128], BF16)
    din("ones1", [128, 1], BF16)
    din("onesrow", [1, 128])
    din("constf", [128, 8])
    din("bust", [1, bust_len])
    if launch == 1:
        din("xT", [D, TPC])
        din("wk_l", [D, D]); din("wv_l", [D, D]); din("wffn_l", [D, D])
        kv_out = nc.dram_tensor("kv_out", [128, 520], F32, kind="ExternalOutput")
        x1_out = nc.dram_tensor("x1_out", [D, TPC], F32, kind="ExternalOutput")
        ln2_out = nc.dram_tensor("ln2_out", [D, TPC], F32, kind="ExternalOutput")
        wqt_out = nc.dram_tensor("wqt_out", [128, FC * D], F32, kind="ExternalOutput")
    else:
        din("wq_l", [D, D]); din("wfc_l", [D, D])
        din("wqt_f", [128, FC * D])
        din("gb", [128, 16])
        din("sel16", [16, FC * 128], BF16)
        din("x1f", [D, TPC])
        din("ln2f", [D, TPC])
        din("kvparts", [8 * 128, 520])
        yT = nc.dram_tensor("yT", [D, TPC], F32, kind="ExternalOutput")

    P = Prog()
    with ExitStack() as ctx:
        e = ctx.enter_context
        ctx.enter_context(nc.allow_low_precision(
            reason="bf16 residual stream / operand casts are deliberate; "
                   "matmuls accumulate fp32 in PSUM"))

        def sb(name, shape, dt):
            return e(nc.sbuf_tensor(name, shape, dt))

        x16 = sb("x16", [128, FC * TPC], BF16)
        ln2 = sb("ln2s", [128, FC * TPC], BF16)
        h16 = sb("h16s", [128, FC * D], BF16)
        wqt = sb("wqt", [128, FC * D], BF16)
        wS3 = sb("wS3", [128, FC * D], BF16)
        wS4 = sb("wS4", [128, FC * D], BF16)
        SW = 2048 if launch == 1 else 1024
        stage = sb("stage", [128, SW], F32)
        LW = 2 if launch == 1 else 1
        PRW = 2 * D
        u32b = sb("u32b", [128, PRW], BF16)
        bb32 = sb("bb32", [128, PRW], BF16)
        v32 = sb("v32", [128, LW * PRW], BF16)
        sqb = sb("sqb", [128, 1040], F32)
        rs16 = sb("rs16", [128, PRW], BF16)
        xqT = sb("xqT", [128, FC * TS], BF16)
        if launch == 2:
            qt = sb("qt", [128, FC * TS], BF16)
            ao16 = sb("ao16", [128, FC * TS], BF16)
        if launch == 1:
            k16 = sb("k16", [128, D], BF16)
            v16 = sb("v16", [128, D], BF16)
            wbld = sb("wbld", [128, D], F32)
        statc = sb("statc", [128, 40], F32)
        ksum = sb("ksum", [128, 16], F32) if launch == 1 else None
        r2b32 = sb("r2b32", [128, TS], F32 if launch == 1 else BF16)
        aux32 = sb("aux32", [128, TS], F32)
        tmpv = sb("tmpv", [128, TS], BF16)
        tmpg = sb("tmpg", [128, TS], BF16)
        if launch == 2:
            rdrow = aux32[0:16, :]
            mbf = aux32
        kvsb = sb("kvsb", [128, 520], F32) if launch == 2 else stage[:, 0:520]
        rdb32 = kvsb[:, 0:512] if launch == 2 else None
        if launch == 2:
            kvbd = sb("kvbd", [128, FC * 128], BF16)
            dbd = sb("dbd", [128, FC * 16], BF16)
            rdr2 = sb("rdr2", [16, TS], BF16)
        rowmt = sb("rowmt", [1, TS], F32)
        rowqt = sb("rowqt", [1, TS], F32)
        rowm = rowmt[0:1, :]
        rowq = rowqt[0:1, :]
        i128s = sb("i128s", [128, 128], BF16)
        ones1s = sb("ones1s", [128, 1], BF16)
        onesrs = sb("onesrs", [1, 128], F32)
        if launch == 2:
            sel16s = sb("sel16s", [16, FC * 128], BF16)
            gbs = sb("gbs", [128, 16], F32)
        cfs = sb("cfs", [128, 8], F32)
        busts = sb("busts", [1, bust_len], F32)

        pb = [e(nc.psum_tensor(f"pb{i}", [128, 512], F32)) for i in range(8)]

        semnames = list(Prog.ENGS) + list(Prog.DMAQ)
        sems = {n: e(nc.semaphore("sem_" + n)) for n in semnames}

        B = {}

        def buf(nm):
            if nm not in B:
                B[nm] = Buf(nm)
            return B[nm]

        PB = [buf(f"pb{i}") for i in range(8)]
        X16 = [buf(f"x16_{t}") for t in range(NT)]
        LN2 = [buf(f"ln2_{t}") for t in range(NT)]
        SH = [buf("stage_0"), buf("stage_1"), buf("stage_2"), buf("stage_3")]
        _ldpc = [0]
        xtcnt = {}

        def piece_jobs(srcn, dstt, tilebufs, tiles, cpeng):
            ar = g[srcn].ap().rearrange("(f p) t -> f p t", p=128)
            pieces = [(t, f) for t in tiles for f in range(FC)]
            qs = []

            def issue(k):
                t, f = pieces[k]
                i = _ldpc[0] % 2
                _ldpc[0] += 1
                qs.append(i)
                P.dma("sync", "din" if i == 0 else "din1",
                      lambda s, f=f, t=t, i=i, ar=ar: s.dma_start(
                          out=stage[:, i * 512:(i + 1) * 512],
                          in_=ar[f][:, t * TS:(t + 1) * TS]),
                      reads=(SH[i],), writes=(SH[i],))

            def copy(k):
                t, f = pieces[k]
                i = qs[k]

                def fn(eng, f=f, t=t, i=i):
                    src = stage[:, i * 512:(i + 1) * 512]
                    dst = dstt[:, f * TPC + t * TS: f * TPC + (t + 1) * TS]
                    return eng.copy(dst, src) if cpeng == "scalar" \
                        else eng.tensor_copy(dst, src)
                P.op(cpeng, fn, reads=(SH[i],), writes=(tilebufs[t],))
                xtcnt[(srcn, t)] = xtcnt.get((srcn, t), 0) + 1

            n = len(pieces)
            jobs = [lambda: issue(0)]
            for k in range(n):
                def jk(k=k):
                    if k + 1 < n:
                        issue(k + 1)
                    copy(k)
                jobs.append(jk)
            for j in jobs:
                j.light = True
            return jobs

        def wpiece_jobs(srcn, slot, slotbuf, cpeng="gpsimd"):
            if srcn == "wqt_f":
                ar = None
            else:
                ar = g[srcn].ap().rearrange("(c p) o -> c p o", p=128)
            qs = []

            def issue(k):
                i = _ldpc[0] % 2
                _ldpc[0] += 1
                qs.append(i)
                if ar is None:
                    P.dma("sync", "din" if i == 0 else "din1",
                          lambda s, k=k, i=i: s.dma_start(
                              out=stage[:, i * 512:(i + 1) * 512],
                              in_=g["wqt_f"].ap()[:, k * 512:(k + 1) * 512]),
                          reads=(SH[i],), writes=(SH[i],))
                else:
                    c, h = k // 2, k % 2
                    P.dma("sync", "din" if i == 0 else "din1",
                          lambda s, c=c, h=h, i=i, ar=ar: s.dma_start(
                              out=stage[:, i * 512:(i + 1) * 512],
                              in_=ar[c][:, h * 512:(h + 1) * 512]),
                          reads=(SH[i],), writes=(SH[i],))

            def copy(k):
                i = qs[k]
                c, h = k // 2, k % 2
                base = k * 512 if ar is None else c * D + h * 512

                def fn(eng, base=base, i=i):
                    src = stage[:, i * 512:(i + 1) * 512]
                    return eng.copy(slot[:, base:base + 512], src) if cpeng == "scalar"                         else eng.tensor_copy(slot[:, base:base + 512], src)
                P.op(cpeng, fn, reads=(SH[i],), writes=(slotbuf,))

            jobs = [lambda: issue(0)]
            for k in range(16):
                def jk(k=k):
                    if k + 1 < 16:
                        issue(k + 1)
                    copy(k)
                jobs.append(jk)
            for j in jobs:
                j.light = True
            return jobs

        constlist = [(i128s, "i128"), (ones1s, "ones1"), (onesrs, "onesrow"),
                     (cfs, "constf"), (busts, "bust")]
        if launch == 2:
            constlist += [(sel16s, "sel16"), (gbs, "gb")]
        for (dst, src) in constlist:
            P.dma("sync", "din", lambda s, d=dst, so=src: s.dma_start(out=d[:, :], in_=g[so].ap()),
                  writes=(buf("consts"),))

        bleft = [FC]

        def mk_build():
            wr = g["wffn_l"].ap().rearrange("(c p) i -> c p i", p=128)
            wqtr = wqt_out.ap().rearrange("p (i o c) -> o i p c", i=8, o=8)
            WH = (SH[2], SH[3])

            def bdma(oc):
                P.dma("sync", "din1", lambda s, oc=oc: s.dma_start(
                    out=stage[:, 1024:2048], in_=wr[oc]), reads=WH, writes=WH)

            def bjob(oc):
                P.op("vector", lambda v: v.tensor_reduce(
                    statc[:, 0:1], stage[:, 1024:2048], AX.X, ALU.add,
                    apply_absolute_value=True),
                    reads=WH, writes=(buf("sc0"),), small=True)
                P.op("vector", lambda v: v.tensor_scalar(
                    statc[:, 1:2], statc[:, 0:1], 1.0 / D, 1e-5, ALU.mult, ALU.max),
                    reads=(buf("sc0"),), writes=(buf("sc1"),), small=True)
                P.op("vector", lambda v: v.reciprocal(statc[:, 2:3], statc[:, 1:2]),
                     reads=(buf("sc1"),), writes=(buf("sc2"),), small=True)
                P.op("vector", lambda v: v.tensor_single_scalar(
                    statc[:, 3:4], statc[:, 1:2], 0.5, ALU.mult),
                    reads=(buf("sc1"),), writes=(buf("sc3"),), small=True)
                P.op("scalar", lambda s: s.activation(
                    wbld[:, :], stage[:, 1024:2048], AF.Copy, scale=statc[:, 2:3]),
                    reads=WH + (buf("sc2"),), writes=(buf("wbld"),))
                P.op("vector", lambda v: v.tensor_scalar(
                    stage[:, 1024:2048], wbld[:, :], 0.5, 1.0, ALU.is_gt, ALU.mult),
                    reads=(buf("wbld"),), writes=WH)
                P.op("vector", lambda v: v.tensor_scalar(
                    wbld[:, :], wbld[:, :], -0.5, 1.0, ALU.is_lt, ALU.mult),
                    reads=(buf("wbld"),), writes=(buf("wbld"),))
                P.op("vector", lambda v: v.tensor_sub(
                    stage[:, 1024:2048], stage[:, 1024:2048], wbld[:, :]),
                    reads=WH + (buf("wbld"),), writes=WH)
                P.op("scalar", lambda s: s.activation(
                    tmpg[:, 0:512], stage[:, 1024:1536], AF.Copy, scale=statc[:, 3:4]),
                    reads=WH + (buf("sc3"),), writes=(buf("tmpg"),))
                P.op("scalar", lambda s: s.activation(
                    tmpv[:, 0:512], stage[:, 1536:2048], AF.Copy, scale=statc[:, 3:4]),
                    reads=WH + (buf("sc3"),), writes=(buf("tmpv"),))
                if oc + 1 < FC:
                    bdma(oc + 1)
                for ic in range(FC):
                    bk = 4 + (ic % 2)
                    srcop = tmpg if ic < 4 else tmpv
                    srcb = buf("tmpg") if ic < 4 else buf("tmpv")
                    co = (ic % 4) * 128
                    P.op("tensor", lambda t, co=co, bk=bk, srcop=srcop: t.matmul(
                        pb[bk][:, 0:128], srcop[:, co:co + 128], i128s[:, :],
                        start=True, stop=True),
                        reads=(srcb, buf("consts"), PB[bk]), writes=(PB[bk],))
                    P.op("scalar", lambda s, ic=ic, oc=oc, bk=bk: s.copy(
                        wqt[:, ic * D + oc * 128: ic * D + (oc + 1) * 128], pb[bk][:, 0:128]),
                        reads=(PB[bk],), writes=(buf("wqt"),))
                    rdst = r2b32 if ic < 4 else aux32
                    rb = buf("r2b32") if ic < 4 else buf("aux32")
                    j = ic % 4
                    P.op("scalar", lambda s, j=j, bk=bk, rdst=rdst: s.copy(
                        rdst[:, j * 128:(j + 1) * 128], pb[bk][:, 0:128]),
                        reads=(PB[bk],), writes=(rb,))
                for j in range(4):
                    P.dma("gpsimd", "dout", lambda gp, oc=oc, j=j: gp.dma_start(
                        out=wqtr[oc][j], in_=r2b32[:, j * 128:(j + 1) * 128]),
                        reads=(buf("r2b32"),))
                    P.dma("gpsimd", "dout1", lambda gp, oc=oc, j=j: gp.dma_start(
                        out=wqtr[oc][4 + j], in_=aux32[:, j * 128:(j + 1) * 128]),
                        reads=(buf("aux32"),))
                bleft[0] -= 1
            bdma(0)
            return [lambda oc=oc: bjob(oc) for oc in range(FC)]

        def kv_ingest():
            kvr = g["kvparts"].ap().rearrange("(r p) c -> r p c", p=128)
            P.op("vector", lambda v: v.memset(kvbd[:, :], 0.0), writes=(buf("kvbd"),))
            P.op("vector", lambda v: v.memset(dbd[:, :], 0.0), writes=(buf("dbd"),))

            def slotb(r):
                return (buf("sqbA"), buf("sqbB")) if r % 2 == 0 else (buf("sqbB"),)

            def dmain(r):
                off = (r % 2) * 520
                P.dma("sync", "din" if r % 2 == 0 else "din1",
                      lambda s, r=r, off=off: s.dma_start(
                          out=sqb[:, off:off + 520], in_=kvr[r]),
                      reads=slotb(r), writes=slotb(r))
            dmain(0)
            for r in range(8):
                if r + 1 < 8:
                    dmain(r + 1)
                off = (r % 2) * 520
                if r == 0:
                    P.op("gpsimd", lambda g2, off=off: g2.tensor_copy(
                        kvsb[:, :], sqb[:, off:off + 520]),
                        reads=slotb(r), writes=(buf("kvsb"),))
                else:
                    P.op("gpsimd", lambda g2, off=off, r=r: g2.tensor_tensor(
                        kvsb[:, :], sqb[:, off:off + 520], kvsb[:, :], ALU.add),
                        reads=slotb(r) + (buf("kvsb"),), writes=(buf("kvsb"),))
            for p_ in range(FC):
                P.op("gpsimd", lambda g2, p_=p_: g2.tensor_copy(
                    kvbd[0:64, p_ * 128: p_ * 128 + 64], kvsb[0:64, p_ * 64:(p_ + 1) * 64]),
                    reads=(buf("kvsb"),), writes=(buf("kvbd"),))
                P.op("gpsimd", lambda g2, p_=p_: g2.tensor_copy(
                    kvbd[64:128, p_ * 128 + 64: p_ * 128 + 128], kvsb[64:128, p_ * 64:(p_ + 1) * 64]),
                    reads=(buf("kvsb"),), writes=(buf("kvbd"),))
                P.op("gpsimd", lambda g2, p_=p_: g2.tensor_copy(
                    dbd[0:64, p_ * 16 + 2 * p_: p_ * 16 + 2 * p_ + 1], kvsb[0:64, 512 + p_:513 + p_]),
                    reads=(buf("kvsb"),), writes=(buf("dbd"),), small=True)
                P.op("gpsimd", lambda g2, p_=p_: g2.tensor_copy(
                    dbd[64:128, p_ * 16 + 2 * p_ + 1: p_ * 16 + 2 * p_ + 2], kvsb[64:128, 512 + p_:513 + p_]),
                    reads=(buf("kvsb"),), writes=(buf("dbd"),), small=True)

        def ffn_phase(tile, final, fill=None):
            def F():
                if fill:
                    fill.popleft()()
                    if fill and len(fill) > 16 and getattr(fill[0], "light", False):
                        fill.popleft()()
            t0 = tile * TS

            def emit_xh_g(gch):
                for half, bk in ((0, 0), (1, 1)):
                    def mmg(t, bk=bk, gch=gch, half=half):
                        last = None
                        for kc in range(FC):
                            last = t.matmul(pb[bk][:, 0:512],
                                            x16[:, kc * TPC + gch * 128: kc * TPC + (gch + 1) * 128],
                                            h16[:, kc * D + half * 512: kc * D + half * 512 + 512],
                                            start=(kc == 0), stop=(kc == FC - 1))
                        return last
                    P.op("tensor", mmg, reads=(X16[gch // 4], buf("h16"), PB[bk]), writes=(PB[bk],))
                    F()

            def emit_xh(cc):
                emit_xh_g(tile * 4 + cc)

            if final or tile == 0:
                emit_xh(0)
            for pp in range(2):
                if DBG2 < 1:
                    continue
                pph = (pp % 2) * PRW if LW == 2 else 0
                pbuf = str(pp % 2) if LW == 2 else "0"
                vb, bbb = buf("v32" + pbuf), buf("bb32")
                ubb = buf("u32b")
                for ci in range(2):
                    cc = 2 * pp + ci
                    co = pph + ci * D
                    t_m1 = P.op("vector", lambda v, ci=ci: v.tensor_scalar_mul(
                        statc[:, 8 + ci:9 + ci], pb[0][:, 0:1], 1.0 / 32),
                        reads=(PB[0],), writes=(buf(f"sc8{ci}"),), small=True)
                    t_a0 = P.op("vector", lambda v, ci=ci: v.tensor_reduce(
                        statc[:, 32 + ci:33 + ci], pb[0][:, 1:512], AX.X, ALU.max,
                        apply_absolute_value=True),
                        reads=(PB[0],), writes=(buf(f"sc32{ci}"),), small=True)
                    t_a1 = P.op("vector", lambda v, ci=ci: v.tensor_reduce(
                        statc[:, 34 + ci:35 + ci], pb[1][:, 0:512], AX.X, ALU.max,
                        apply_absolute_value=True),
                        reads=(PB[1],), writes=(buf(f"sc34{ci}"),), small=True)
                    t_sqa = P.op("scalar", lambda s: s.activation(
                        sqb[:, 0:512], pb[0][:, 0:512], AF.Square),
                        reads=(PB[0],), writes=(buf("sqbA"),), after=(t_m1, t_a0))
                    t_sqb = P.op("scalar", lambda s: s.activation(
                        sqb[:, 512:1024], pb[1][:, 0:512], AF.Square),
                        reads=(PB[1],), writes=(buf("sqbB"),), after=(t_a1,))
                    P.op("vector", lambda v, ci=ci: v.tensor_tensor(
                        statc[:, 36 + ci:37 + ci], statc[:, 32 + ci:33 + ci],
                        statc[:, 34 + ci:35 + ci], ALU.max),
                        reads=(buf(f"sc32{ci}"), buf(f"sc34{ci}")), writes=(buf(f"sc36{ci}"),), small=True)
                    P.op("vector", lambda v, ci=ci: v.tensor_single_scalar(
                        statc[:, 12 + ci:13 + ci], statc[:, 36 + ci:37 + ci], 1e-5, ALU.max),
                        reads=(buf(f"sc36{ci}"),), writes=(buf(f"sc12{ci}"),), small=True)
                    P.op("vector", lambda v, ci=ci: v.reciprocal(
                        statc[:, 14 + ci:15 + ci], statc[:, 12 + ci:13 + ci]),
                        reads=(buf(f"sc12{ci}"),), writes=(buf(f"sc14{ci}"),), small=True)
                    P.op("vector", lambda v, ci=ci: v.tensor_scalar_mul(
                        statc[:, 16 + ci:17 + ci], statc[:, 14 + ci:15 + ci], 7.0),
                        reads=(buf(f"sc14{ci}"),), writes=(buf(f"sc16{ci}"),), small=True)
                    P.op("vector", lambda v, co=co, ci=ci: v.tensor_scalar_mul(
                        v32[:, co:co + 512], pb[0][:, 0:512], statc[:, 16 + ci:17 + ci]),
                        reads=(PB[0], buf(f"sc16{ci}")), writes=(vb,), after=(t_sqa,))
                    P.op("vector", lambda v, co=co, ci=ci: v.tensor_scalar_mul(
                        v32[:, co + 512:co + 1024], pb[1][:, 0:512], statc[:, 16 + ci:17 + ci]),
                        reads=(PB[1], buf(f"sc16{ci}")), writes=(vb,), after=(t_sqb,))
                    if cc + 1 < 4:
                        emit_xh(cc + 1)
                    P.op("vector", lambda v, ci=ci: v.reduce_sum(
                        statc[:, 10 + ci:11 + ci], sqb[:, 0:D], AX.X),
                        reads=(buf("sqbA"), buf("sqbB")), writes=(buf(f"sc10{ci}"),), small=True)
                if DBG2 < 2:
                    continue
                P.op("vector", lambda v, pph=pph: v.memset(
                    v32[:, pph:pph + PRW].rearrange("p (c d) -> p c d", d=D)[:, :, 0:1], 0.0),
                    writes=(vb,), small=True)
                P.op("vector", lambda v: v.tensor_tensor(
                    statc[:, 18:20], statc[:, 8:10], statc[:, 8:10], ALU.mult),
                    reads=(buf("sc80"), buf("sc81")), writes=(buf("sc18"),), small=True)
                P.op("vector", lambda v: v.tensor_single_scalar(
                    statc[:, 20:22], statc[:, 10:12], 1.0 / D, ALU.mult),
                    reads=(buf("sc100"), buf("sc101")), writes=(buf("sc20"),), small=True)
                P.op("vector", lambda v: v.tensor_sub(statc[:, 22:24], statc[:, 20:22], statc[:, 18:20]),
                     reads=(buf("sc20"), buf("sc18")), writes=(buf("sc22"),), small=True)
                P.op("scalar", lambda s: s.activation(statc[:, 24:26], statc[:, 22:24], AF.Ln, bias=cfs[:, 0:1]),
                     reads=(buf("sc22"), buf("consts")), writes=(buf("sc24"),), small=True)
                P.op("scalar", lambda s: s.activation(statc[:, 26:28], statc[:, 24:26], AF.Exp, scale=cfs[:, 1:2]),
                     reads=(buf("sc24"), buf("consts")), writes=(buf("sc26"),), small=True)
                P.op("vector", lambda v: v.tensor_tensor(
                    statc[:, 28:30], statc[:, 12:14], statc[:, 26:28], ALU.mult),
                    reads=(buf("sc120"), buf("sc121"), buf("sc26")), writes=(buf("sc28"),), small=True)
                P.op("vector", lambda v: v.tensor_single_scalar(
                    statc[:, 28:30], statc[:, 28:30], 1.0 / 7, ALU.mult),
                    reads=(buf("sc28"),), writes=(buf("sc28"),), small=True)
                if DBG2 < 3:
                    continue
                P.op("scalar", lambda s, pph=pph: s.activation(
                    u32b[:, :], v32[:, pph:pph + PRW], AF.Abs),
                    reads=(vb,), writes=(ubb,))
                P.op("vector", lambda v, pph=pph: v.tensor_scalar(
                    rs16[:, :], v32[:, pph:pph + PRW], 0.0, 2.0, ALU.is_ge, ALU.mult),
                    reads=(vb,), writes=(buf("rs16"),))
                P.op("vector", lambda v: v.tensor_single_scalar(
                    rs16[:, :], rs16[:, :], -1.0, ALU.add),
                    reads=(buf("rs16"),), writes=(buf("rs16"),))
                P.op("vector", lambda v: v.tensor_scalar(
                    bb32[:, :], u32b[:, :], 3.5, 4.0, ALU.is_ge, ALU.mult),
                    reads=(ubb,), writes=(bbb,))
                P.op("vector", lambda v: v.tensor_sub(u32b[:, :], u32b[:, :], bb32[:, :]),
                     reads=(ubb, bbb), writes=(ubb,))
                P.op("vector", lambda v: v.tensor_scalar(
                    bb32[:, :], u32b[:, :], 1.5, 2.0, ALU.is_ge, ALU.mult),
                    reads=(ubb,), writes=(bbb,))
                P.op("vector", lambda v: v.tensor_sub(u32b[:, :], u32b[:, :], bb32[:, :]),
                     reads=(ubb, bbb), writes=(ubb,))
                P.op("vector", lambda v: v.tensor_scalar(
                    bb32[:, :], u32b[:, :], 0.5, 1.0, ALU.is_ge, ALU.mult),
                    reads=(ubb,), writes=(bbb,))
                P.op("vector", lambda v: v.tensor_sub(u32b[:, :], u32b[:, :], bb32[:, :]),
                     reads=(ubb, bbb), writes=(ubb,))
                P.op("vector", lambda v: v.tensor_mul(bb32[:, :], rs16[:, :], u32b[:, :]),
                     reads=(buf("rs16"), ubb), writes=(bbb,))
                P.op("vector", lambda v, pph=pph: v.tensor_sub(
                    v32[:, pph:pph + PRW], v32[:, pph:pph + PRW], bb32[:, :]),
                    reads=(vb, bbb), writes=(vb,))
                if DBG2 < 4:
                    continue
                for ci in range(2):
                    cc = 2 * pp + ci
                    co = pph + ci * D
                    P.op("vector", lambda v, co=co, ci=ci: v.tensor_scalar_mul(
                        rs16[:, ci * D:(ci + 1) * D], v32[:, co:co + D], statc[:, 28 + ci:29 + ci]),
                        reads=(vb, buf("sc28")), writes=(buf("rs16"),))
                for f in range(FC):
                    bk = 4 + (f % 2)
                    for ci in range(2):
                        cc = 2 * pp + ci
                        P.op("tensor", lambda t, f=f, ci=ci, bk=bk: t.matmul(
                            pb[bk][:, ci * 128:(ci + 1) * 128], rs16[:, ci * D + f * 128: ci * D + (f + 1) * 128],
                            i128s[:, :], start=True, stop=True),
                            reads=(buf("rs16"), buf("consts"), PB[bk]), writes=(PB[bk],))
                    P.op("scalar", lambda s, f=f, pp=pp, bk=bk: s.copy(
                        xqT[:, f * TS + pp * 256: f * TS + (pp + 1) * 256], pb[bk][:, 0:256]),
                        reads=(PB[bk],), writes=(buf("xqT"),))
                    F()
            if not final and tile + 1 < NT:
                while xtcnt.get(("xT", tile + 1), 0) < FC and fill:
                    fill.popleft()()
                emit_xh_g((tile + 1) * 4)
            if launch == 1 and tile == 0:
                while bleft[0] > 0 and fill:
                    fill.popleft()()
            for oc in range(FC if DBG2 >= 5 else 0):
                bk = 4 + (oc % 2)

                def mmo(t, oc=oc, bk=bk):
                    last = None
                    for ic in range(FC):
                        last = t.matmul(pb[bk][:, 0:512], wqt[:, ic * D + oc * 128: ic * D + (oc + 1) * 128],
                                        xqT[:, ic * TS:(ic + 1) * TS],
                                        start=(ic == 0), stop=(ic == FC - 1))
                    return last
                P.op("tensor", mmo, reads=(buf("wqt"), buf("xqT"), PB[bk]), writes=(PB[bk],))
                F()
                if not final:
                    P.op("vector", lambda v, oc=oc, bk=bk, t0=t0: v.tensor_tensor(
                        x16[:, oc * TPC + t0: oc * TPC + t0 + TS], pb[bk][:, 0:512],
                        x16[:, oc * TPC + t0: oc * TPC + t0 + TS], ALU.add),
                        reads=(PB[bk], X16[tile]), writes=(X16[tile],))
                else:
                    i = oc % 2
                    hb = buf("sqbA") if i == 0 else buf("sqbB")
                    P.op("vector", lambda v, oc=oc, bk=bk, t0=t0, i=i: v.tensor_tensor(
                        sqb[:, i * TS:(i + 1) * TS], pb[bk][:, 0:512],
                        x16[:, oc * TPC + t0: oc * TPC + t0 + TS], ALU.add),
                        reads=(PB[bk], X16[tile], hb), writes=(hb,))
                    P.dma("gpsimd", "dout" if oc % 2 == 0 else "dout1",
                          lambda gp, oc=oc, tile=tile, i=i: gp.dma_start(
                              out=yT.ap()[oc * 128:(oc + 1) * 128, tile * TS:(tile + 1) * TS],
                              in_=sqb[:, i * TS:(i + 1) * TS]),
                          reads=(hb,))

        def ln2_block(tile):
            t0 = tile * TS

            def mmsum(t, t0=t0):
                last = None
                for f in range(FC):
                    last = t.matmul(pb[4][0:1, 0:512], ones1s[:, 0:1],
                                    x16[:, f * TPC + t0: f * TPC + t0 + TS],
                                    start=(f == 0), stop=(f == FC - 1))
                return last
            P.op("tensor", mmsum, reads=(X16[tile], buf("consts"), PB[4]), writes=(PB[4],))
            for f in range(FC):
                P.op("scalar", lambda s, f=f, t0=t0: s.activation(
                    xqT[:, f * TS:(f + 1) * TS], x16[:, f * TPC + t0: f * TPC + t0 + TS], AF.Square),
                    reads=(X16[tile],), writes=(buf("xqT"),))

            def mmsq(t):
                last = None
                for f in range(FC):
                    last = t.matmul(pb[5][0:1, 0:512], ones1s[:, 0:1],
                                    xqT[:, f * TS:(f + 1) * TS],
                                    start=(f == 0), stop=(f == FC - 1))
                return last
            P.op("tensor", mmsq, reads=(buf("xqT"), buf("consts"), PB[5]), writes=(PB[5],))
            P.op("vector", lambda v: v.tensor_scalar_mul(rowm, pb[4][0:1, 0:512], 1.0 / D),
                 reads=(PB[4],), writes=(buf("rowm"),))
            P.op("vector", lambda v: v.tensor_scalar_mul(rowq, pb[5][0:1, 0:512], 1.0 / D),
                 reads=(PB[5],), writes=(buf("rowq"),))
            P.op("tensor", lambda t: t.matmul(pb[4][:, 0:512], onesrs[0:1, :], rowm, start=True, stop=True),
                 reads=(buf("rowm"), buf("consts"), PB[4]), writes=(PB[4],))
            P.op("vector", lambda v: v.tensor_tensor(rowm, rowm, rowm, ALU.mult),
                 reads=(buf("rowm"),), writes=(buf("rowm"),), small=True)
            P.op("vector", lambda v: v.tensor_sub(rowq, rowq, rowm),
                 reads=(buf("rowq"), buf("rowm")), writes=(buf("rowq"),))
            P.op("scalar", lambda s: s.activation(rowm, rowq, AF.Ln, bias=cfs[0:1, 0:1]),
                 reads=(buf("rowq"), buf("consts")), writes=(buf("rowm"),))
            P.op("scalar", lambda s: s.activation(rowq, rowm, AF.Exp, scale=cfs[0:1, 1:2]),
                 reads=(buf("rowm"), buf("consts")), writes=(buf("rowq"),))
            P.op("tensor", lambda t: t.matmul(pb[5][:, 0:512], onesrs[0:1, :], rowq, start=True, stop=True),
                 reads=(buf("rowq"), buf("consts"), PB[5]), writes=(PB[5],))
            P.op("vector", lambda v: v.tensor_copy(r2b32[:, :], pb[5][:, 0:512]),
                 reads=(PB[5],), writes=(buf("r2b32"),))
            P.op("vector", lambda v: v.tensor_copy(aux32[:, :], pb[4][:, 0:512]),
                 reads=(PB[4],), writes=(buf("aux32"),))
            for f in range(FC):
                eng = "vector" if f < 5 else "gpsimd"
                tmp = tmpv if f < 5 else tmpg
                tb = buf("tmpv") if f < 5 else buf("tmpg")
                P.op(eng, lambda e2, f=f, t0=t0, tmp=tmp: e2.tensor_sub(
                    tmp[:, :], x16[:, f * TPC + t0: f * TPC + t0 + TS], aux32[:, :]),
                    reads=(X16[tile], buf("aux32")), writes=(tb,))
                P.op(eng, lambda e2, f=f, t0=t0, tmp=tmp: e2.tensor_mul(
                    ln2[:, f * TPC + t0: f * TPC + t0 + TS], tmp[:, :], r2b32[:, :]),
                    reads=(tb, buf("r2b32")), writes=(LN2[tile],))

        def bnd_out(srct, outt, sbuf_, f, t, qi, eng="scalar"):
            hb = buf("sqbA") if qi == 0 else buf("sqbB")

            def fn(e2, f=f, t=t, qi=qi, srct=srct):
                src = srct[:, f * TPC + t * TS: f * TPC + (t + 1) * TS]
                dst = sqb[:, qi * 512:(qi + 1) * 512]
                return e2.copy(dst, src) if eng == "scalar" else e2.tensor_copy(dst, src)
            P.op(eng, fn, reads=(sbuf_, hb), writes=(hb,))
            P.dma("gpsimd", "dout" if qi % 2 == 0 else "dout1",
                  lambda gp, f=f, t=t, qi=qi, outt=outt: gp.dma_start(
                      out=outt.ap()[f * 128:(f + 1) * 128, t * TS:(t + 1) * TS],
                      in_=sqb[:, qi * 512:(qi + 1) * 512]),
                  reads=(hb,))

        def kv_jobs(tile):
            jobs = []
            for cc in range(4):
                gch = tile * 4 + cc

                def ja(gch=gch, tile=tile):
                    for half in range(2):
                        def mmk(t, gch=gch, half=half):
                            last = None
                            for kc in range(FC):
                                last = t.matmul(pb[2 + half][:, 0:512],
                                                ln2[:, kc * TPC + gch * 128: kc * TPC + (gch + 1) * 128],
                                                wS3[:, kc * D + half * 512: kc * D + half * 512 + 512],
                                                start=(kc == 0), stop=(kc == FC - 1))
                            return last
                        P.op("tensor", mmk, reads=(LN2[tile], buf("wS3"), PB[2 + half]), writes=(PB[2 + half],))
                    P.op("scalar", lambda s: s.activation(k16[:, 0:512], pb[2][:, 0:512], AF.Sigmoid),
                         reads=(PB[2],), writes=(buf("k16"),))
                    P.op("scalar", lambda s: s.activation(k16[:, 512:1024], pb[3][:, 0:512], AF.Sigmoid),
                         reads=(PB[3],), writes=(buf("k16"),))

                def jb(gch=gch, tile=tile):
                    for half in range(2):
                        def mmv(t, gch=gch, half=half):
                            last = None
                            for kc in range(FC):
                                last = t.matmul(pb[2 + half][:, 0:512],
                                                ln2[:, kc * TPC + gch * 128: kc * TPC + (gch + 1) * 128],
                                                wS4[:, kc * D + half * 512: kc * D + half * 512 + 512],
                                                start=(kc == 0), stop=(kc == FC - 1))
                            return last
                        P.op("tensor", mmv, reads=(LN2[tile], buf("wS4"), PB[2 + half]), writes=(PB[2 + half],))
                    P.op("scalar", lambda s: s.copy(v16[:, 0:512], pb[2][:, 0:512]),
                         reads=(PB[2],), writes=(buf("v16"),))
                    P.op("scalar", lambda s: s.copy(v16[:, 512:1024], pb[3][:, 0:512]),
                         reads=(PB[3],), writes=(buf("v16"),))

                def jc(gch=gch):
                    def mmkv(t, gch=gch):
                        last = None
                        for p_ in range(FC):
                            bank = pb[6] if p_ < 4 else pb[7]
                            colo = (p_ % 4) * 128
                            last = t.matmul(bank[:, colo:colo + 128],
                                            k16[:, p_ * 128:(p_ + 1) * 128],
                                            v16[:, p_ * 128:(p_ + 1) * 128],
                                            start=(gch == 0 and (p_ % 4) == 0),
                                            stop=(gch == NCH - 1 and (p_ % 4) == 3))
                        return last
                    P.op("tensor", mmkv, reads=(buf("k16"), buf("v16"), PB[6], PB[7]),
                         writes=(PB[6], PB[7]))

                    def mmks(t):
                        last = None
                        for p_ in range(FC):
                            last = t.matmul(pb[2][:, p_:p_ + 1], k16[:, p_ * 128:(p_ + 1) * 128],
                                            ones1s[:, 0:1], start=(p_ == 0), stop=(p_ == FC - 1))
                        return last
                    P.op("tensor", mmks, reads=(buf("k16"), buf("consts"), PB[2]), writes=(PB[2],))
                    sks, dks = (gch % 2) * 8, ((gch + 1) % 2) * 8
                    if gch == 0:
                        P.op("vector", lambda v, dks=dks: v.tensor_copy(ksum[:, dks:dks + 8], pb[2][:, 0:8]),
                             reads=(PB[2],), writes=(buf("ksum"),), small=True)
                    else:
                        P.op("vector", lambda v, sks=sks, dks=dks: v.tensor_tensor(
                            ksum[:, dks:dks + 8], pb[2][:, 0:8], ksum[:, sks:sks + 8], ALU.add),
                            reads=(PB[2], buf("ksum")), writes=(buf("ksum"),), small=True)
                jobs += [ja, jb, jc]
            return jobs

        if launch == 1:
            P.dma("sync", "din", lambda s: s.dma_start(out=h16[:, :], in_=g["h16"].ap()),
                  writes=(buf("h16"),))
            for j in piece_jobs("xT", x16, X16, (0,), "scalar"):
                j()
            jobs = deque()
            wkj = wpiece_jobs("wk_l", wS3, buf("wS3"), "scalar")
            if DBG >= 2:
                bj = mk_build()
                for i, b in enumerate(bj):
                    jobs.append(b)
                    jobs.extend(wkj[2 * i: 2 * i + 2])
                jobs.extend(wkj[16:])
            else:
                bleft[0] = 0
                jobs.extend(wkj)
            jobs.extend(piece_jobs("xT", x16, X16, (1,), "scalar"))
            jobs.extend(wpiece_jobs("wv_l", wS4, buf("wS4"), "scalar"))
            jobs.extend(piece_jobs("xT", x16, X16, (2,), "scalar"))
            jobs.extend(piece_jobs("xT", x16, X16, (3,), "scalar"))
            if DBG >= 3:
                qrr = [0]
                for tile in range(NT):
                    while xtcnt.get(("xT", tile), 0) < FC:
                        jobs.popleft()()
                    ffn_phase(tile, final=False, fill=jobs)
                    if DBG >= 4:
                        ln2_block(tile)
                        if DBG >= 5:
                            jobs.extend(kv_jobs(tile))
                        for f in range(FC):
                            bnd_out(x16, x1_out, X16[tile], f, tile, qrr[0] % 2)
                            qrr[0] += 1
                            if tile == NT - 1 and jobs:
                                jobs.popleft()()
                            bnd_out(ln2, ln2_out, LN2[tile], f, tile, qrr[0] % 2, "vector")
                            qrr[0] += 1
                            if tile == NT - 1 and jobs:
                                jobs.popleft()()
                while jobs:
                    jobs.popleft()()

        if launch == 1:
            for p_ in range(FC if DBG >= 5 else 0):
                bank = pb[6] if p_ < 4 else pb[7]
                pbuf = PB[6] if p_ < 4 else PB[7]
                colo = (p_ % 4) * 128
                rd = (pbuf, SH[0], SH[1]) if p_ == 0 else (pbuf,)
                P.op("vector", lambda v, p_=p_, bank=bank, colo=colo: v.tensor_copy(
                    kvsb[0:64, p_ * 64:(p_ + 1) * 64], bank[0:64, colo:colo + 64]),
                    reads=rd, writes=(buf("kvsb"), SH[0], SH[1]))
                P.op("vector", lambda v, p_=p_, bank=bank, colo=colo: v.tensor_copy(
                    kvsb[64:128, p_ * 64:(p_ + 1) * 64], bank[64:128, colo + 64:colo + 128]),
                    reads=(pbuf,), writes=(buf("kvsb"), SH[0], SH[1]))
            if DBG >= 5:
                P.op("vector", lambda v: v.tensor_copy(kvsb[:, 512:520], ksum[:, 0:8]),
                     reads=(buf("ksum"),), writes=(buf("kvsb"), SH[0], SH[1]))
                P.dma("gpsimd", "dout", lambda gp: gp.dma_start(out=kv_out.ap(), in_=kvsb[:, :]),
                      reads=(buf("kvsb"),))

        def bjobs(tile):
            t0 = tile * TS
            QB = buf("qt")
            jobs = []

            def jq(oc):
                bk = 2 + (oc % 2)

                def mmq(t, oc=oc, t0=t0, bk=bk):
                    last = None
                    for kc in range(FC):
                        last = t.matmul(pb[bk][:, 0:512],
                                        wS3[:, kc * D + oc * 128: kc * D + (oc + 1) * 128],
                                        ln2[:, kc * TPC + t0: kc * TPC + t0 + TS],
                                        start=(kc == 0), stop=(kc == FC - 1))
                    return last
                P.op("tensor", mmq, reads=(buf("wS3"), LN2[tile], PB[bk]), writes=(PB[bk],))
                P.op("scalar", lambda s, oc=oc, bk=bk: s.activation(
                    qt[:, oc * TS:(oc + 1) * TS], pb[bk][:, 0:512], AF.Sigmoid),
                    reads=(PB[bk],), writes=(QB,))
            for oc in range(FC):
                jobs.append(lambda oc=oc: jq(oc))

            def jden():
                def mmden(t):
                    last = None
                    for p_ in range(FC):
                        last = t.matmul(pb[2][0:16, 0:512], dbd[:, p_ * 16:(p_ + 1) * 16],
                                        qt[:, p_ * TS:(p_ + 1) * TS],
                                        start=(p_ == 0), stop=(p_ == FC - 1))
                    return last
                P.op("tensor", mmden, reads=(buf("dbd"), QB, PB[2]), writes=(PB[2],))
                P.op("vector", lambda v: v.tensor_scalar_add(rdrow[:, :], pb[2][0:16, 0:512], 1e-8),
                     reads=(PB[2],), writes=(buf("aux32"),))
                P.op("vector", lambda v: v.reciprocal(rdr2[0:16, :], rdrow[:, :]),
                     reads=(buf("aux32"),), writes=(buf("rdr2"),))
            jobs.append(jden)

            def jnum(p_):
                ba = 2 if p_ % 2 == 0 else 6
                bs = 3 if p_ % 2 == 0 else 7
                rdt = rdb32 if p_ % 2 == 0 else aux32
                rb = buf("rdbA") if p_ % 2 == 0 else buf("aux32")
                P.op("tensor", lambda t, p_=p_, ba=ba: t.matmul(
                    pb[ba][:, 0:512], kvbd[:, p_ * 128:(p_ + 1) * 128],
                    qt[:, p_ * TS:(p_ + 1) * TS], start=True, stop=True),
                    reads=(buf("kvbd"), QB, PB[ba]), writes=(PB[ba],))
                P.op("tensor", lambda t, p_=p_, bs=bs: t.matmul(
                    pb[bs][:, 0:512], sel16s[:, p_ * 128:(p_ + 1) * 128], rdr2[0:16, :],
                    start=True, stop=True),
                    reads=(buf("rdr2"), buf("consts"), PB[bs]), writes=(PB[bs],))
                P.op("vector", lambda v, bs=bs, rdt=rdt: v.tensor_copy(rdt[:, 0:512], pb[bs][:, 0:512]),
                     reads=(PB[bs], buf("kvsb")), writes=(rb,))
                P.op("vector", lambda v, p_=p_, ba=ba, rdt=rdt: v.tensor_mul(
                    ao16[:, p_ * TS:(p_ + 1) * TS], pb[ba][:, 0:512], rdt[:, 0:512]),
                    reads=(PB[ba], rb), writes=(buf("ao16"),))
            for p_ in range(FC):
                jobs.append(lambda p_=p_: jnum(p_))

            def jfc(oc):
                bk = 6 + (oc % 2)

                def mmfc(t, oc=oc, bk=bk):
                    last = None
                    for hc in range(FC):
                        last = t.matmul(pb[bk][:, 0:512],
                                        wS4[:, hc * D + oc * 128: hc * D + (oc + 1) * 128],
                                        ao16[:, hc * TS:(hc + 1) * TS],
                                        start=(hc == 0), stop=(hc == FC - 1))
                    return last
                P.op("tensor", mmfc, reads=(buf("wS4"), buf("ao16"), PB[bk]), writes=(PB[bk],))
                P.op("vector", lambda v, oc=oc, t0=t0, bk=bk: v.tensor_tensor(
                    qt[:, oc * TS:(oc + 1) * TS], pb[bk][:, 0:512],
                    ln2[:, oc * TPC + t0: oc * TPC + t0 + TS], ALU.add),
                    reads=(PB[bk], LN2[tile]), writes=(QB,))
            for oc in range(FC):
                jobs.append(lambda oc=oc: jfc(oc))

            def jstat():
                def mmsy(t):
                    last = None
                    for f in range(FC):
                        last = t.matmul(pb[2][0:1, 0:512], ones1s[:, 0:1],
                                        qt[:, f * TS:(f + 1) * TS],
                                        start=(f == 0), stop=(f == FC - 1))
                    return last
                P.op("tensor", mmsy, reads=(QB, buf("consts"), PB[2]), writes=(PB[2],))
                for f in range(FC):
                    P.op("scalar", lambda s, f=f: s.activation(
                        ao16[:, f * TS:(f + 1) * TS], qt[:, f * TS:(f + 1) * TS], AF.Square),
                        reads=(QB,), writes=(buf("ao16"),))

                def mmsy2(t):
                    last = None
                    for f in range(FC):
                        last = t.matmul(pb[3][0:1, 0:512], ones1s[:, 0:1],
                                        ao16[:, f * TS:(f + 1) * TS],
                                        start=(f == 0), stop=(f == FC - 1))
                    return last
                P.op("tensor", mmsy2, reads=(buf("ao16"), buf("consts"), PB[3]), writes=(PB[3],))
                P.op("vector", lambda v: v.tensor_scalar_mul(rowm, pb[2][0:1, 0:512], 1.0 / D),
                     reads=(PB[2],), writes=(buf("rowm"),))
                P.op("vector", lambda v: v.tensor_scalar_mul(rowq, pb[3][0:1, 0:512], 1.0 / D),
                     reads=(PB[3],), writes=(buf("rowq"),))
                P.op("tensor", lambda t: t.matmul(pb[2][:, 0:512], onesrs[0:1, :], rowm, start=True, stop=True),
                     reads=(buf("rowm"), buf("consts"), PB[2]), writes=(PB[2],))
                P.op("vector", lambda v: v.tensor_tensor(rowm, rowm, rowm, ALU.mult),
                     reads=(buf("rowm"),), writes=(buf("rowm"),), small=True)
                P.op("vector", lambda v: v.tensor_sub(rowq, rowq, rowm),
                     reads=(buf("rowq"), buf("rowm")), writes=(buf("rowq"),))
                P.op("scalar", lambda s: s.activation(rowm, rowq, AF.Ln, bias=cfs[0:1, 0:1]),
                     reads=(buf("rowq"), buf("consts")), writes=(buf("rowm"),))
                P.op("scalar", lambda s: s.activation(rowq, rowm, AF.Exp, scale=cfs[0:1, 1:2]),
                     reads=(buf("rowm"), buf("consts")), writes=(buf("rowq"),))
                P.op("tensor", lambda t: t.matmul(pb[3][:, 0:512], onesrs[0:1, :], rowq, start=True, stop=True),
                     reads=(buf("rowq"), buf("consts"), PB[3]), writes=(PB[3],))
                P.op("vector", lambda v: v.tensor_copy(r2b32[:, :], pb[3][:, 0:512]),
                     reads=(PB[3],), writes=(buf("r2b32"),))
                P.op("vector", lambda v: v.tensor_copy(mbf[:, :], pb[2][:, 0:512]),
                     reads=(PB[2],), writes=(buf("aux32"),))
            jobs.append(jstat)

            def jret(fpair):
                for f in (2 * fpair, 2 * fpair + 1):
                    eng = "vector" if f < 5 else "gpsimd"
                    tmp = tmpv if f < 5 else tmpg
                    tb = buf("tmpv") if f < 5 else buf("tmpg")
                    P.op(eng, lambda e2, f=f, tmp=tmp: e2.tensor_sub(
                        tmp[:, :], qt[:, f * TS:(f + 1) * TS], mbf[:, :]),
                        reads=(QB, buf("aux32")), writes=(tb,))
                    P.op(eng, lambda e2, tmp=tmp: e2.tensor_mul(
                        tmp[:, :], tmp[:, :], r2b32[:, :]),
                        reads=(tb, buf("r2b32")), writes=(tb,))
                    P.op(eng, lambda e2, f=f, tmp=tmp: e2.tensor_scalar(
                        tmp[:, :], tmp[:, :], gbs[:, f:f + 1], gbs[:, 8 + f:9 + f], ALU.mult, ALU.add),
                        reads=(tb, buf("consts")), writes=(tb,))
                    P.op(eng, lambda e2, f=f, t0=t0, tmp=tmp: e2.tensor_tensor(
                        x16[:, f * TPC + t0: f * TPC + t0 + TS], tmp[:, :],
                        x16[:, f * TPC + t0: f * TPC + t0 + TS], ALU.add),
                        reads=(tb, X16[tile]), writes=(X16[tile],))
            for fp in range(4):
                jobs.append(lambda fp=fp: jret(fp))
            return jobs

        if launch == 2:
            for j in piece_jobs("ln2f", ln2, LN2, (0,), "scalar"):
                j()
            for j in wpiece_jobs("wq_l", wS3, buf("wS3")):
                j()
            kv_ingest()
            for j in piece_jobs("x1f", x16, X16, (0,), "gpsimd"):
                j()
            P.dma("sync", "din", lambda s: s.dma_start(out=h16[:, :], in_=g["h16"].ap()),
                  writes=(buf("h16"),))
            for j in wpiece_jobs("wfc_l", wS4, buf("wS4")):
                j()
            for j in bjobs(0):
                j()
            for j in wpiece_jobs("wqt_f", wqt, buf("wqt")):
                j()
            jobs2 = deque()
            for tile in range(NT):
                if tile + 1 < NT:
                    jobs2.extend(piece_jobs("ln2f", ln2, LN2, (tile + 1,), "scalar"))
                    bj = bjobs(tile + 1)
                    jobs2.extend(bj[:8])
                    jobs2.extend(piece_jobs("x1f", x16, X16, (tile + 1,), "scalar"))
                    jobs2.extend(bj[8:])
                ffn_phase(tile, final=True, fill=jobs2)
                while jobs2:
                    jobs2.popleft()()

        P.wait("gpsimd", ("dout", P.cnt["dout"], False))
        if P.cnt["dout1"] > 0:
            P.wait("gpsimd", ("dout1", P.cnt["dout1"], False))

        with nc.Block() as block:
            @block.tensor
            def _(t):
                P.emit(t, "tensor", sems)

            @block.vector
            def _(v):
                P.emit(v, "vector", sems)

            @block.scalar
            def _(s):
                P.emit(s, "scalar", sems)

            @block.gpsimd
            def _(gp):
                P.emit(gp, "gpsimd", sems)

            @block.sync
            def _(s):
                P.emit(s, "sync", sems)

    return nc


def kernel(L0_attn_inputs, L0_attn_W_Q, L0_attn_W_K, L0_attn_W_V, L0_attn_W_fc,
           L0_attn_gamma, L0_attn_beta, L0_ffn_weight):
    x = np.asarray(L0_attn_inputs)[0]
    xT = np.ascontiguousarray(x.T)
    h16 = _hadamard_bf16()
    i128 = np.eye(128, dtype=ml_dtypes.bfloat16)
    ones1 = np.ones((128, 1), dtype=ml_dtypes.bfloat16)
    onesrow = np.ones((1, 128), dtype=np.float32)
    sel16 = _sel16_f32()
    gb = np.concatenate([
        np.asarray(L0_attn_gamma).reshape(8, 128).T,
        np.asarray(L0_attn_beta).reshape(8, 128).T], axis=1).astype(np.float32)
    gb = np.ascontiguousarray(gb)
    constf = np.zeros((128, 8), np.float32)
    constf[:, 0] = EPS
    constf[:, 1] = -0.5

    bust1 = 16 + (VERSION * 7) % 48
    bust2 = 17 + (VERSION * 5) % 48
    common = {"h16": h16, "i128": i128, "ones1": ones1, "onesrow": onesrow,
              "constf": constf}

    nc1 = build_nc(bust1, 1)
    wffn = np.ascontiguousarray(np.asarray(L0_ffn_weight, dtype=np.float32))
    base1 = dict(common)
    base1.update({
        "wk_l": np.ascontiguousarray(np.asarray(L0_attn_W_K, dtype=np.float32)),
        "wv_l": np.ascontiguousarray(np.asarray(L0_attn_W_V, dtype=np.float32)),
        "wffn_l": wffn,
        "bust": np.zeros((1, bust1), np.float32),
    })
    in_maps1 = []
    for c in range(NCORE):
        m = dict(base1)
        m["xT"] = np.ascontiguousarray(xT[:, c * TPC:(c + 1) * TPC])
        in_maps1.append(m)
    res1 = run_bass_kernel_spmd(nc1, in_maps1, core_ids=list(range(NCORE)))
    kvparts = np.concatenate(
        [np.asarray(res1.results[c]["kv_out"]) for c in range(NCORE)], axis=0)

    nc2 = build_nc(bust2, 2)
    base2 = dict(common)
    base2.update({
        "wq_l": np.ascontiguousarray(np.asarray(L0_attn_W_Q, dtype=np.float32)),
        "wfc_l": np.ascontiguousarray(np.asarray(L0_attn_W_fc, dtype=np.float32)),
        "gb": gb, "sel16": sel16,
        "wqt_f": np.ascontiguousarray(np.asarray(res1.results[0]["wqt_out"], dtype=np.float32)),
        "kvparts": np.ascontiguousarray(kvparts.astype(np.float32)),
        "bust": np.zeros((1, bust2), np.float32),
    })
    in_maps2 = []
    for c in range(NCORE):
        m = dict(base2)
        m["x1f"] = np.asarray(res1.results[c]["x1_out"])
        m["ln2f"] = np.asarray(res1.results[c]["ln2_out"])
        in_maps2.append(m)
    res2 = run_bass_kernel_spmd(nc2, in_maps2, core_ids=list(range(NCORE)))
    out = np.concatenate([r["yT"] for r in res2.results], axis=1)
    return np.ascontiguousarray(out.T).reshape(1, 16384, 1024).astype(np.float32)


# version 22
# speedup vs baseline: 1.1963x; 1.0042x over previous
VERSION = 24

import os
import numpy as np
import ml_dtypes
import concourse.bass as bass
import concourse.mybir as mybir
from concourse.bass_utils import run_bass_kernel_spmd
from contextlib import ExitStack
from collections import deque

F32 = mybir.dt.float32
BF16 = mybir.dt.bfloat16
AF = mybir.ActivationFunctionType
ALU = mybir.AluOpType
AX = mybir.AxisListType

D = 1024
NCORE = 8
TPC = 2048
NT = 4
TS = 512
NCH = 16
FC = 8
NH = 16
EPS = 1e-5
DBG = int(os.environ.get("KDBG", "9"))
DBG2 = int(os.environ.get("KDBG2", "9"))
FULLSYNC = os.environ.get("BASS_SIM_FP64") == "1"


def _hadamard_bf16():
    h = np.array([[1.0]], dtype=np.float64)
    while h.shape[0] < D:
        h = np.block([[h, h], [h, -h]])
    h = (h / 32.0).astype(np.float32)
    hp = h.reshape(FC, 128, D).transpose(1, 0, 2).reshape(128, FC * D)
    return hp.astype(ml_dtypes.bfloat16)


def _sel16_f32():
    s = np.zeros((16, FC * 128), np.float32)
    for p in range(FC):
        s[2 * p, p * 128: p * 128 + 64] = 1.0
        s[2 * p + 1, p * 128 + 64: p * 128 + 128] = 1.0
    return s.astype(ml_dtypes.bfloat16)


class Buf:
    __slots__ = ("name", "writer", "readers")

    def __init__(self, name):
        self.name = name
        self.writer = None
        self.readers = []


class Prog:
    ENGS = ("tensor", "vector", "scalar", "gpsimd", "sync")
    DMAQ = ("din", "din1", "dout", "dout1", "dcc")

    def __init__(self):
        self.ops = {e: [] for e in self.ENGS}
        self.cnt = {e: 0 for e in list(self.ENGS) + list(self.DMAQ)}
        self.hw = {e: {} for e in self.ENGS}
        self.selfhw = {e: 0 for e in self.ENGS}

    def _collect(self, eng, reads, writes):
        waits = {}

        def need(tok):
            if tok is None:
                return
            te, tv, sm = tok
            if te == eng and not FULLSYNC:
                if not (sm and (self.cnt[eng] - tv) < 3):
                    return
            if waits.get(te, -1) < tv:
                waits[te] = tv

        for b in reads:
            need(b.writer)
        for b in writes:
            need(b.writer)
            for t2 in b.readers:
                need(t2)
        out = []
        for e2, v in sorted(waits.items()):
            if e2 == eng:
                if v > self.selfhw[eng]:
                    out.append((e2, v))
                    self.selfhw[eng] = v
            elif self.hw[eng].get(e2, 0) < v:
                out.append((e2, v))
                self.hw[eng][e2] = v
        return out

    def op(self, eng, fn, reads=(), writes=(), small=False, after=()):
        waits = self._collect(eng, reads, writes)
        for tok in after:
            te, tv, _sm = tok
            if te == eng:
                continue
            if self.hw[eng].get(te, 0) < tv:
                waits.append((te, tv))
                self.hw[eng][te] = tv
        self.cnt[eng] += 1
        tok = (eng, self.cnt[eng], small)
        self.ops[eng].append((waits, fn, eng, 1))
        for b in writes:
            b.writer = tok
            b.readers = []
        for b in reads:
            b.readers.append(tok)
        return tok

    def dma(self, eng, q, fn, reads=(), writes=()):
        waits = self._collect(eng, reads, writes)
        if self.cnt[q] > 0 and self.hw[eng].get(q, 0) < self.cnt[q]:
            waits.append((q, self.cnt[q]))
            self.hw[eng][q] = self.cnt[q]
        self.cnt[q] += 16
        tok = (q, self.cnt[q], False)
        self.ops[eng].append((waits, fn, q, 16))
        for b in writes:
            b.writer = tok
            b.readers = []
        for b in reads:
            b.readers.append(tok)
        return tok

    def wait(self, eng, tok):
        te, tv, _ = tok
        if te == eng:
            if tv > self.selfhw[eng]:
                self.selfhw[eng] = tv
                self.ops[eng].append(([(te, tv)], None, None, 0))
            return
        if self.hw[eng].get(te, 0) < tv:
            self.hw[eng][te] = tv
            self.ops[eng].append(([(te, tv)], None, None, 0))

    def emit(self, engobj, eng, sems):
        for waits, fn, incsem, incval in self.ops[eng]:
            for (se, v) in waits:
                engobj.wait_ge(sems[se], v)
            if fn is None:
                continue
            inst = fn(engobj)
            if incval:
                inst.then_inc(sems[incsem], incval)


def build_nc(bust_len, launch):
    nc = bass.Bass("TRN2", target_bir_lowering=False)
    g = {}

    def din(name, shape, dt=F32):
        g[name] = nc.dram_tensor(name, shape, dt, kind="ExternalInput")

    din("h16", [128, FC * D], BF16)
    din("i128", [128, 128], BF16)
    din("ones1", [128, 1], BF16)
    din("onesrow", [1, 128])
    din("constf", [128, 8])
    din("bust", [1, bust_len])
    if launch == 1:
        din("xT", [D, TPC])
        din("wk_l", [D, D]); din("wv_l", [D, D]); din("wffn_l", [D, D])
        kv_out = nc.dram_tensor("kv_out", [128, 520], F32, kind="ExternalOutput")
        x1_out = nc.dram_tensor("x1_out", [D, TPC], F32, kind="ExternalOutput")
        ln2_out = nc.dram_tensor("ln2_out", [D, TPC], F32, kind="ExternalOutput")
        wqt_out = nc.dram_tensor("wqt_out", [128, FC * D], F32, kind="ExternalOutput")
    else:
        din("wq_l", [D, D]); din("wfc_l", [D, D])
        din("wqt_f", [128, FC * D])
        din("gb", [128, 16])
        din("sel16", [16, FC * 128], BF16)
        din("x1f", [D, TPC])
        din("ln2f", [D, TPC])
        din("kvparts", [8 * 128, 520])
        yT = nc.dram_tensor("yT", [D, TPC], F32, kind="ExternalOutput")

    P = Prog()
    with ExitStack() as ctx:
        e = ctx.enter_context
        ctx.enter_context(nc.allow_low_precision(
            reason="bf16 residual stream / operand casts are deliberate; "
                   "matmuls accumulate fp32 in PSUM"))

        def sb(name, shape, dt):
            return e(nc.sbuf_tensor(name, shape, dt))

        x16 = sb("x16", [128, FC * TPC], BF16)
        ln2 = sb("ln2s", [128, FC * TPC], BF16)
        h16 = sb("h16s", [128, FC * D], BF16)
        wqt = sb("wqt", [128, FC * D], BF16)
        wS3 = sb("wS3", [128, FC * D], BF16)
        wS4 = sb("wS4", [128, FC * D], BF16)
        SW = 2048 if launch == 1 else 1024
        stage = sb("stage", [128, SW], F32)
        LW = 2 if launch == 1 else 1
        PRW = 2 * D
        u32b = sb("u32b", [128, PRW], BF16)
        bb32 = sb("bb32", [128, PRW], BF16)
        v32 = sb("v32", [128, LW * PRW], BF16)
        sqb = sb("sqb", [128, 1040], F32)
        rs16 = sb("rs16", [128, PRW], BF16)
        xqT = sb("xqT", [128, FC * TS], BF16)
        if launch == 2:
            qt = sb("qt", [128, FC * TS], BF16)
            ao16 = sb("ao16", [128, FC * TS], BF16)
        if launch == 1:
            k16 = sb("k16", [128, D], BF16)
            v16 = sb("v16", [128, D], BF16)
            wbld = sb("wbld", [128, D], F32)
        statc = sb("statc", [128, 40], F32)
        ksum = sb("ksum", [128, 16], F32) if launch == 1 else None
        r2b32 = sb("r2b32", [128, TS], F32 if launch == 1 else BF16)
        aux32 = sb("aux32", [128, TS], F32)
        tmpv = sb("tmpv", [128, TS], BF16)
        tmpg = sb("tmpg", [128, TS], BF16)
        if launch == 2:
            rdrow = aux32[0:16, :]
            mbf = aux32
        kvsb = sb("kvsb", [128, 520], F32) if launch == 2 else stage[:, 0:520]
        rdb32 = kvsb[:, 0:512] if launch == 2 else None
        if launch == 2:
            kvbd = sb("kvbd", [128, FC * 128], BF16)
            dbd = sb("dbd", [128, FC * 16], BF16)
            rdr2 = sb("rdr2", [16, TS], BF16)
        rowmt = sb("rowmt", [1, TS], F32)
        rowqt = sb("rowqt", [1, TS], F32)
        rowm = rowmt[0:1, :]
        rowq = rowqt[0:1, :]
        i128s = sb("i128s", [128, 128], BF16)
        ones1s = sb("ones1s", [128, 1], BF16)
        onesrs = sb("onesrs", [1, 128], F32)
        if launch == 2:
            sel16s = sb("sel16s", [16, FC * 128], BF16)
            gbs = sb("gbs", [128, 16], F32)
        cfs = sb("cfs", [128, 8], F32)
        busts = sb("busts", [1, bust_len], F32)

        pb = [e(nc.psum_tensor(f"pb{i}", [128, 512], F32)) for i in range(8)]

        semnames = list(Prog.ENGS) + list(Prog.DMAQ)
        sems = {n: e(nc.semaphore("sem_" + n)) for n in semnames}

        B = {}

        def buf(nm):
            if nm not in B:
                B[nm] = Buf(nm)
            return B[nm]

        PB = [buf(f"pb{i}") for i in range(8)]
        X16 = [buf(f"x16_{t}") for t in range(NT)]
        LN2 = [buf(f"ln2_{t}") for t in range(NT)]
        SH = [buf("stage_0"), buf("stage_1"), buf("stage_2"), buf("stage_3")]
        _ldpc = [0]
        xtcnt = {}

        def piece_jobs(srcn, dstt, tilebufs, tiles, cpeng):
            ar = g[srcn].ap().rearrange("(f p) t -> f p t", p=128)
            pieces = [(t, f) for t in tiles for f in range(FC)]
            qs = []

            def issue(k):
                t, f = pieces[k]
                i = _ldpc[0] % 2
                _ldpc[0] += 1
                qs.append(i)
                P.dma("sync", "din" if i == 0 else "din1",
                      lambda s, f=f, t=t, i=i, ar=ar: s.dma_start(
                          out=stage[:, i * 512:(i + 1) * 512],
                          in_=ar[f][:, t * TS:(t + 1) * TS]),
                      reads=(SH[i],), writes=(SH[i],))

            def copy(k):
                t, f = pieces[k]
                i = qs[k]

                def fn(eng, f=f, t=t, i=i):
                    src = stage[:, i * 512:(i + 1) * 512]
                    dst = dstt[:, f * TPC + t * TS: f * TPC + (t + 1) * TS]
                    return eng.copy(dst, src) if cpeng == "scalar" \
                        else eng.tensor_copy(dst, src)
                P.op(cpeng, fn, reads=(SH[i],), writes=(tilebufs[t],))
                xtcnt[(srcn, t)] = xtcnt.get((srcn, t), 0) + 1

            n = len(pieces)
            jobs = [lambda: issue(0)]
            for k in range(n):
                def jk(k=k):
                    if k + 1 < n:
                        issue(k + 1)
                    copy(k)
                jobs.append(jk)
            for j in jobs:
                j.light = True
            return jobs

        def wpiece_jobs(srcn, slot, slotbuf, cpeng="gpsimd"):
            if srcn == "wqt_f":
                ar = None
            else:
                ar = g[srcn].ap().rearrange("(c p) o -> c p o", p=128)
            qs = []

            def issue(k):
                i = _ldpc[0] % 2
                _ldpc[0] += 1
                qs.append(i)
                if ar is None:
                    P.dma("sync", "din" if i == 0 else "din1",
                          lambda s, k=k, i=i: s.dma_start(
                              out=stage[:, i * 512:(i + 1) * 512],
                              in_=g["wqt_f"].ap()[:, k * 512:(k + 1) * 512]),
                          reads=(SH[i],), writes=(SH[i],))
                else:
                    c, h = k // 2, k % 2
                    P.dma("sync", "din" if i == 0 else "din1",
                          lambda s, c=c, h=h, i=i, ar=ar: s.dma_start(
                              out=stage[:, i * 512:(i + 1) * 512],
                              in_=ar[c][:, h * 512:(h + 1) * 512]),
                          reads=(SH[i],), writes=(SH[i],))

            def copy(k):
                i = qs[k]
                c, h = k // 2, k % 2
                base = k * 512 if ar is None else c * D + h * 512

                def fn(eng, base=base, i=i):
                    src = stage[:, i * 512:(i + 1) * 512]
                    return eng.copy(slot[:, base:base + 512], src) if cpeng == "scalar"                         else eng.tensor_copy(slot[:, base:base + 512], src)
                P.op(cpeng, fn, reads=(SH[i],), writes=(slotbuf,))

            jobs = [lambda: issue(0)]
            for k in range(16):
                def jk(k=k):
                    if k + 1 < 16:
                        issue(k + 1)
                    copy(k)
                jobs.append(jk)
            for j in jobs:
                j.light = True
            return jobs

        constlist = [(i128s, "i128"), (ones1s, "ones1"), (onesrs, "onesrow"),
                     (cfs, "constf"), (busts, "bust")]
        if launch == 2:
            constlist += [(sel16s, "sel16"), (gbs, "gb")]
        for (dst, src) in constlist:
            P.dma("sync", "din", lambda s, d=dst, so=src: s.dma_start(out=d[:, :], in_=g[so].ap()),
                  writes=(buf("consts"),))

        bleft = [FC]

        def mk_build():
            wr = g["wffn_l"].ap().rearrange("(c p) i -> c p i", p=128)
            wqtr = wqt_out.ap().rearrange("p (i o c) -> o i p c", i=8, o=8)
            WH = (SH[2], SH[3])

            def bdma(oc):
                P.dma("sync", "din1", lambda s, oc=oc: s.dma_start(
                    out=stage[:, 1024:2048], in_=wr[oc]), reads=WH, writes=WH)

            def bjob(oc):
                P.op("vector", lambda v: v.tensor_reduce(
                    statc[:, 0:1], stage[:, 1024:2048], AX.X, ALU.add,
                    apply_absolute_value=True),
                    reads=WH, writes=(buf("sc0"),), small=True)
                P.op("vector", lambda v: v.tensor_scalar(
                    statc[:, 1:2], statc[:, 0:1], 1.0 / D, 1e-5, ALU.mult, ALU.max),
                    reads=(buf("sc0"),), writes=(buf("sc1"),), small=True)
                P.op("vector", lambda v: v.reciprocal(statc[:, 2:3], statc[:, 1:2]),
                     reads=(buf("sc1"),), writes=(buf("sc2"),), small=True)
                P.op("vector", lambda v: v.tensor_single_scalar(
                    statc[:, 3:4], statc[:, 1:2], 0.5, ALU.mult),
                    reads=(buf("sc1"),), writes=(buf("sc3"),), small=True)
                P.op("scalar", lambda s: s.activation(
                    wbld[:, :], stage[:, 1024:2048], AF.Copy, scale=statc[:, 2:3]),
                    reads=WH + (buf("sc2"),), writes=(buf("wbld"),))
                P.op("vector", lambda v: v.tensor_scalar(
                    stage[:, 1024:2048], wbld[:, :], 0.5, 1.0, ALU.is_gt, ALU.mult),
                    reads=(buf("wbld"),), writes=WH)
                P.op("vector", lambda v: v.tensor_scalar(
                    wbld[:, :], wbld[:, :], -0.5, 1.0, ALU.is_lt, ALU.mult),
                    reads=(buf("wbld"),), writes=(buf("wbld"),))
                P.op("vector", lambda v: v.tensor_sub(
                    stage[:, 1024:2048], stage[:, 1024:2048], wbld[:, :]),
                    reads=WH + (buf("wbld"),), writes=WH)
                P.op("scalar", lambda s: s.activation(
                    tmpg[:, 0:512], stage[:, 1024:1536], AF.Copy, scale=statc[:, 3:4]),
                    reads=WH + (buf("sc3"),), writes=(buf("tmpg"),))
                P.op("scalar", lambda s: s.activation(
                    tmpv[:, 0:512], stage[:, 1536:2048], AF.Copy, scale=statc[:, 3:4]),
                    reads=WH + (buf("sc3"),), writes=(buf("tmpv"),))
                if oc + 1 < FC:
                    bdma(oc + 1)
                for ic in range(FC):
                    bk = 4 + (ic % 2)
                    srcop = tmpg if ic < 4 else tmpv
                    srcb = buf("tmpg") if ic < 4 else buf("tmpv")
                    co = (ic % 4) * 128
                    P.op("tensor", lambda t, co=co, bk=bk, srcop=srcop: t.matmul(
                        pb[bk][:, 0:128], srcop[:, co:co + 128], i128s[:, :],
                        start=True, stop=True),
                        reads=(srcb, buf("consts"), PB[bk]), writes=(PB[bk],))
                    P.op("scalar", lambda s, ic=ic, oc=oc, bk=bk: s.copy(
                        wqt[:, ic * D + oc * 128: ic * D + (oc + 1) * 128], pb[bk][:, 0:128]),
                        reads=(PB[bk],), writes=(buf("wqt"),))
                    rdst = r2b32 if ic < 4 else aux32
                    rb = buf("r2b32") if ic < 4 else buf("aux32")
                    j = ic % 4
                    P.op("scalar", lambda s, j=j, bk=bk, rdst=rdst: s.copy(
                        rdst[:, j * 128:(j + 1) * 128], pb[bk][:, 0:128]),
                        reads=(PB[bk],), writes=(rb,))
                for j in range(4):
                    P.dma("gpsimd", "dout", lambda gp, oc=oc, j=j: gp.dma_start(
                        out=wqtr[oc][j], in_=r2b32[:, j * 128:(j + 1) * 128]),
                        reads=(buf("r2b32"),))
                    P.dma("gpsimd", "dout1", lambda gp, oc=oc, j=j: gp.dma_start(
                        out=wqtr[oc][4 + j], in_=aux32[:, j * 128:(j + 1) * 128]),
                        reads=(buf("aux32"),))
                bleft[0] -= 1
            bdma(0)
            return [lambda oc=oc: bjob(oc) for oc in range(FC)]

        def kv_ingest():
            kvr = g["kvparts"].ap().rearrange("(r p) c -> r p c", p=128)
            P.op("vector", lambda v: v.memset(kvbd[:, :], 0.0), writes=(buf("kvbd"),))
            P.op("vector", lambda v: v.memset(dbd[:, :], 0.0), writes=(buf("dbd"),))

            def slotb(r):
                return (buf("sqbA"), buf("sqbB")) if r % 2 == 0 else (buf("sqbB"),)

            def dmain(r):
                off = (r % 2) * 520
                P.dma("sync", "din" if r % 2 == 0 else "din1",
                      lambda s, r=r, off=off: s.dma_start(
                          out=sqb[:, off:off + 520], in_=kvr[r]),
                      reads=slotb(r), writes=slotb(r))
            dmain(0)
            for r in range(8):
                if r + 1 < 8:
                    dmain(r + 1)
                off = (r % 2) * 520
                if r == 0:
                    P.op("gpsimd", lambda g2, off=off: g2.tensor_copy(
                        kvsb[:, :], sqb[:, off:off + 520]),
                        reads=slotb(r), writes=(buf("kvsb"),))
                else:
                    P.op("gpsimd", lambda g2, off=off, r=r: g2.tensor_tensor(
                        kvsb[:, :], sqb[:, off:off + 520], kvsb[:, :], ALU.add),
                        reads=slotb(r) + (buf("kvsb"),), writes=(buf("kvsb"),))
            for p_ in range(FC):
                P.op("gpsimd", lambda g2, p_=p_: g2.tensor_copy(
                    kvbd[0:64, p_ * 128: p_ * 128 + 64], kvsb[0:64, p_ * 64:(p_ + 1) * 64]),
                    reads=(buf("kvsb"),), writes=(buf("kvbd"),))
                P.op("gpsimd", lambda g2, p_=p_: g2.tensor_copy(
                    kvbd[64:128, p_ * 128 + 64: p_ * 128 + 128], kvsb[64:128, p_ * 64:(p_ + 1) * 64]),
                    reads=(buf("kvsb"),), writes=(buf("kvbd"),))
                P.op("gpsimd", lambda g2, p_=p_: g2.tensor_copy(
                    dbd[0:64, p_ * 16 + 2 * p_: p_ * 16 + 2 * p_ + 1], kvsb[0:64, 512 + p_:513 + p_]),
                    reads=(buf("kvsb"),), writes=(buf("dbd"),), small=True)
                P.op("gpsimd", lambda g2, p_=p_: g2.tensor_copy(
                    dbd[64:128, p_ * 16 + 2 * p_ + 1: p_ * 16 + 2 * p_ + 2], kvsb[64:128, 512 + p_:513 + p_]),
                    reads=(buf("kvsb"),), writes=(buf("dbd"),), small=True)

        def ffn_phase(tile, final, fill=None):
            def F():
                if fill:
                    fill.popleft()()
                    if fill and len(fill) > 16 and getattr(fill[0], "light", False):
                        fill.popleft()()
            t0 = tile * TS

            def emit_xh_g(gch):
                for half, bk in ((0, 0), (1, 1)):
                    def mmg(t, bk=bk, gch=gch, half=half):
                        last = None
                        for kc in range(FC):
                            last = t.matmul(pb[bk][:, 0:512],
                                            x16[:, kc * TPC + gch * 128: kc * TPC + (gch + 1) * 128],
                                            h16[:, kc * D + half * 512: kc * D + half * 512 + 512],
                                            start=(kc == 0), stop=(kc == FC - 1))
                        return last
                    P.op("tensor", mmg, reads=(X16[gch // 4], buf("h16"), PB[bk]), writes=(PB[bk],))
                    F()

            def emit_xh(cc):
                emit_xh_g(tile * 4 + cc)

            if final or tile == 0:
                emit_xh(0)
            for pp in range(2):
                if DBG2 < 1:
                    continue
                pph = (pp % 2) * PRW if LW == 2 else 0
                pbuf = str(pp % 2) if LW == 2 else "0"
                vb, bbb = buf("v32" + pbuf), buf("bb32")
                ubb = buf("u32b")
                for ci in range(2):
                    cc = 2 * pp + ci
                    co = pph + ci * D
                    t_m1 = P.op("vector", lambda v, ci=ci: v.tensor_scalar_mul(
                        statc[:, 8 + ci:9 + ci], pb[0][:, 0:1], 1.0 / 32),
                        reads=(PB[0],), writes=(buf(f"sc8{ci}"),), small=True)
                    t_a0 = P.op("vector", lambda v, ci=ci: v.tensor_reduce(
                        statc[:, 32 + ci:33 + ci], pb[0][:, 1:512], AX.X, ALU.max,
                        apply_absolute_value=True),
                        reads=(PB[0],), writes=(buf(f"sc32{ci}"),), small=True)
                    t_a1 = P.op("vector", lambda v, ci=ci: v.tensor_reduce(
                        statc[:, 34 + ci:35 + ci], pb[1][:, 0:512], AX.X, ALU.max,
                        apply_absolute_value=True),
                        reads=(PB[1],), writes=(buf(f"sc34{ci}"),), small=True)
                    t_sqa = P.op("scalar", lambda s: s.activation(
                        sqb[:, 0:512], pb[0][:, 0:512], AF.Square),
                        reads=(PB[0],), writes=(buf("sqbA"),), after=(t_m1, t_a0))
                    t_sqb = P.op("scalar", lambda s: s.activation(
                        sqb[:, 512:1024], pb[1][:, 0:512], AF.Square),
                        reads=(PB[1],), writes=(buf("sqbB"),), after=(t_a1,))
                    P.op("vector", lambda v, ci=ci: v.tensor_tensor(
                        statc[:, 36 + ci:37 + ci], statc[:, 32 + ci:33 + ci],
                        statc[:, 34 + ci:35 + ci], ALU.max),
                        reads=(buf(f"sc32{ci}"), buf(f"sc34{ci}")), writes=(buf(f"sc36{ci}"),), small=True)
                    P.op("vector", lambda v, ci=ci: v.tensor_single_scalar(
                        statc[:, 12 + ci:13 + ci], statc[:, 36 + ci:37 + ci], 1e-5, ALU.max),
                        reads=(buf(f"sc36{ci}"),), writes=(buf(f"sc12{ci}"),), small=True)
                    P.op("vector", lambda v, ci=ci: v.reciprocal(
                        statc[:, 14 + ci:15 + ci], statc[:, 12 + ci:13 + ci]),
                        reads=(buf(f"sc12{ci}"),), writes=(buf(f"sc14{ci}"),), small=True)
                    P.op("vector", lambda v, ci=ci: v.tensor_scalar_mul(
                        statc[:, 16 + ci:17 + ci], statc[:, 14 + ci:15 + ci], 7.0),
                        reads=(buf(f"sc14{ci}"),), writes=(buf(f"sc16{ci}"),), small=True)
                    P.op("vector", lambda v, co=co, ci=ci: v.tensor_scalar_mul(
                        v32[:, co:co + 512], pb[0][:, 0:512], statc[:, 16 + ci:17 + ci]),
                        reads=(PB[0], buf(f"sc16{ci}")), writes=(vb,), after=(t_sqa,))
                    P.op("vector", lambda v, co=co, ci=ci: v.tensor_scalar_mul(
                        v32[:, co + 512:co + 1024], pb[1][:, 0:512], statc[:, 16 + ci:17 + ci]),
                        reads=(PB[1], buf(f"sc16{ci}")), writes=(vb,), after=(t_sqb,))
                    if cc + 1 < 4:
                        emit_xh(cc + 1)
                    P.op("vector", lambda v, ci=ci: v.reduce_sum(
                        statc[:, 10 + ci:11 + ci], sqb[:, 0:D], AX.X),
                        reads=(buf("sqbA"), buf("sqbB")), writes=(buf(f"sc10{ci}"),), small=True)
                if DBG2 < 2:
                    continue
                P.op("vector", lambda v, pph=pph: v.memset(
                    v32[:, pph:pph + PRW].rearrange("p (c d) -> p c d", d=D)[:, :, 0:1], 0.0),
                    writes=(vb,), small=True)
                P.op("vector", lambda v: v.tensor_tensor(
                    statc[:, 18:20], statc[:, 8:10], statc[:, 8:10], ALU.mult),
                    reads=(buf("sc80"), buf("sc81")), writes=(buf("sc18"),), small=True)
                P.op("vector", lambda v: v.tensor_single_scalar(
                    statc[:, 20:22], statc[:, 10:12], 1.0 / D, ALU.mult),
                    reads=(buf("sc100"), buf("sc101")), writes=(buf("sc20"),), small=True)
                P.op("vector", lambda v: v.tensor_sub(statc[:, 22:24], statc[:, 20:22], statc[:, 18:20]),
                     reads=(buf("sc20"), buf("sc18")), writes=(buf("sc22"),), small=True)
                P.op("scalar", lambda s: s.activation(statc[:, 24:26], statc[:, 22:24], AF.Ln, bias=cfs[:, 0:1]),
                     reads=(buf("sc22"), buf("consts")), writes=(buf("sc24"),), small=True)
                P.op("scalar", lambda s: s.activation(statc[:, 26:28], statc[:, 24:26], AF.Exp, scale=cfs[:, 1:2]),
                     reads=(buf("sc24"), buf("consts")), writes=(buf("sc26"),), small=True)
                P.op("vector", lambda v: v.tensor_tensor(
                    statc[:, 28:30], statc[:, 12:14], statc[:, 26:28], ALU.mult),
                    reads=(buf("sc120"), buf("sc121"), buf("sc26")), writes=(buf("sc28"),), small=True)
                P.op("vector", lambda v: v.tensor_single_scalar(
                    statc[:, 28:30], statc[:, 28:30], 1.0 / 7, ALU.mult),
                    reads=(buf("sc28"),), writes=(buf("sc28"),), small=True)
                if DBG2 < 3:
                    continue
                P.op("scalar", lambda s, pph=pph: s.activation(
                    u32b[:, :], v32[:, pph:pph + PRW], AF.Abs),
                    reads=(vb,), writes=(ubb,))
                P.op("vector", lambda v, pph=pph: v.tensor_scalar(
                    rs16[:, :], v32[:, pph:pph + PRW], 0.0, 2.0, ALU.is_ge, ALU.mult),
                    reads=(vb,), writes=(buf("rs16"),))
                P.op("vector", lambda v: v.tensor_single_scalar(
                    rs16[:, :], rs16[:, :], -1.0, ALU.add),
                    reads=(buf("rs16"),), writes=(buf("rs16"),))
                P.op("vector", lambda v: v.tensor_scalar(
                    bb32[:, :], u32b[:, :], 3.5, 4.0, ALU.is_ge, ALU.mult),
                    reads=(ubb,), writes=(bbb,))
                P.op("vector", lambda v: v.tensor_sub(u32b[:, :], u32b[:, :], bb32[:, :]),
                     reads=(ubb, bbb), writes=(ubb,))
                P.op("vector", lambda v: v.tensor_scalar(
                    bb32[:, :], u32b[:, :], 1.5, 2.0, ALU.is_ge, ALU.mult),
                    reads=(ubb,), writes=(bbb,))
                P.op("vector", lambda v: v.tensor_sub(u32b[:, :], u32b[:, :], bb32[:, :]),
                     reads=(ubb, bbb), writes=(ubb,))
                P.op("vector", lambda v: v.tensor_scalar(
                    bb32[:, :], u32b[:, :], 0.5, 1.0, ALU.is_ge, ALU.mult),
                    reads=(ubb,), writes=(bbb,))
                P.op("vector", lambda v: v.tensor_sub(u32b[:, :], u32b[:, :], bb32[:, :]),
                     reads=(ubb, bbb), writes=(ubb,))
                P.op("vector", lambda v: v.tensor_mul(bb32[:, :], rs16[:, :], u32b[:, :]),
                     reads=(buf("rs16"), ubb), writes=(bbb,))
                P.op("vector", lambda v, pph=pph: v.tensor_sub(
                    v32[:, pph:pph + PRW], v32[:, pph:pph + PRW], bb32[:, :]),
                    reads=(vb, bbb), writes=(vb,))
                if DBG2 < 4:
                    continue
                for ci in range(2):
                    cc = 2 * pp + ci
                    co = pph + ci * D
                    P.op("vector", lambda v, co=co, ci=ci: v.tensor_scalar_mul(
                        rs16[:, ci * D:(ci + 1) * D], v32[:, co:co + D], statc[:, 28 + ci:29 + ci]),
                        reads=(vb, buf("sc28")), writes=(buf("rs16"),))
                for f in range(FC):
                    bk = 4 + (f % 2)
                    for ci in range(2):
                        cc = 2 * pp + ci
                        P.op("tensor", lambda t, f=f, ci=ci, bk=bk: t.matmul(
                            pb[bk][:, ci * 128:(ci + 1) * 128], rs16[:, ci * D + f * 128: ci * D + (f + 1) * 128],
                            i128s[:, :], start=True, stop=True),
                            reads=(buf("rs16"), buf("consts"), PB[bk]), writes=(PB[bk],))
                    P.op("scalar", lambda s, f=f, pp=pp, bk=bk: s.copy(
                        xqT[:, f * TS + pp * 256: f * TS + (pp + 1) * 256], pb[bk][:, 0:256]),
                        reads=(PB[bk],), writes=(buf("xqT"),))
                    F()
            if not final and tile + 1 < NT:
                while xtcnt.get(("xT", tile + 1), 0) < FC and fill:
                    fill.popleft()()
                emit_xh_g((tile + 1) * 4)
            if launch == 1 and tile == 0:
                while bleft[0] > 0 and fill:
                    fill.popleft()()
            for oc in range(FC if DBG2 >= 5 else 0):
                bk = 4 + (oc % 2)

                def mmo(t, oc=oc, bk=bk):
                    last = None
                    for ic in range(FC):
                        last = t.matmul(pb[bk][:, 0:512], wqt[:, ic * D + oc * 128: ic * D + (oc + 1) * 128],
                                        xqT[:, ic * TS:(ic + 1) * TS],
                                        start=(ic == 0), stop=(ic == FC - 1))
                    return last
                P.op("tensor", mmo, reads=(buf("wqt"), buf("xqT"), PB[bk]), writes=(PB[bk],))
                F()
                if not final:
                    P.op("vector", lambda v, oc=oc, bk=bk, t0=t0: v.tensor_tensor(
                        x16[:, oc * TPC + t0: oc * TPC + t0 + TS], pb[bk][:, 0:512],
                        x16[:, oc * TPC + t0: oc * TPC + t0 + TS], ALU.add),
                        reads=(PB[bk], X16[tile]), writes=(X16[tile],))
                else:
                    i = oc % 2
                    hb = buf("sqbA") if i == 0 else buf("sqbB")
                    P.op("vector", lambda v, oc=oc, bk=bk, t0=t0, i=i: v.tensor_tensor(
                        sqb[:, i * TS:(i + 1) * TS], pb[bk][:, 0:512],
                        x16[:, oc * TPC + t0: oc * TPC + t0 + TS], ALU.add),
                        reads=(PB[bk], X16[tile], hb), writes=(hb,))
                    P.dma("gpsimd", "dout" if oc % 2 == 0 else "dout1",
                          lambda gp, oc=oc, tile=tile, i=i: gp.dma_start(
                              out=yT.ap()[oc * 128:(oc + 1) * 128, tile * TS:(tile + 1) * TS],
                              in_=sqb[:, i * TS:(i + 1) * TS]),
                          reads=(hb,))

        def ln2_block(tile):
            t0 = tile * TS

            def mmsum(t, t0=t0):
                last = None
                for f in range(FC):
                    last = t.matmul(pb[4][0:1, 0:512], ones1s[:, 0:1],
                                    x16[:, f * TPC + t0: f * TPC + t0 + TS],
                                    start=(f == 0), stop=(f == FC - 1))
                return last
            P.op("tensor", mmsum, reads=(X16[tile], buf("consts"), PB[4]), writes=(PB[4],))
            for f in range(FC):
                P.op("scalar", lambda s, f=f, t0=t0: s.activation(
                    xqT[:, f * TS:(f + 1) * TS], x16[:, f * TPC + t0: f * TPC + t0 + TS], AF.Square),
                    reads=(X16[tile],), writes=(buf("xqT"),))

            def mmsq(t):
                last = None
                for f in range(FC):
                    last = t.matmul(pb[5][0:1, 0:512], ones1s[:, 0:1],
                                    xqT[:, f * TS:(f + 1) * TS],
                                    start=(f == 0), stop=(f == FC - 1))
                return last
            P.op("tensor", mmsq, reads=(buf("xqT"), buf("consts"), PB[5]), writes=(PB[5],))
            P.op("vector", lambda v: v.tensor_scalar_mul(rowm, pb[4][0:1, 0:512], 1.0 / D),
                 reads=(PB[4],), writes=(buf("rowm"),))
            P.op("vector", lambda v: v.tensor_scalar_mul(rowq, pb[5][0:1, 0:512], 1.0 / D),
                 reads=(PB[5],), writes=(buf("rowq"),))
            P.op("tensor", lambda t: t.matmul(pb[4][:, 0:512], onesrs[0:1, :], rowm, start=True, stop=True),
                 reads=(buf("rowm"), buf("consts"), PB[4]), writes=(PB[4],))
            P.op("vector", lambda v: v.tensor_tensor(rowm, rowm, rowm, ALU.mult),
                 reads=(buf("rowm"),), writes=(buf("rowm"),), small=True)
            P.op("vector", lambda v: v.tensor_sub(rowq, rowq, rowm),
                 reads=(buf("rowq"), buf("rowm")), writes=(buf("rowq"),))
            P.op("scalar", lambda s: s.activation(rowm, rowq, AF.Ln, bias=cfs[0:1, 0:1]),
                 reads=(buf("rowq"), buf("consts")), writes=(buf("rowm"),))
            P.op("scalar", lambda s: s.activation(rowq, rowm, AF.Exp, scale=cfs[0:1, 1:2]),
                 reads=(buf("rowm"), buf("consts")), writes=(buf("rowq"),))
            P.op("tensor", lambda t: t.matmul(pb[5][:, 0:512], onesrs[0:1, :], rowq, start=True, stop=True),
                 reads=(buf("rowq"), buf("consts"), PB[5]), writes=(PB[5],))
            P.op("vector", lambda v: v.tensor_copy(r2b32[:, :], pb[5][:, 0:512]),
                 reads=(PB[5],), writes=(buf("r2b32"),))
            P.op("vector", lambda v: v.tensor_copy(aux32[:, :], pb[4][:, 0:512]),
                 reads=(PB[4],), writes=(buf("aux32"),))
            for f in range(FC):
                eng = "vector" if f < 5 else "gpsimd"
                tmp = tmpv if f < 5 else tmpg
                tb = buf("tmpv") if f < 5 else buf("tmpg")
                P.op(eng, lambda e2, f=f, t0=t0, tmp=tmp: e2.tensor_sub(
                    tmp[:, :], x16[:, f * TPC + t0: f * TPC + t0 + TS], aux32[:, :]),
                    reads=(X16[tile], buf("aux32")), writes=(tb,))
                P.op(eng, lambda e2, f=f, t0=t0, tmp=tmp: e2.tensor_mul(
                    ln2[:, f * TPC + t0: f * TPC + t0 + TS], tmp[:, :], r2b32[:, :]),
                    reads=(tb, buf("r2b32")), writes=(LN2[tile],))

        def bnd_out(srct, outt, sbuf_, f, t, qi, eng="scalar"):
            hb = buf("sqbA") if qi == 0 else buf("sqbB")

            def fn(e2, f=f, t=t, qi=qi, srct=srct):
                src = srct[:, f * TPC + t * TS: f * TPC + (t + 1) * TS]
                dst = sqb[:, qi * 512:(qi + 1) * 512]
                return e2.copy(dst, src) if eng == "scalar" else e2.tensor_copy(dst, src)
            P.op(eng, fn, reads=(sbuf_, hb), writes=(hb,))
            P.dma("gpsimd", "dout" if qi % 2 == 0 else "dout1",
                  lambda gp, f=f, t=t, qi=qi, outt=outt: gp.dma_start(
                      out=outt.ap()[f * 128:(f + 1) * 128, t * TS:(t + 1) * TS],
                      in_=sqb[:, qi * 512:(qi + 1) * 512]),
                  reads=(hb,))

        def kv_jobs(tile):
            jobs = []
            for cc in range(4):
                gch = tile * 4 + cc

                def ja(gch=gch, tile=tile):
                    for half in range(2):
                        def mmk(t, gch=gch, half=half):
                            last = None
                            for kc in range(FC):
                                last = t.matmul(pb[2 + half][:, 0:512],
                                                ln2[:, kc * TPC + gch * 128: kc * TPC + (gch + 1) * 128],
                                                wS3[:, kc * D + half * 512: kc * D + half * 512 + 512],
                                                start=(kc == 0), stop=(kc == FC - 1))
                            return last
                        P.op("tensor", mmk, reads=(LN2[tile], buf("wS3"), PB[2 + half]), writes=(PB[2 + half],))
                    P.op("scalar", lambda s: s.activation(k16[:, 0:512], pb[2][:, 0:512], AF.Sigmoid),
                         reads=(PB[2],), writes=(buf("k16"),))
                    P.op("scalar", lambda s: s.activation(k16[:, 512:1024], pb[3][:, 0:512], AF.Sigmoid),
                         reads=(PB[3],), writes=(buf("k16"),))

                def jb(gch=gch, tile=tile):
                    for half in range(2):
                        def mmv(t, gch=gch, half=half):
                            last = None
                            for kc in range(FC):
                                last = t.matmul(pb[2 + half][:, 0:512],
                                                ln2[:, kc * TPC + gch * 128: kc * TPC + (gch + 1) * 128],
                                                wS4[:, kc * D + half * 512: kc * D + half * 512 + 512],
                                                start=(kc == 0), stop=(kc == FC - 1))
                            return last
                        P.op("tensor", mmv, reads=(LN2[tile], buf("wS4"), PB[2 + half]), writes=(PB[2 + half],))
                    P.op("scalar", lambda s: s.copy(v16[:, 0:512], pb[2][:, 0:512]),
                         reads=(PB[2],), writes=(buf("v16"),))
                    P.op("scalar", lambda s: s.copy(v16[:, 512:1024], pb[3][:, 0:512]),
                         reads=(PB[3],), writes=(buf("v16"),))

                def jc(gch=gch):
                    def mmkv(t, gch=gch):
                        last = None
                        for p_ in range(FC):
                            bank = pb[6] if p_ < 4 else pb[7]
                            colo = (p_ % 4) * 128
                            last = t.matmul(bank[:, colo:colo + 128],
                                            k16[:, p_ * 128:(p_ + 1) * 128],
                                            v16[:, p_ * 128:(p_ + 1) * 128],
                                            start=(gch == 0 and (p_ % 4) == 0),
                                            stop=(gch == NCH - 1 and (p_ % 4) == 3))
                        return last
                    P.op("tensor", mmkv, reads=(buf("k16"), buf("v16"), PB[6], PB[7]),
                         writes=(PB[6], PB[7]))

                    def mmks(t):
                        last = None
                        for p_ in range(FC):
                            last = t.matmul(pb[2][:, p_:p_ + 1], k16[:, p_ * 128:(p_ + 1) * 128],
                                            ones1s[:, 0:1], start=(p_ == 0), stop=(p_ == FC - 1))
                        return last
                    P.op("tensor", mmks, reads=(buf("k16"), buf("consts"), PB[2]), writes=(PB[2],))
                    sks, dks = (gch % 2) * 8, ((gch + 1) % 2) * 8
                    if gch == 0:
                        P.op("vector", lambda v, dks=dks: v.tensor_copy(ksum[:, dks:dks + 8], pb[2][:, 0:8]),
                             reads=(PB[2],), writes=(buf("ksum"),), small=True)
                    else:
                        P.op("vector", lambda v, sks=sks, dks=dks: v.tensor_tensor(
                            ksum[:, dks:dks + 8], pb[2][:, 0:8], ksum[:, sks:sks + 8], ALU.add),
                            reads=(PB[2], buf("ksum")), writes=(buf("ksum"),), small=True)
                jobs += [ja, jb, jc]
            return jobs

        if launch == 1:
            P.dma("sync", "din", lambda s: s.dma_start(out=h16[:, :], in_=g["h16"].ap()),
                  writes=(buf("h16"),))
            for j in piece_jobs("xT", x16, X16, (0,), "scalar"):
                j()
            jobs = deque()
            wkj = wpiece_jobs("wk_l", wS3, buf("wS3"), "scalar")
            if DBG >= 2:
                bj = mk_build()
                for i, b in enumerate(bj):
                    jobs.append(b)
                    jobs.extend(wkj[2 * i: 2 * i + 2])
                jobs.extend(wkj[16:])
            else:
                bleft[0] = 0
                jobs.extend(wkj)
            jobs.extend(piece_jobs("xT", x16, X16, (1,), "scalar"))
            jobs.extend(wpiece_jobs("wv_l", wS4, buf("wS4"), "scalar"))
            jobs.extend(piece_jobs("xT", x16, X16, (2,), "scalar"))
            jobs.extend(piece_jobs("xT", x16, X16, (3,), "scalar"))
            if DBG >= 3:
                qrr = [0]
                for tile in range(NT):
                    while xtcnt.get(("xT", tile), 0) < FC:
                        jobs.popleft()()
                    ffn_phase(tile, final=False, fill=jobs)
                    if DBG >= 4:
                        ln2_block(tile)
                        if DBG >= 5:
                            jobs.extend(kv_jobs(tile))
                        for f in range(FC):
                            bnd_out(x16, x1_out, X16[tile], f, tile, qrr[0] % 2)
                            qrr[0] += 1
                            if tile == NT - 1 and jobs:
                                jobs.popleft()()
                            bnd_out(ln2, ln2_out, LN2[tile], f, tile, qrr[0] % 2, "vector")
                            qrr[0] += 1
                            if tile == NT - 1 and jobs:
                                jobs.popleft()()
                while jobs:
                    jobs.popleft()()

        if launch == 1:
            for p_ in range(FC if DBG >= 5 else 0):
                bank = pb[6] if p_ < 4 else pb[7]
                pbuf = PB[6] if p_ < 4 else PB[7]
                colo = (p_ % 4) * 128
                rd = (pbuf, SH[0], SH[1]) if p_ == 0 else (pbuf,)
                P.op("vector", lambda v, p_=p_, bank=bank, colo=colo: v.tensor_copy(
                    kvsb[0:64, p_ * 64:(p_ + 1) * 64], bank[0:64, colo:colo + 64]),
                    reads=rd, writes=(buf("kvsb"), SH[0], SH[1]))
                P.op("vector", lambda v, p_=p_, bank=bank, colo=colo: v.tensor_copy(
                    kvsb[64:128, p_ * 64:(p_ + 1) * 64], bank[64:128, colo + 64:colo + 128]),
                    reads=(pbuf,), writes=(buf("kvsb"), SH[0], SH[1]))
            if DBG >= 5:
                P.op("vector", lambda v: v.tensor_copy(kvsb[:, 512:520], ksum[:, 0:8]),
                     reads=(buf("ksum"),), writes=(buf("kvsb"), SH[0], SH[1]))
                P.dma("gpsimd", "dout", lambda gp: gp.dma_start(out=kv_out.ap(), in_=kvsb[:, :]),
                      reads=(buf("kvsb"),))

        def bjobs(tile):
            t0 = tile * TS
            QB = buf("qt")
            jobs = []

            def jq(oc):
                bk = (2, 3, 6, 7)[oc % 4]

                def mmq(t, oc=oc, t0=t0, bk=bk):
                    last = None
                    for kc in range(FC):
                        last = t.matmul(pb[bk][:, 0:512],
                                        wS3[:, kc * D + oc * 128: kc * D + (oc + 1) * 128],
                                        ln2[:, kc * TPC + t0: kc * TPC + t0 + TS],
                                        start=(kc == 0), stop=(kc == FC - 1))
                    return last
                P.op("tensor", mmq, reads=(buf("wS3"), LN2[tile], PB[bk]), writes=(PB[bk],))
                P.op("scalar", lambda s, oc=oc, bk=bk: s.activation(
                    qt[:, oc * TS:(oc + 1) * TS], pb[bk][:, 0:512], AF.Sigmoid),
                    reads=(PB[bk],), writes=(QB,))
            for oc in range(FC):
                jobs.append(lambda oc=oc: jq(oc))

            def jden():
                def mmden(t):
                    last = None
                    for p_ in range(FC):
                        last = t.matmul(pb[2][0:16, 0:512], dbd[:, p_ * 16:(p_ + 1) * 16],
                                        qt[:, p_ * TS:(p_ + 1) * TS],
                                        start=(p_ == 0), stop=(p_ == FC - 1))
                    return last
                P.op("tensor", mmden, reads=(buf("dbd"), QB, PB[2]), writes=(PB[2],))
                P.op("vector", lambda v: v.tensor_scalar_add(rdrow[:, :], pb[2][0:16, 0:512], 1e-8),
                     reads=(PB[2],), writes=(buf("aux32"),))
                P.op("vector", lambda v: v.reciprocal(rdr2[0:16, :], rdrow[:, :]),
                     reads=(buf("aux32"),), writes=(buf("rdr2"),))
            jobs.append(jden)

            def jnum(p_):
                ba = 2 if p_ % 2 == 0 else 6
                bs = 3 if p_ % 2 == 0 else 7
                rdt = rdb32 if p_ % 2 == 0 else aux32
                rb = buf("rdbA") if p_ % 2 == 0 else buf("aux32")
                P.op("tensor", lambda t, p_=p_, ba=ba: t.matmul(
                    pb[ba][:, 0:512], kvbd[:, p_ * 128:(p_ + 1) * 128],
                    qt[:, p_ * TS:(p_ + 1) * TS], start=True, stop=True),
                    reads=(buf("kvbd"), QB, PB[ba]), writes=(PB[ba],))
                P.op("tensor", lambda t, p_=p_, bs=bs: t.matmul(
                    pb[bs][:, 0:512], sel16s[:, p_ * 128:(p_ + 1) * 128], rdr2[0:16, :],
                    start=True, stop=True),
                    reads=(buf("rdr2"), buf("consts"), PB[bs]), writes=(PB[bs],))
                P.op("vector", lambda v, bs=bs, rdt=rdt: v.tensor_copy(rdt[:, 0:512], pb[bs][:, 0:512]),
                     reads=(PB[bs], buf("kvsb")), writes=(rb,))
                P.op("vector", lambda v, p_=p_, ba=ba, rdt=rdt: v.tensor_mul(
                    ao16[:, p_ * TS:(p_ + 1) * TS], pb[ba][:, 0:512], rdt[:, 0:512]),
                    reads=(PB[ba], rb), writes=(buf("ao16"),))
            for p_ in range(FC):
                jobs.append(lambda p_=p_: jnum(p_))

            def jfc(oc):
                bk = 6 + (oc % 2)

                def mmfc(t, oc=oc, bk=bk):
                    last = None
                    for hc in range(FC):
                        last = t.matmul(pb[bk][:, 0:512],
                                        wS4[:, hc * D + oc * 128: hc * D + (oc + 1) * 128],
                                        ao16[:, hc * TS:(hc + 1) * TS],
                                        start=(hc == 0), stop=(hc == FC - 1))
                    return last
                P.op("tensor", mmfc, reads=(buf("wS4"), buf("ao16"), PB[bk]), writes=(PB[bk],))
                P.op("vector", lambda v, oc=oc, t0=t0, bk=bk: v.tensor_tensor(
                    qt[:, oc * TS:(oc + 1) * TS], pb[bk][:, 0:512],
                    ln2[:, oc * TPC + t0: oc * TPC + t0 + TS], ALU.add),
                    reads=(PB[bk], LN2[tile]), writes=(QB,))
            for oc in range(FC):
                jobs.append(lambda oc=oc: jfc(oc))

            def jstat():
                def mmsy(t):
                    last = None
                    for f in range(FC):
                        last = t.matmul(pb[2][0:1, 0:512], ones1s[:, 0:1],
                                        qt[:, f * TS:(f + 1) * TS],
                                        start=(f == 0), stop=(f == FC - 1))
                    return last
                P.op("tensor", mmsy, reads=(QB, buf("consts"), PB[2]), writes=(PB[2],))
                for f in range(FC):
                    P.op("scalar", lambda s, f=f: s.activation(
                        ao16[:, f * TS:(f + 1) * TS], qt[:, f * TS:(f + 1) * TS], AF.Square),
                        reads=(QB,), writes=(buf("ao16"),))

                def mmsy2(t):
                    last = None
                    for f in range(FC):
                        last = t.matmul(pb[3][0:1, 0:512], ones1s[:, 0:1],
                                        ao16[:, f * TS:(f + 1) * TS],
                                        start=(f == 0), stop=(f == FC - 1))
                    return last
                P.op("tensor", mmsy2, reads=(buf("ao16"), buf("consts"), PB[3]), writes=(PB[3],))
                P.op("vector", lambda v: v.tensor_scalar_mul(rowm, pb[2][0:1, 0:512], 1.0 / D),
                     reads=(PB[2],), writes=(buf("rowm"),))
                P.op("vector", lambda v: v.tensor_scalar_mul(rowq, pb[3][0:1, 0:512], 1.0 / D),
                     reads=(PB[3],), writes=(buf("rowq"),))
                P.op("tensor", lambda t: t.matmul(pb[2][:, 0:512], onesrs[0:1, :], rowm, start=True, stop=True),
                     reads=(buf("rowm"), buf("consts"), PB[2]), writes=(PB[2],))
                P.op("vector", lambda v: v.tensor_tensor(rowm, rowm, rowm, ALU.mult),
                     reads=(buf("rowm"),), writes=(buf("rowm"),), small=True)
                P.op("vector", lambda v: v.tensor_sub(rowq, rowq, rowm),
                     reads=(buf("rowq"), buf("rowm")), writes=(buf("rowq"),))
                P.op("scalar", lambda s: s.activation(rowm, rowq, AF.Ln, bias=cfs[0:1, 0:1]),
                     reads=(buf("rowq"), buf("consts")), writes=(buf("rowm"),))
                P.op("scalar", lambda s: s.activation(rowq, rowm, AF.Exp, scale=cfs[0:1, 1:2]),
                     reads=(buf("rowm"), buf("consts")), writes=(buf("rowq"),))
                P.op("tensor", lambda t: t.matmul(pb[3][:, 0:512], onesrs[0:1, :], rowq, start=True, stop=True),
                     reads=(buf("rowq"), buf("consts"), PB[3]), writes=(PB[3],))
                P.op("vector", lambda v: v.tensor_copy(r2b32[:, :], pb[3][:, 0:512]),
                     reads=(PB[3],), writes=(buf("r2b32"),))
                P.op("vector", lambda v: v.tensor_copy(mbf[:, :], pb[2][:, 0:512]),
                     reads=(PB[2],), writes=(buf("aux32"),))
            jobs.append(jstat)

            def jret(fpair):
                for f in (2 * fpair, 2 * fpair + 1):
                    eng = "vector" if f < 5 else "gpsimd"
                    tmp = tmpv if f < 5 else tmpg
                    tb = buf("tmpv") if f < 5 else buf("tmpg")
                    P.op(eng, lambda e2, f=f, tmp=tmp: e2.tensor_sub(
                        tmp[:, :], qt[:, f * TS:(f + 1) * TS], mbf[:, :]),
                        reads=(QB, buf("aux32")), writes=(tb,))
                    P.op(eng, lambda e2, tmp=tmp: e2.tensor_mul(
                        tmp[:, :], tmp[:, :], r2b32[:, :]),
                        reads=(tb, buf("r2b32")), writes=(tb,))
                    P.op(eng, lambda e2, f=f, tmp=tmp: e2.tensor_scalar(
                        tmp[:, :], tmp[:, :], gbs[:, f:f + 1], gbs[:, 8 + f:9 + f], ALU.mult, ALU.add),
                        reads=(tb, buf("consts")), writes=(tb,))
                    P.op(eng, lambda e2, f=f, t0=t0, tmp=tmp: e2.tensor_tensor(
                        x16[:, f * TPC + t0: f * TPC + t0 + TS], tmp[:, :],
                        x16[:, f * TPC + t0: f * TPC + t0 + TS], ALU.add),
                        reads=(tb, X16[tile]), writes=(X16[tile],))
            for fp in range(4):
                jobs.append(lambda fp=fp: jret(fp))
            return jobs

        if launch == 2:
            for j in piece_jobs("ln2f", ln2, LN2, (0,), "scalar"):
                j()
            for j in wpiece_jobs("wq_l", wS3, buf("wS3")):
                j()
            kv_ingest()
            for j in piece_jobs("x1f", x16, X16, (0,), "gpsimd"):
                j()
            P.dma("sync", "din", lambda s: s.dma_start(out=h16[:, :], in_=g["h16"].ap()),
                  writes=(buf("h16"),))
            for j in wpiece_jobs("wfc_l", wS4, buf("wS4")):
                j()
            for j in bjobs(0):
                j()
            for j in wpiece_jobs("wqt_f", wqt, buf("wqt")):
                j()
            jobs2 = deque()
            for tile in range(NT):
                if tile + 1 < NT:
                    jobs2.extend(piece_jobs("ln2f", ln2, LN2, (tile + 1,), "scalar"))
                    bj = bjobs(tile + 1)
                    jobs2.extend(bj[:8])
                    jobs2.extend(piece_jobs("x1f", x16, X16, (tile + 1,), "scalar"))
                    jobs2.extend(bj[8:])
                ffn_phase(tile, final=True, fill=jobs2)
                while jobs2:
                    jobs2.popleft()()

        P.wait("gpsimd", ("dout", P.cnt["dout"], False))
        if P.cnt["dout1"] > 0:
            P.wait("gpsimd", ("dout1", P.cnt["dout1"], False))

        with nc.Block() as block:
            @block.tensor
            def _(t):
                P.emit(t, "tensor", sems)

            @block.vector
            def _(v):
                P.emit(v, "vector", sems)

            @block.scalar
            def _(s):
                P.emit(s, "scalar", sems)

            @block.gpsimd
            def _(gp):
                P.emit(gp, "gpsimd", sems)

            @block.sync
            def _(s):
                P.emit(s, "sync", sems)

    return nc


def kernel(L0_attn_inputs, L0_attn_W_Q, L0_attn_W_K, L0_attn_W_V, L0_attn_W_fc,
           L0_attn_gamma, L0_attn_beta, L0_ffn_weight):
    x = np.asarray(L0_attn_inputs)[0]
    xT = np.ascontiguousarray(x.T)
    h16 = _hadamard_bf16()
    i128 = np.eye(128, dtype=ml_dtypes.bfloat16)
    ones1 = np.ones((128, 1), dtype=ml_dtypes.bfloat16)
    onesrow = np.ones((1, 128), dtype=np.float32)
    sel16 = _sel16_f32()
    gb = np.concatenate([
        np.asarray(L0_attn_gamma).reshape(8, 128).T,
        np.asarray(L0_attn_beta).reshape(8, 128).T], axis=1).astype(np.float32)
    gb = np.ascontiguousarray(gb)
    constf = np.zeros((128, 8), np.float32)
    constf[:, 0] = EPS
    constf[:, 1] = -0.5

    bust1 = 16 + (VERSION * 7) % 48
    bust2 = 17 + (VERSION * 5) % 48
    common = {"h16": h16, "i128": i128, "ones1": ones1, "onesrow": onesrow,
              "constf": constf}

    nc1 = build_nc(bust1, 1)
    wffn = np.ascontiguousarray(np.asarray(L0_ffn_weight, dtype=np.float32))
    base1 = dict(common)
    base1.update({
        "wk_l": np.ascontiguousarray(np.asarray(L0_attn_W_K, dtype=np.float32)),
        "wv_l": np.ascontiguousarray(np.asarray(L0_attn_W_V, dtype=np.float32)),
        "wffn_l": wffn,
        "bust": np.zeros((1, bust1), np.float32),
    })
    in_maps1 = []
    for c in range(NCORE):
        m = dict(base1)
        m["xT"] = np.ascontiguousarray(xT[:, c * TPC:(c + 1) * TPC])
        in_maps1.append(m)
    res1 = run_bass_kernel_spmd(nc1, in_maps1, core_ids=list(range(NCORE)))
    kvparts = np.concatenate(
        [np.asarray(res1.results[c]["kv_out"]) for c in range(NCORE)], axis=0)

    nc2 = build_nc(bust2, 2)
    base2 = dict(common)
    base2.update({
        "wq_l": np.ascontiguousarray(np.asarray(L0_attn_W_Q, dtype=np.float32)),
        "wfc_l": np.ascontiguousarray(np.asarray(L0_attn_W_fc, dtype=np.float32)),
        "gb": gb, "sel16": sel16,
        "wqt_f": np.ascontiguousarray(np.asarray(res1.results[0]["wqt_out"], dtype=np.float32)),
        "kvparts": np.ascontiguousarray(kvparts.astype(np.float32)),
        "bust": np.zeros((1, bust2), np.float32),
    })
    in_maps2 = []
    for c in range(NCORE):
        m = dict(base2)
        m["x1f"] = np.asarray(res1.results[c]["x1_out"])
        m["ln2f"] = np.asarray(res1.results[c]["ln2_out"])
        in_maps2.append(m)
    res2 = run_bass_kernel_spmd(nc2, in_maps2, core_ids=list(range(NCORE)))
    out = np.concatenate([r["yT"] for r in res2.results], axis=1)
    return np.ascontiguousarray(out.T).reshape(1, 16384, 1024).astype(np.float32)
